# version 05
# baseline (speedup 1.0000x reference)
import numpy as np
import ml_dtypes
import concourse.bass as bass
import concourse.mybir as mybir
from concourse.bass_utils import run_bass_kernel_spmd
from contextlib import ExitStack

F32 = mybir.dt.float32
BF16 = mybir.dt.bfloat16
I32 = mybir.dt.int32
AF = mybir.ActivationFunctionType
ALU = mybir.AluOpType
AX = mybir.AxisListType

NCORE = 8
T = 2048
C = 16
H = 16
HD = 128
D = 2048
DB = 16
II = 8192
IB = 64
HB = 16
M = 4
SCALE = HD ** -0.5
TTN = 512
NTT = T // TTN
NSP = TTN // 512
NBW = 4
NWB = IB // NBW
CCR = H * 128 + 16

bf16 = ml_dtypes.bfloat16
ENGS = ("tensor", "vector", "scalar", "gpsimd", "sync")

def _dinc(eng, ins, sem, k=1):
    eng.drain().then_inc(sem, k)
    return ins




class Ctr:
    def __init__(self, handle, name):
        self.h = handle
        self.v = 0
        self.name = name

    def inc(self, k=1):
        self.v += k
        return self.v


class Phase:
    def __init__(self, block):
        self.block = block
        self.s = {k: [] for k in ENGS}

    def op(self, eng, fn):
        self.s[eng].append(fn)

    def wait(self, eng, ctr, val):
        if val > 0:
            self.s[eng].append(lambda e, h=ctr.h, v=val: e.wait_ge(h, v))

    def emit(self):
        for name in ENGS:
            ops = self.s[name]
            if not ops:
                continue

            def sec(e, ops=ops):
                for f in ops:
                    f(e)
            getattr(self.block, name)(sec)
        self.s = {k: [] for k in ENGS}


def build(dbg_names=(), upto=6, no_cc=False):
    nc = bass.Bass("TRN2", target_bir_lowering=False, detect_race_conditions=False)
    ap = {}

    def din(name, shape, dt):
        ap[name] = nc.dram_tensor(name, shape, dt, kind="ExternalInput").ap()

    din("x", [T, D], F32)
    din("wqT", [H, DB, 128, 128], F32)
    din("wkT", [H, DB, 128, 128], F32)
    din("wvT", [DB, 128, D], F32)
    din("woT", [DB, 128, D], F32)
    din("gw", [IB, HB, 128, 128], I32)
    din("uw", [IB, HB, 128, 128], I32)
    din("dw", [HB, IB, 128, 128], I32)
    din("gsT", [128, IB], F32)
    din("usT", [128, IB], F32)
    din("dsT", [128, HB], F32)
    din("mem", [M, H, 128, 128], F32)
    din("mnTh", [128, H * M], F32)
    din("mnrows", [M * H, 128], F32)
    din("gate", [1, H], F32)
    din("tril", [128, 128], BF16)
    din("i_epsc", [128, 1], F32)
    din("i_sent", [128, 8], F32)
    din("i_onesb", [128, 1], BF16)
    din("i_onesr", [1, 128], F32)
    din("rmask", [1, 8], F32)
    din("msel", [M * H, M], F32)
    out = nc.dram_tensor("out", [T, D], F32, kind="ExternalOutput").ap()

    st_xn = nc.dram_tensor("st_xn", [DB, 128, T], BF16, kind="Internal").ap()
    st_qp = nc.dram_tensor("st_qp", [H, 128, T], BF16, kind="Internal").ap()
    st_intra = nc.dram_tensor("st_intra", [H, 128, C * 129], BF16, kind="Internal").ap()
    st_sloc = nc.dram_tensor("st_sloc", [H, 128, C * 129], BF16, kind="Internal").ap()
    st_gf = nc.dram_tensor("st_gf", [C, 128, D], BF16, kind="Internal").ap()
    scr_gw = nc.dram_tensor("scr_gw", [IB, 128, HB * 128], BF16, kind="Internal").ap()
    scr_uw = nc.dram_tensor("scr_uw", [IB, 128, HB * 128], BF16, kind="Internal").ap()
    scr_dw = nc.dram_tensor("scr_dw", [HB, 128, IB * 128], BF16, kind="Internal").ap()
    cc_in = nc.dram_tensor("cc_in", [CCR, 129], F32, kind="Internal").ap()
    cc_out = nc.dram_tensor("cc_out", [NCORE * CCR, 129], F32, kind="Internal").ap()
    relbuf = nc.dram_tensor("relbuf", [1, H * M], F32, kind="Internal").ap()
    wbuf = nc.dram_tensor("wbuf", [H, M], F32, kind="Internal").ap()
    dbg = {}
    _dbg_shapes = {
        "stqp": ([H, 128, T], BF16), "stintra": ([H, 128, C * 129], BF16),
        "stsloc": ([H, 128, C * 129], BF16), "stgf": ([C, 128, D], BF16),
        "stxn": ([DB, 128, T], BF16), "srun": ([128, H * 129], F32),
        "relp": ([1, H * M], F32), "sinit": ([128, H * 129], F32),
        "wb": ([128, H * M], F32), "gg": ([128, 3 * H], F32),
        "att0": ([128, D], BF16), "ccout": ([NCORE * CCR, 129], F32),
        "xnfm": ([128, DB * T], BF16),
        "stats": ([128, 8 * C], F32),
        "hbuf0": ([128, IB * TTN], BF16),
        "xnsl0": ([128, HB * TTN], BF16),
    }
    for name in dbg_names:
        sh, dt = _dbg_shapes[name]
        dbg[name] = nc.dram_tensor("dbg_" + name, sh, dt, kind="ExternalOutput").ap()

    with ExitStack() as Gs:
        def sbuf(name, shape, dt, stack):
            return stack.enter_context(nc.sbuf_tensor(name, shape, dt))

        s_run = sbuf("s_run", [128, H * 129], F32, Gs)
        qsum = sbuf("qsum", [128, H], F32, Gs)
        mnT_f = sbuf("mnT_f", [128, M * H], F32, Gs)
        mnT_b = sbuf("mnT_b", [128, M * H], BF16, Gs)
        trilb = sbuf("trilb", [128, 128], BF16, Gs)
        onesb = sbuf("onesb", [128, 1], BF16, Gs)
        onesr = sbuf("onesr", [1, 128], F32, Gs)
        rmk = sbuf("rmk", [1, 8], F32, Gs)
        epsc = sbuf("epsc", [128, 1], F32, Gs)
        dvflush = sbuf("dvflush", [128, 1], F32, Gs)
        attS = Gs.enter_context(ExitStack())
        xn_fm = sbuf("xn_fm", [128, DB * T], BF16, attS)
        v_tm = sbuf("v_tm", [128, C * D], BF16, attS)

        pA = Gs.enter_context(nc.psum_tensor("pA", [128, 2048], F32))
        pB = Gs.enter_context(nc.psum_tensor("pB", [128, 2048], F32))

        _sems = {}

        def S(name):
            if name not in _sems:
                _sems[name] = Ctr(Gs.enter_context(nc.semaphore(name)), name)
            return _sems[name]

        sPE, sDV, sAC, sCC, sB = S("sPE"), S("sDV"), S("sAC"), S("sCC"), S("sB")
        sGt, sst = S("sGt"), S("sst")
        sx = [S("sx0"), S("sx1")]
        stpS = [S("stpS0"), S("stpS1")]
        stpC = [S("stpC0"), S("stpC1")]
        swv = S("swv")
        swq = [S("swq0"), S("swq1")]
        swk = [S("swk0"), S("swk1")]
        skp = [S("skp0"), S("skp1")]
        sqst = [S("sqst0"), S("sqst1")]
        sist, ssst = S("sist"), S("ssst")
        sxn, ssc = S("sxn"), S("ssc")
        sgw = [S("sgw0"), S("sgw1")]
        suw = [S("suw0"), S("suw1")]
        sdw = [S("sdw0"), S("sdw1")]
        sscw = [S("sscw0"), S("sscw1")]
        sdww = [S("sdww0"), S("sdww1")]
        sgf = [S("sgf0"), S("sgf1")]
        satp = [S("satp0"), S("satp1")]
        sqp = [S("sqp0"), S("sqp1")]
        sin_ = [S("sin0"), S("sin1")]
        ssl = [S("ssl0"), S("ssl1")]
        sgfc = [S("sgfc0"), S("sgfc1")]
        srelg, smnr, sgt2, smsl = S("srelg"), S("smnr"), S("sgt2"), S("smsl")
        ssg = [S("ssg0"), S("ssg1")]
        srlt, swfl, srel, swbf = S("srlt"), S("swfl"), S("srel"), S("swbf")

        block = Gs.enter_context(nc.Block())
        P = Phase(block)

        def barrier():
            vals = [(s, s.v) for nm, s in sorted(_sems.items())
                    if nm not in ("sCC", "sB")]
            bv = sB.inc()
            for s, v in vals:
                if v:
                    P.wait("gpsimd", s, v)
            P.op("gpsimd", lambda g: g.engine_nop().then_inc(sB.h, 1))
            for eng in ("tensor", "vector", "scalar", "sync"):
                P.wait(eng, sB, bv)
            P.emit()

        def dbg_dump(name, src):
            if name in dbg:
                P.op("gpsimd", lambda g, o=dbg[name], i=src: g.dma_start(out=o, in_=i)
                     .then_inc(sGt.h, 16))
                sGt.inc(16)

        for dst, src in ((trilb[:], ap["tril"]), (mnT_f[:], ap["mnTh"]),
                         (mnT_b[:], ap["mnTh"]), (rmk[:], ap["rmask"]),
                         (epsc[:], ap["i_epsc"]), (onesb[:], ap["i_onesb"]),
                         (onesr[:], ap["i_onesr"])):
            P.op("gpsimd", lambda g, d=dst, s=src: g.dma_start(out=d, in_=s).then_inc(sGt.h, 16))
            sGt.inc(16)

        def _init(v):
            v.memset(s_run[:], 0.0)
            v.memset(qsum[:], 0.0)
            v.tensor_copy(out=dvflush[:], in_=s_run[:, 0:1])
            v.drain().then_inc(sDV.h, 1)
        P.op("vector", _init)
        sDV.inc()
        P.emit()
        barrier()

        with ExitStack() as ph:
            x_sb = [sbuf(f"x_sb{i}", [128, D], F32, ph) for i in range(2)]
            xn_tm = [sbuf(f"xn_tm{i}", [128, D], BF16, ph) for i in range(2)]
            scrap = sbuf("scrap", [128, D], BF16, ph)
            xsumt = sbuf("xsumt", [128, 1], F32, ph)
            xsqt = sbuf("xsqt", [128, 1], F32, ph)
            meant = sbuf("meant", [128, 1], F32, ph)
            m2t = sbuf("m2t", [128, 1], F32, ph)
            vart = sbuf("vart", [128, 1], F32, ph)
            stdt = sbuf("stdt", [128, 1], F32, ph)
            rstdt = sbuf("rstdt", [128, 1], F32, ph)
            negmr = sbuf("negmr", [128, 1], F32, ph)
            flushc = sbuf("flushc", [128, 1], F32, ph)
            xcen = sbuf("xcen", [128, D], F32, ph)
            acflush = sbuf("acflush", [128, 1], F32, ph)
            statlog = sbuf("statlog", [128, 8 * C], F32, ph)

            if "stats" in dbg:
                for ii, tl in enumerate((xsumt, xsqt, meant, m2t, vart, stdt, rstdt, flushc)):
                    P.op("gpsimd", lambda g, ii=ii, tl=tl: g.dma_start(
                        out=tl[:], in_=ap["i_sent"][:, ii:ii + 1],
                        allow_slow_non_contiguous=True).then_inc(sGt.h, 16))
                    sGt.inc(16)
                sent_ld = sGt.v
                P.wait("vector", sGt, sent_ld)
                P.wait("scalar", sGt, sent_ld)
            xload = [sx[c % 2].inc(16) for c in range(C)]
            ln_red = [0] * C; ln_var = [0] * C; ln_rcp = [0] * C; ln_xn = [0] * C
            for c in range(C):
                ln_red[c] = sDV.inc(); ln_var[c] = sDV.inc()
                ln_rcp[c] = sDV.inc(); ln_xn[c] = sDV.inc()
            ln_sq = [0] * C; ln_sqrt = [0] * C
            xn_tpS = [0] * C; xn_tpC = [0] * C
            for c in range(C):
                ln_sq[c] = sAC.inc(); ln_sqrt[c] = sAC.inc()
                xn_tpC[c] = stpC[c % 2].inc(16 * 8)
            for c in range(C):
                xn_tpS[c] = stpS[c % 2].inc(16 * 8)
            xn_stash = [sst.inc(16) for _ in range(DB)]

            for c in range(C):
                if c >= 2:
                    P.wait("gpsimd", sDV, ln_xn[c - 2])
                P.op("gpsimd", lambda g, c=c: g.dma_start(
                    out=x_sb[c % 2][:], in_=ap["x"][c * 128:(c + 1) * 128, :]).then_inc(sx[c % 2].h, 16))

            for c in range(C):
                if c >= 1:
                    P.wait("vector", sAC, ln_sqrt[c - 1])
                P.wait("vector", sx[c % 2], xload[c])
                P.op("vector", lambda v, c=c: _dinc(v, v.reduce_sum(
                    out=xsumt[:], in_=x_sb[c % 2][:], axis=AX.X), sDV.h))
                P.wait("vector", sDV, ln_red[c])
                P.wait("vector", sAC, ln_sq[c])
                P.op("vector", lambda v: v.tensor_scalar_mul(out=meant[:], in0=xsumt[:], scalar1=1.0 / D))
                P.op("vector", lambda v: v.drain())
                P.op("vector", lambda v: v.tensor_mul(out=m2t[:], in0=meant[:], in1=meant[:]))
                P.op("vector", lambda v: v.drain())
                P.op("vector", lambda v: v.scalar_tensor_tensor(
                    out=vart[:], in0=xsqt[:], scalar=1.0 / D, in1=m2t[:],
                    op0=ALU.mult, op1=ALU.subtract))
                P.op("vector", lambda v: _dinc(v, v.tensor_copy(out=dvflush[:], in_=vart[:])
                     , sDV.h))
                P.wait("vector", sAC, ln_sqrt[c])
                P.op("vector", lambda v: _dinc(v, v.reciprocal(out=rstdt[:], in_=stdt[:])
                     , sDV.h))
                P.wait("vector", sDV, ln_rcp[c])
                if c >= 2:
                    P.wait("vector", stpS[c % 2], xn_tpS[c - 2])
                    P.wait("vector", stpC[c % 2], xn_tpC[c - 2])
                P.op("vector", lambda v, c=c: v.tensor_scalar(
                    out=xcen[:], in0=x_sb[c % 2][:], scalar1=meant[:],
                    scalar2=None, op0=ALU.subtract))
                P.op("vector", lambda v: v.drain())
                P.op("vector", lambda v, c=c: v.tensor_scalar(
                    out=xn_tm[c % 2][:], in0=xcen[:], scalar1=rstdt[:],
                    scalar2=None, op0=ALU.mult))
                P.op("vector", lambda v: v.drain())
                if "stats" in dbg:
                    for ii, tl in enumerate((xsumt, xsqt, meant, m2t, vart, stdt, rstdt)):
                        P.op("vector", lambda v, c=c, ii=ii, tl=tl: v.tensor_copy(
                            out=statlog[:, c * 8 + ii:c * 8 + ii + 1], in_=tl[:]))
                P.op("vector", lambda v, c=c: _dinc(v, v.reduce_sum(
                    out=flushc[:], in_=xn_tm[c % 2][:], axis=AX.X), sDV.h))

            for c in range(C):
                P.wait("scalar", sx[c % 2], xload[c])
                if c >= 1:
                    P.wait("scalar", sDV, ln_var[c - 1])
                P.op("scalar", lambda s, c=c: s.activation(
                    out=scrap[:], in_=x_sb[c % 2][:], func=AF.Square,
                    accum_out=xsqt[:]))
                P.op("scalar", lambda s: _dinc(s, s.copy(out=acflush[:], in_=xsqt[:]),
                                               sAC.h))
                P.wait("scalar", sDV, ln_var[c])
                if c >= 1:
                    P.wait("scalar", sDV, ln_rcp[c - 1])
                P.op("scalar", lambda s: s.activation(
                    out=stdt[:], in_=vart[:], func=AF.Sqrt, bias=epsc[:]))
                P.op("scalar", lambda s: _dinc(s, s.copy(out=acflush[:], in_=stdt[:]),
                                               sAC.h))


            for c in range(C):
                P.wait("sync", sDV, ln_xn[c])
                for db in range(8):
                    P.op("sync", lambda e, c=c, db=db: e.dma_start_transpose(
                        xn_fm[:, db * T + c * 128: db * T + (c + 1) * 128],
                        xn_tm[c % 2][:, db * 128:(db + 1) * 128]).then_inc(stpS[c % 2].h, 16))
                for db in range(8, 16):
                    P.op("sync", lambda e, c=c, db=db: e.dma_start_transpose(
                        xn_fm[:, db * T + c * 128: db * T + (c + 1) * 128],
                        xn_tm[c % 2][:, db * 128:(db + 1) * 128]).then_inc(stpC[c % 2].h, 16))
            P.wait("sync", stpC[0], xn_tpC[C - 2])
            P.wait("sync", stpC[1], xn_tpC[C - 1])
            P.wait("sync", stpS[0], xn_tpS[C - 2])
            P.wait("sync", stpS[1], xn_tpS[C - 1])
            for db in range(DB):
                P.op("sync", lambda e, db=db: e.dma_start(
                    out=st_xn[db], in_=xn_fm[:, db * T:(db + 1) * T]).then_inc(sst.h, 16))
            P.emit()
            barrier()
        dbg_dump("stxn", st_xn)
        if "stats" in dbg:
            P.op("sync", lambda e: e.dma_start(out=dbg["stats"], in_=statlog[:])
                 .then_inc(sst.h, 16))
            sst.inc(16)
        if "xnfm" in dbg:
            P.op("sync", lambda e: e.dma_start(out=dbg["xnfm"], in_=xn_fm[:])
                 .then_inc(sst.h, 16))
            sst.inc(16)
            barrier()
        P.emit()
        if upto <= 1:
            return nc

        with ExitStack() as ph:
            wv_b = sbuf("wv_b", [128, DB * 1024], BF16, ph)
            wvl = [[swv.inc(16) for _ in range(DB)] for _ in range(2)]
            pe_v = [sPE.inc() for _ in range(2 * C)]
            vev = [sDV.inc() for _ in range(2 * C)]

            P.wait("gpsimd", sDV, ln_xn[C - 1])
            P.wait("gpsimd", sAC, sAC.v)
            for hf in range(2):
                for db in range(DB):
                    if hf == 1 and db == 0:
                        P.wait("gpsimd", sPE, pe_v[C - 1])
                    P.op("gpsimd", lambda g, hf=hf, db=db: g.dma_start(
                        out=wv_b[:, db * 1024:(db + 1) * 1024],
                        in_=ap["wvT"][db][:, hf * 1024:(hf + 1) * 1024]).then_inc(swv.h, 16))

            for hf in range(2):
                for tc in range(C):
                    i = hf * C + tc
                    P.wait("tensor", swv, wvl[hf][DB - 1])
                    if i >= 2:
                        P.wait("tensor", sDV, vev[i - 2])
                    base = (tc % 2) * 1024
                    for db in range(DB):
                        lhs = xn_fm[:, db * T + tc * 128: db * T + (tc + 1) * 128]
                        for sp in range(2):
                            o = pA[:, base + sp * 512: base + (sp + 1) * 512]
                            rhs = wv_b[:, db * 1024 + sp * 512: db * 1024 + (sp + 1) * 512]
                            if db == DB - 1 and sp == 1:
                                P.op("tensor", lambda t, o=o, a=lhs, b=rhs:
                                     t.matmul(o, a, b, start=False, stop=True).then_inc(sPE.h, 1))
                            else:
                                P.op("tensor", lambda t, o=o, a=lhs, b=rhs, st=(db == 0),
                                     fi=(db == DB - 1): t.matmul(o, a, b, start=st, stop=fi))

            for hf in range(2):
                for tc in range(C):
                    i = hf * C + tc
                    P.wait("vector", sPE, pe_v[i])
                    base = (tc % 2) * 1024
                    P.op("vector", lambda v, tc=tc, hf=hf, base=base: _dinc(v, v.tensor_copy(
                        out=v_tm[:, tc * D + hf * 1024: tc * D + (hf + 1) * 1024],
                        in_=pA[:, base: base + 1024]), sDV.h))
            P.emit()
            v_done_dv = sDV.v
        if upto <= 2:
            barrier()
            return nc

        with ExitStack() as ph:
            wq_b = [sbuf(f"wq_b{i}", [128, D], BF16, ph) for i in range(2)]
            wk_b = [sbuf(f"wk_b{i}", [128, D], BF16, ph) for i in range(2)]
            q_sb = sbuf("q_sb", [128, T], BF16, ph)
            k_sb = sbuf("k_sb", [128, T], BF16, ph)
            qp_h = [sbuf(f"qp_h{i}", [128, T], BF16, ph) for i in range(2)]
            kp_fm = [sbuf(f"kp_fm{i}", [128, T], BF16, ph) for i in range(2)]
            kp_tm = [sbuf(f"kp_tm{i}", [128, T], BF16, ph) for i in range(2)]
            tmin = sbuf("tmin", [128, T], BF16, ph)
            texp = sbuf("texp", [128, T], BF16, ph)
            scm = [sbuf(f"scm{i}", [128, 128], BF16, ph) for i in range(2)]
            intra_stg = [sbuf("instg", [128, C * 129], BF16, ph)] * 2
            sloc_stg = [sbuf("slstg", [128, C * 129], BF16, ph)] * 2
            relp = sbuf("relp", [16, 129], F32, ph)

            wq_l = [0] * H; wk_l = [0] * H
            q_done = [0] * H; k_done = [0] * H
            sc_done = [[0] * C for _ in range(H)]
            kv_done = [[0] * C for _ in range(H)]
            num_done = [[0] * C for _ in range(H)]
            for h in range(H):
                wq_l[h] = swq[h % 2].inc(16)
                wk_l[h] = swk[h % 2].inc(16)
            for h in range(H):
                q_done[h] = sPE.inc()
                k_done[h] = sPE.inc()
                for c in range(C):
                    sc_done[h][c] = sPE.inc()
                    kv_done[h][c] = sPE.inc()
                    num_done[h][c] = sPE.inc()
            rel_done = sPE.inc()
            q_ev = [0] * H; texp_qv = [0] * H; k_ev = [0] * H; texp_kv = [0] * H
            for h in range(H):
                q_ev[h] = sAC.inc()
                texp_qv[h] = sAC.inc()
                k_ev[h] = sAC.inc()
                texp_kv[h] = sAC.inc()
            qp_dv = [0] * H; kp_dv = [0] * H
            mask_dv = [[0] * C for _ in range(H)]
            sadd_dv = [[0] * C for _ in range(H)]
            nev_dv = [[0] * C for _ in range(H)]
            for h in range(H):
                sDV.inc()
                qp_dv[h] = sDV.inc()
                sDV.inc()
                kp_dv[h] = sDV.inc()
                for c in range(C):
                    mask_dv[h][c] = sDV.inc()
                    sadd_dv[h][c] = sDV.inc()
                    nev_dv[h][c] = sDV.inc()
            relp_dv = sDV.inc()
            kp_tp = [skp[h % 2].inc(16 * C) for h in range(H)]
            cc_wr = sst.inc(16 * (H + 1))
            qp_st = [0] * H; intra_st = [0] * H; sloc_st = [0] * H
            for h in range(H):
                qp_st[h] = sqst[h % 2].inc(16)
                intra_st[h] = sist.inc(16)
                sloc_st[h] = ssst.inc(16)
            cc_done = sCC.inc()

            P.wait("gpsimd", sPE, pe_v[2 * C - 1])
            P.wait("gpsimd", stpS[1], xn_tpS[C - 1])
            P.wait("gpsimd", stpC[1], xn_tpC[C - 1])
            P.wait("gpsimd", sst, xn_stash[DB - 1])
            for h in range(H):
                if h >= 2:
                    P.wait("gpsimd", sPE, q_done[h - 2])
                P.op("gpsimd", lambda g, h=h: g.dma_start(
                    out=wq_b[h % 2].ap().rearrange("p (a b) -> p a b", a=DB),
                    in_=ap["wqT"][h].rearrange("a p b -> p a b")).then_inc(swq[h % 2].h, 16))
                if h >= 2:
                    P.wait("gpsimd", sPE, k_done[h - 2])
                P.op("gpsimd", lambda g, h=h: g.dma_start(
                    out=wk_b[h % 2].ap().rearrange("p (a b) -> p a b", a=DB),
                    in_=ap["wkT"][h].rearrange("a p b -> p a b")).then_inc(swk[h % 2].h, 16))
            P.wait("gpsimd", sst, cc_wr)
            if no_cc:
                P.op("gpsimd", lambda g: g.engine_nop().then_inc(sCC.h))
            else:
                P.op("gpsimd", lambda g: g.collective_compute(
                    "AllGather", ALU.bypass, replica_groups=[list(range(NCORE))],
                    ins=[cc_in.opt()], outs=[cc_out.opt()]).then_inc(sCC.h))

            for h in range(H):
                P.wait("tensor", swq[h % 2], wq_l[h])
                if h == 0:
                    P.wait("tensor", sDV, v_done_dv)
                if h >= 1:
                    P.wait("tensor", sDV, nev_dv[h - 1][C - 1])
                    P.wait("tensor", sDV, mask_dv[h - 1][C - 1])
                for db in range(DB):
                    lhs = wq_b[h % 2][:, db * 128:(db + 1) * 128]
                    for sp in range(4):
                        o = pA[:, sp * 512:(sp + 1) * 512]
                        rhs = xn_fm[:, db * T + sp * 512: db * T + (sp + 1) * 512]
                        if db == DB - 1 and sp == 3:
                            P.op("tensor", lambda t, o=o, a=lhs, b=rhs:
                                 t.matmul(o, a, b, start=False, stop=True).then_inc(sPE.h, 1))
                        else:
                            P.op("tensor", lambda t, o=o, a=lhs, b=rhs, st=(db == 0),
                                 fi=(db == DB - 1): t.matmul(o, a, b, start=st, stop=fi))
                P.wait("tensor", swk[h % 2], wk_l[h])
                if h >= 1:
                    P.wait("tensor", sDV, sadd_dv[h - 1][C - 1])
                for db in range(DB):
                    lhs = wk_b[h % 2][:, db * 128:(db + 1) * 128]
                    for sp in range(4):
                        o = pB[:, sp * 512:(sp + 1) * 512]
                        rhs = xn_fm[:, db * T + sp * 512: db * T + (sp + 1) * 512]
                        if db == DB - 1 and sp == 3:
                            P.op("tensor", lambda t, o=o, a=lhs, b=rhs:
                                 t.matmul(o, a, b, start=False, stop=True).then_inc(sPE.h, 1))
                        else:
                            P.op("tensor", lambda t, o=o, a=lhs, b=rhs, st=(db == 0),
                                 fi=(db == DB - 1): t.matmul(o, a, b, start=st, stop=fi))
                P.wait("tensor", sDV, qp_dv[h])
                P.wait("tensor", sDV, kp_dv[h])
                P.wait("tensor", skp[h % 2], kp_tp[h])
                P.wait("tensor", sAC, q_ev[h])
                P.wait("tensor", sAC, k_ev[h])
                for c in range(C):
                    cs = slice(c * 128, (c + 1) * 128)
                    vt = v_tm[:, c * D + h * 128: c * D + (h + 1) * 128]
                    if c >= 2:
                        P.wait("tensor", sDV, mask_dv[h][c - 2])
                    P.op("tensor", lambda t, o=pA[:, (c % 2) * 512:(c % 2) * 512 + 128],
                         a=kp_fm[h % 2][:, cs], b=qp_h[h % 2][:, cs]:
                         t.matmul(o, a, b, start=True, stop=True).then_inc(sPE.h, 1))
                    if c >= 2:
                        P.wait("tensor", sDV, sadd_dv[h][c - 2])
                    P.op("tensor", lambda t, o=pB[:, (c % 2) * 512:(c % 2) * 512 + 128],
                         a=kp_tm[h % 2][:, cs], b=vt:
                         t.matmul(o, a, b, start=True, stop=True))
                    P.op("tensor", lambda t, o=pB[:, (c % 2) * 512 + 128:(c % 2) * 512 + 129],
                         a=kp_tm[h % 2][:, cs]:
                         t.matmul(o, a, onesb[:], start=True, stop=True).then_inc(sPE.h, 1))
                    P.wait("tensor", sDV, mask_dv[h][c])
                    if c >= 2:
                        P.wait("tensor", sDV, nev_dv[h][c - 2])
                    P.op("tensor", lambda t, o=pA[:, (2 + c % 2) * 512:(2 + c % 2) * 512 + 128],
                         a=scm[c % 2][:], b=vt:
                         t.matmul(o, a, b, start=True, stop=True))
                    P.op("tensor", lambda t, o=pA[:, (2 + c % 2) * 512 + 128:(2 + c % 2) * 512 + 129],
                         a=scm[c % 2][:]:
                         t.matmul(o, a, onesb[:], start=True, stop=True).then_inc(sPE.h, 1))
            P.wait("tensor", sAC, q_ev[H - 1])
            P.wait("tensor", sDV, sadd_dv[H - 1][C - 1])
            P.wait("tensor", sDV, nev_dv[H - 1][C - 1])
            P.wait("tensor", sDV, mask_dv[H - 1][C - 1])
            for h in range(H):
                o = pA[0:1, h * 4:(h + 1) * 4]
                rhs = mnT_f[:, h * 4:(h + 1) * 4]
                if h == H - 1:
                    P.op("tensor", lambda t, o=o, a=qsum[:, h:h + 1], b=rhs:
                         t.matmul(o, a, b, start=True, stop=True).then_inc(sPE.h, 1))
                else:
                    P.op("tensor", lambda t, o=o, a=qsum[:, h:h + 1], b=rhs:
                         t.matmul(o, a, b, start=True, stop=True))

            for h in range(H):
                P.wait("scalar", sPE, q_done[h])
                if h >= 1:
                    P.wait("scalar", sDV, qp_dv[h - 1])
                P.op("scalar", lambda s, h=h: _dinc(s, s.activation(
                    out=q_sb[:], in_=pA[:, 0:2048], func=AF.Copy,
                    accum_out=qsum[:, h:h + 1]), sAC.h))
                P.wait("scalar", sDV, qp_dv[h] - 1)
                P.op("scalar", lambda s: _dinc(s, s.activation(
                    out=texp[:], in_=tmin[:], func=AF.Exp), sAC.h))
                P.wait("scalar", sPE, k_done[h])
                if h >= 1:
                    P.wait("scalar", sDV, kp_dv[h - 1])
                P.op("scalar", lambda s, h=h: _dinc(s, s.activation(
                    out=k_sb[:], in_=pB[:, 0:2048], func=AF.Copy), sAC.h))
                P.wait("scalar", sDV, kp_dv[h] - 1)
                P.op("scalar", lambda s: _dinc(s, s.activation(
                    out=texp[:], in_=tmin[:], func=AF.Exp), sAC.h))
                P.wait("scalar", sDV, qp_dv[h])
                P.op("scalar", lambda e, h=h: e.dma_start(
                    out=st_qp[h], in_=qp_h[h % 2][:]).then_inc(sqst[h % 2].h, 16))
                P.wait("scalar", sDV, nev_dv[h][C - 1])
                P.op("scalar", lambda e, h=h: e.dma_start(
                    out=st_intra[h], in_=intra_stg[h % 2][:]).then_inc(sist.h, 16))
                P.wait("scalar", sDV, sadd_dv[h][C - 1])
                P.op("scalar", lambda e, h=h: e.dma_start(
                    out=st_sloc[h], in_=sloc_stg[h % 2][:]).then_inc(ssst.h, 16))

            P.op("vector", lambda v: v.memset(relp[:], 0.0))
            for h in range(H):
                P.wait("vector", sAC, q_ev[h])
                P.op("vector", lambda v: _dinc(v, v.tensor_scalar_min(
                    out=tmin[:], in0=q_sb[:], scalar1=0.0), sDV.h))
                P.wait("vector", sAC, texp_qv[h])
                if h >= 2:
                    P.wait("vector", sqst[h % 2], qp_st[h - 2])
                    P.wait("vector", sPE, sc_done[h - 2][C - 1])
                    P.wait("vector", sPE, num_done[h - 2][C - 1])
                P.op("vector", lambda v, h=h: _dinc(v, v.scalar_tensor_tensor(
                    out=qp_h[h % 2][:], in0=q_sb[:], scalar=0.0, in1=texp[:],
                    op0=ALU.max, op1=ALU.add), sDV.h))
                P.wait("vector", sAC, k_ev[h])
                P.op("vector", lambda v: _dinc(v, v.tensor_scalar_min(
                    out=tmin[:], in0=k_sb[:], scalar1=0.0), sDV.h))
                P.wait("vector", sAC, texp_kv[h])
                if h >= 2:
                    P.wait("vector", skp[h % 2], kp_tp[h - 2])
                P.op("vector", lambda v, h=h: _dinc(v, v.scalar_tensor_tensor(
                    out=kp_fm[h % 2][:], in0=k_sb[:], scalar=0.0, in1=texp[:],
                    op0=ALU.max, op1=ALU.add), sDV.h))
                if h >= 1:
                    P.wait("vector", sist, intra_st[h - 1])
                    P.wait("vector", ssst, sloc_st[h - 1])
                for c in range(C):
                    P.wait("vector", sPE, sc_done[h][c])
                    if c >= 2:
                        P.wait("vector", sPE, num_done[h][c - 2])
                    elif h >= 1:
                        P.wait("vector", sPE, num_done[h - 1][C - 2 + c])
                    P.op("vector", lambda v, c=c: _dinc(v, v.tensor_tensor(
                        out=scm[c % 2][:], in0=pA[:, (c % 2) * 512:(c % 2) * 512 + 128],
                        in1=trilb[:], op=ALU.mult), sDV.h))
                    P.op("vector", lambda v, h=h, c=c: v.tensor_copy(
                        out=sloc_stg[h % 2][:, c * 129:(c + 1) * 129],
                        in_=s_run[:, h * 129:(h + 1) * 129]))
                    P.op("vector", lambda v: v.drain())
                    P.wait("vector", sPE, kv_done[h][c])
                    P.op("vector", lambda v, h=h, c=c: _dinc(v, v.tensor_add(
                        out=s_run[:, h * 129:(h + 1) * 129],
                        in0=s_run[:, h * 129:(h + 1) * 129],
                        in1=pB[:, (c % 2) * 512:(c % 2) * 512 + 129]), sDV.h))
                    P.wait("vector", sPE, num_done[h][c])
                    P.op("vector", lambda v, h=h, c=c: _dinc(v, v.tensor_copy(
                        out=intra_stg[h % 2][:, c * 129:(c + 1) * 129],
                        in_=pA[:, (2 + c % 2) * 512:(2 + c % 2) * 512 + 129]), sDV.h))
            P.wait("vector", sPE, rel_done)
            P.op("vector", lambda v: _dinc(v, v.tensor_scalar_mul(
                out=relp[0:1, 0:H * M], in0=pA[0:1, 0:H * M],
                scalar1=SCALE / (T * NCORE)), sDV.h))

            for h in range(H):
                P.wait("sync", sDV, kp_dv[h])
                if h >= 2:
                    P.wait("sync", sPE, kv_done[h - 2][C - 1])
                for c in range(C):
                    P.op("sync", lambda e, h=h, c=c: e.dma_start_transpose(
                        kp_tm[h % 2][:, c * 128:(c + 1) * 128],
                        kp_fm[h % 2][:, c * 128:(c + 1) * 128]).then_inc(skp[h % 2].h, 16))
            for h in range(H):
                P.wait("sync", sDV, sadd_dv[h][C - 1])
                P.op("sync", lambda e, h=h: e.dma_start(
                    out=cc_in[h * 128:(h + 1) * 128, :],
                    in_=s_run[:, h * 129:(h + 1) * 129]).then_inc(sst.h, 16))
            P.wait("sync", sDV, relp_dv)
            P.op("sync", lambda e: e.dma_start(
                out=cc_in[H * 128:H * 128 + 16, :], in_=relp[:, 0:129]).then_inc(sst.h, 16))
            P.emit()
            barrier()
        dbg_dump("stqp", st_qp)
        dbg_dump("stintra", st_intra)
        dbg_dump("stsloc", st_sloc)
        dbg_dump("srun", s_run[:])
        P.emit()
        attS.close()
        if upto <= 3:
            return nc

        with ExitStack() as ph:
            h_buf = sbuf("h_buf", [128, IB * TTN], BF16, ph)
            xnsl = sbuf("xnsl", [128, HB * TTN], BF16, ph)
            gw_q = [sbuf(f"gw_q{i}", [128, NBW * HB * 128], BF16, ph) for i in range(2)]
            uw_q = [sbuf(f"uw_q{i}", [128, NBW * HB * 128], BF16, ph) for i in range(2)]
            dw_q = [sbuf(f"dw_q{i}", [128, IB * 128], BF16, ph) for i in range(2)]
            g_sb = [sbuf(f"g_sb{i}", [128, TTN], BF16, ph) for i in range(2)]
            u_sb = [sbuf(f"u_sb{i}", [128, TTN], BF16, ph) for i in range(2)]
            gfst = [sbuf(f"gfst{i}", [128, 512], BF16, ph) for i in range(2)]
            gftst = [sbuf(f"gftst{i}", [128, 512], BF16, ph) for i in range(2)]
            gst = sbuf("gst", [128, IB], F32, ph)
            ust = sbuf("ust", [128, IB], F32, ph)
            dsc = sbuf("dsc", [128, HB], F32, ph)

            for dst, src in ((gst[:], ap["gsT"]), (ust[:], ap["usT"]), (dsc[:], ap["dsT"])):
                P.op("gpsimd", lambda g, d=dst, s=src: g.dma_start(out=d, in_=s).then_inc(ssc.h, 16))
            ld_scales = ssc.inc(48)

            prev = None
            for tt in range(NTT):
                xnl = [sxn.inc(16) for _ in range(HB)]
                gwl = [0] * NWB; uwl = [0] * NWB
                for b in range(NWB):
                    gwl[b] = sgw[b % 2].inc(16)
                    uwl[b] = suw[b % 2].inc(16)
                dwl = [sdw[hb % 2].inc(16) for hb in range(HB)]
                g_done = [0] * IB; u_done = [0] * IB
                for ib in range(IB):
                    g_done[ib] = sPE.inc()
                    u_done[ib] = sPE.inc()
                dn_done = [[sPE.inc() for _ in range(NSP)] for _ in range(HB)]
                gwsh = [0] * NWB; uwsh = [0] * NWB
                uev = [0] * IB; h_done = [0] * IB
                for ib in range(IB):
                    if tt == 0 and ib % NBW == 0:
                        gwsh[ib // NBW] = sDV.inc()
                        uwsh[ib // NBW] = sDV.inc()
                    uev[ib] = sDV.inc()
                    h_done[ib] = sDV.inc()
                dwsh = [sDV.inc() if tt == 0 else 0 for _ in range(HB)]
                g_ev = [sAC.inc() for _ in range(IB)]
                scw_g = [0] * NWB; scw_u = [0] * NWB
                if tt == 0:
                    for b in range(NWB):
                        scw_g[b] = sscw[b % 2].inc(16)
                        scw_u[b] = sscw[b % 2].inc(16)
                gf_ev = [[0] * NSP for _ in range(HB)]
                scw_d = [0] * HB
                for hb in range(HB):
                    if tt == 0:
                        scw_d[hb] = sdww[hb % 2].inc(16)
                    for ts in range(NSP):
                        gf_ev[hb][ts] = sAC.inc()
                gf_tpa = [[0] * NSP for _ in range(HB)]
                gf_tp = [[0] * NSP for _ in range(HB)]
                for hb in range(HB):
                    for ts in range(NSP):
                        par = (hb * NSP + ts) % 2
                        gf_tpa[hb][ts] = sgf[par].inc(16 * 4)
                        gf_tp[hb][ts] = sgf[par].inc(16 * 4)

                if prev is not None:
                    P.wait("gpsimd", sPE, prev["u_done"][IB - 1])
                for hb in range(HB):
                    P.op("gpsimd", lambda g, hb=hb, tt=tt: g.dma_start(
                        out=xnsl[:, hb * TTN:(hb + 1) * TTN],
                        in_=st_xn[hb][:, tt * TTN:(tt + 1) * TTN]).then_inc(sxn.h, 16))
                for b in range(NWB):
                    if b >= 2:
                        P.wait("gpsimd", sPE, g_done[(b - 2) * NBW + NBW - 1])
                        if tt == 0:
                            P.wait("gpsimd", sscw[b % 2], scw_g[b - 2])
                    elif prev is not None:
                        P.wait("gpsimd", sPE, prev["g_done"][IB - 1])
                        if tt == 1:
                            P.wait("gpsimd", sscw[b % 2], prev["scw_g"][NWB - 2 + b])
                    if tt == 0:
                        P.op("gpsimd", lambda g, b=b: g.dma_start(
                            out=gw_q[b % 2].ap().rearrange("p (n a b) -> p n a b", n=NBW, a=HB),
                            in_=ap["gw"][b * NBW:(b + 1) * NBW].rearrange("n a p b -> p n a b"))
                            .then_inc(sgw[b % 2].h, 16))
                    else:
                        P.op("gpsimd", lambda g, b=b: g.dma_start(
                            out=gw_q[b % 2].ap().rearrange("p (n f) -> p n f", n=NBW),
                            in_=scr_gw[b * NBW:(b + 1) * NBW].rearrange("n p f -> p n f"))
                            .then_inc(sgw[b % 2].h, 16))
                    if b >= 2:
                        P.wait("gpsimd", sPE, u_done[(b - 2) * NBW + NBW - 1])
                        if tt == 0:
                            P.wait("gpsimd", sscw[b % 2], scw_u[b - 2])
                    elif prev is not None:
                        P.wait("gpsimd", sPE, prev["u_done"][IB - 1])
                        if tt == 1:
                            P.wait("gpsimd", sscw[b % 2], prev["scw_u"][NWB - 2 + b])
                    if tt == 0:
                        P.op("gpsimd", lambda g, b=b: g.dma_start(
                            out=uw_q[b % 2].ap().rearrange("p (n a b) -> p n a b", n=NBW, a=HB),
                            in_=ap["uw"][b * NBW:(b + 1) * NBW].rearrange("n a p b -> p n a b"))
                            .then_inc(suw[b % 2].h, 16))
                    else:
                        P.op("gpsimd", lambda g, b=b: g.dma_start(
                            out=uw_q[b % 2].ap().rearrange("p (n f) -> p n f", n=NBW),
                            in_=scr_uw[b * NBW:(b + 1) * NBW].rearrange("n p f -> p n f"))
                            .then_inc(suw[b % 2].h, 16))
                for hb in range(HB):
                    if hb >= 2:
                        P.wait("gpsimd", sPE, dn_done[hb - 2][NSP - 1])
                        if tt == 0:
                            P.wait("gpsimd", sdww[hb % 2], scw_d[hb - 2])
                    elif prev is not None:
                        P.wait("gpsimd", sPE, prev["dn_done"][HB - 1][NSP - 1])
                        if tt == 1:
                            P.wait("gpsimd", sdww[hb % 2], prev["scw_d"][HB - 2 + hb])
                    if tt == 0:
                        P.op("gpsimd", lambda g, hb=hb: g.dma_start(
                            out=dw_q[hb % 2].ap().rearrange("p (a b) -> p a b", a=IB),
                            in_=ap["dw"][hb].rearrange("a p b -> p a b")).then_inc(sdw[hb % 2].h, 16))
                    else:
                        P.op("gpsimd", lambda g, hb=hb: g.dma_start(
                            out=dw_q[hb % 2][:], in_=scr_dw[hb]).then_inc(sdw[hb % 2].h, 16))

                for ib in range(IB):
                    if tt == 0 and ib % NBW == 0:
                        b = ib // NBW
                        P.wait("vector", sgw[b % 2], gwl[b])
                        P.op("vector", lambda v, b=b: _dinc(v, v.tensor_scalar_add(
                            out=gw_q[b % 2][:], in0=gw_q[b % 2][:], scalar1=-128.0)
                            , sDV.h))
                        P.wait("vector", suw[b % 2], uwl[b])
                        P.op("vector", lambda v, b=b: _dinc(v, v.tensor_scalar_add(
                            out=uw_q[b % 2][:], in0=uw_q[b % 2][:], scalar1=-128.0)
                            , sDV.h))
                    if ib == 0 and prev is not None:
                        P.wait("vector", sPE, prev["dn_done"][HB - 1][NSP - 1])
                    P.wait("vector", sPE, u_done[ib])
                    ubase = TTN * (ib % 2)
                    P.op("vector", lambda v, ib=ib, ubase=ubase: _dinc(v, v.tensor_scalar_mul(
                        out=u_sb[ib % 2][:], in0=pB[:, ubase:ubase + TTN],
                        scalar1=ust[:, ib:ib + 1]), sDV.h))
                    P.wait("vector", sAC, g_ev[ib])
                    P.op("vector", lambda v, ib=ib: _dinc(v, v.tensor_mul(
                        out=h_buf[:, ib * TTN:(ib + 1) * TTN],
                        in0=g_sb[ib % 2][:], in1=u_sb[ib % 2][:]), sDV.h))
                if tt == 0:
                    for hb in range(HB):
                        P.wait("vector", sdw[hb % 2], dwl[hb])
                        P.op("vector", lambda v, hb=hb: _dinc(v, v.tensor_scalar_add(
                            out=dw_q[hb % 2][:], in0=dw_q[hb % 2][:], scalar1=-128.0)
                            , sDV.h))

                for ib in range(IB):
                    gbase = TTN * (ib % 2)
                    bi, bo = ib // NBW, ib % NBW
                    if ib == 0:
                        P.wait("tensor", sxn, xnl[HB - 1])
                        P.wait("tensor", ssc, ld_scales)
                        if prev is not None:
                            P.wait("tensor", sAC, prev["gf_ev"][HB - 1][NSP - 1])
                            P.wait("tensor", sDV, prev["uev"][IB - 1])
                    if bo == 0:
                        if tt == 0:
                            P.wait("tensor", sDV, gwsh[bi])
                        else:
                            P.wait("tensor", sgw[bi % 2], gwl[bi])
                    if ib >= 2:
                        P.wait("tensor", sAC, g_ev[ib - 2])
                    for hb in range(HB):
                        lhs = gw_q[bi % 2][:, (bo * HB + hb) * 128:(bo * HB + hb + 1) * 128]
                        for sp in range(NSP):
                            o = pA[:, gbase + sp * 512: gbase + (sp + 1) * 512]
                            rhs = xnsl[:, hb * TTN + sp * 512: hb * TTN + (sp + 1) * 512]
                            if hb == HB - 1 and sp == NSP - 1:
                                P.op("tensor", lambda t, o=o, a=lhs, b=rhs:
                                     t.matmul(o, a, b, start=False, stop=True).then_inc(sPE.h, 1))
                            else:
                                P.op("tensor", lambda t, o=o, a=lhs, b=rhs, st=(hb == 0),
                                     fi=(hb == HB - 1): t.matmul(o, a, b, start=st, stop=fi))
                    if bo == 0:
                        if tt == 0:
                            P.wait("tensor", sDV, uwsh[bi])
                        else:
                            P.wait("tensor", suw[bi % 2], uwl[bi])
                    if ib >= 2:
                        P.wait("tensor", sDV, uev[ib - 2])
                    for hb in range(HB):
                        lhs = uw_q[bi % 2][:, (bo * HB + hb) * 128:(bo * HB + hb + 1) * 128]
                        for sp in range(NSP):
                            o = pB[:, gbase + sp * 512: gbase + (sp + 1) * 512]
                            rhs = xnsl[:, hb * TTN + sp * 512: hb * TTN + (sp + 1) * 512]
                            if hb == HB - 1 and sp == NSP - 1:
                                P.op("tensor", lambda t, o=o, a=lhs, b=rhs:
                                     t.matmul(o, a, b, start=False, stop=True).then_inc(sPE.h, 1))
                            else:
                                P.op("tensor", lambda t, o=o, a=lhs, b=rhs, st=(hb == 0),
                                     fi=(hb == HB - 1): t.matmul(o, a, b, start=st, stop=fi))
                if tt == 0 and "hbuf0" in dbg:
                    P.wait("sync", sDV, h_done[IB - 1])
                    P.op("sync", lambda e: e.dma_start(out=dbg["hbuf0"], in_=h_buf[:])
                         .then_inc(sst.h, 16))
                    sst.inc(16)
                if tt == 0 and "xnsl0" in dbg:
                    P.wait("sync", sxn, xnl[HB - 1])
                    P.op("sync", lambda e: e.dma_start(out=dbg["xnsl0"], in_=xnsl[:])
                         .then_inc(sst.h, 16))
                    sst.inc(16)
                P.wait("tensor", sDV, h_done[IB - 1])
                P.wait("tensor", sAC, g_ev[IB - 1])
                for hb in range(HB):
                    for ts in range(NSP):
                        idx = hb * NSP + ts
                        par = idx % 2
                        if ts == 0:
                            if tt == 0:
                                P.wait("tensor", sDV, dwsh[hb])
                            else:
                                P.wait("tensor", sdw[hb % 2], dwl[hb])
                        if idx >= 2:
                            P.wait("tensor", sAC, gf_ev[(idx - 2) // NSP][(idx - 2) % NSP])
                        for ib in range(IB):
                            lhs = dw_q[hb % 2][:, ib * 128:(ib + 1) * 128]
                            rhs = h_buf[:, ib * TTN + ts * 512: ib * TTN + (ts + 1) * 512]
                            o = pA[:, par * 512:(par + 1) * 512]
                            if ib == IB - 1:
                                P.op("tensor", lambda t, o=o, a=lhs, b=rhs:
                                     t.matmul(o, a, b, start=False, stop=True).then_inc(sPE.h, 1))
                            else:
                                P.op("tensor", lambda t, o=o, a=lhs, b=rhs, st=(ib == 0):
                                     t.matmul(o, a, b, start=st, stop=False))

                for ib in range(IB):
                    P.wait("scalar", sPE, g_done[ib])
                    if ib >= 2:
                        P.wait("scalar", sDV, h_done[ib - 2])
                    elif prev is not None:
                        P.wait("scalar", sDV, prev["h_done"][IB - 1])
                    gbase = TTN * (ib % 2)
                    P.op("scalar", lambda s, ib=ib, gbase=gbase: _dinc(s, s.activation(
                        out=g_sb[ib % 2][:], in_=pA[:, gbase:gbase + TTN], func=AF.Silu,
                        scale=gst[:, ib:ib + 1]), sAC.h))
                    if tt == 0 and ib % NBW == 0:
                        b = ib // NBW
                        P.wait("scalar", sDV, gwsh[b])
                        P.op("scalar", lambda e, b=b: e.dma_start(
                            out=scr_gw[b * NBW:(b + 1) * NBW].rearrange("n p f -> p n f"),
                            in_=gw_q[b % 2].ap().rearrange("p (n f) -> p n f", n=NBW))
                            .then_inc(sscw[b % 2].h, 16))
                        P.wait("scalar", sDV, uwsh[b])
                        P.op("scalar", lambda e, b=b: e.dma_start(
                            out=scr_uw[b * NBW:(b + 1) * NBW].rearrange("n p f -> p n f"),
                            in_=uw_q[b % 2].ap().rearrange("p (n f) -> p n f", n=NBW))
                            .then_inc(sscw[b % 2].h, 16))
                for hb in range(HB):
                    if tt == 0:
                        P.wait("scalar", sDV, dwsh[hb])
                        P.op("scalar", lambda e, hb=hb: e.dma_start(
                            out=scr_dw[hb], in_=dw_q[hb % 2][:]).then_inc(sdww[hb % 2].h, 16))
                    for ts in range(NSP):
                        idx = hb * NSP + ts
                        par = idx % 2
                        P.wait("scalar", sPE, dn_done[hb][ts])
                        if idx >= 2:
                            P.wait("scalar", sgf[idx % 2], gf_tp[(idx - 2) // NSP][(idx - 2) % NSP])
                        elif prev is not None:
                            P.wait("scalar", sgf[idx % 2], prev["gf_tp"][HB - 1][NSP - 1])
                        P.op("scalar", lambda s, hb=hb, par=par: _dinc(s, s.activation(
                            out=gfst[par][:], in_=pA[:, par * 512:(par + 1) * 512],
                            func=AF.Silu, scale=dsc[:, hb:hb + 1]), sAC.h))

                for hb in range(HB):
                    for ts in range(NSP):
                        idx = hb * NSP + ts
                        par = idx % 2
                        P.wait("sync", sAC, gf_ev[hb][ts])
                        if idx >= 2:
                            P.wait("sync", sgf[par], gf_tp[(idx - 2) // NSP][(idx - 2) % NSP])
                        elif prev is not None:
                            P.wait("sync", sgf[par], prev["gf_tp"][HB - 1][NSP - 1])
                        for tc4 in range(4):
                            P.op("sync", lambda e, par=par, tc4=tc4: e.dma_start_transpose(
                                gftst[par][:, tc4 * 128:(tc4 + 1) * 128],
                                gfst[par][:, tc4 * 128:(tc4 + 1) * 128]).then_inc(sgf[par].h, 16))
                        P.wait("sync", sgf[par], gf_tpa[hb][ts])
                        for tc4 in range(4):
                            tchunk = tt * (TTN // 128) + ts * 4 + tc4
                            P.op("sync", lambda e, par=par, tc4=tc4, tchunk=tchunk, hb=hb:
                                 e.dma_start(
                                     out=st_gf[tchunk][:, hb * 128:(hb + 1) * 128],
                                     in_=gftst[par][:, tc4 * 128:(tc4 + 1) * 128])
                                 .then_inc(sgf[par].h, 16))
                P.emit()
                prev = dict(g_done=g_done, u_done=u_done, dn_done=dn_done,
                            gf_ev=gf_ev, gf_tp=gf_tp, h_done=h_done, uev=uev,
                            scw_g=scw_g, scw_u=scw_u, scw_d=scw_d)
            barrier()
        dbg_dump("stgf", st_gf)
        P.emit()
        if upto <= 4:
            return nc

        with ExitStack() as ph5:
            mem_bf = sbuf("mem_bf", [128, H * 512], BF16, Gs)
            s_runb = sbuf("s_runb", [128, H * 129], BF16, Gs)
            w_oT = sbuf("w_oT", [128, DB * D], BF16, Gs)
            W_b = sbuf("W_b", [128, H * M], F32, Gs)
            G_sb = sbuf("G_sb", [128, H], F32, Gs)
            Gc_sb = sbuf("Gc_sb", [128, H], F32, Gs)
            Geff = sbuf("Geff", [128, H], F32, Gs)

            sgath = [sbuf(f"sgath{i}", [128, 8 * 129], F32, ph5) for i in range(2)]
            relg = sbuf("relg", [1, 8 * H * M], F32, ph5)
            rel_all = sbuf("rel_all", [1, H * M], F32, ph5)
            rel_t = sbuf("rel_t", [H, M], F32, ph5)
            mnr_sb = sbuf("mnr_sb", [M * H, 128], F32, ph5)
            mnr1 = sbuf("mnr1", [M * H, 1], F32, ph5)
            msel_sb = sbuf("msel_sb", [M * H, M], F32, ph5)
            e_sb = sbuf("e_sb", [1, M], F32, ph5)
            esum = sbuf("esum", [1, 1], F32, ph5)
            allf = sbuf("allf", [1, 1], F32, ph5)
            notall = sbuf("notall", [1, 1], F32, ph5)
            e_b = sbuf("e_b", [H, M], F32, ph5)
            one_m = sbuf("one_m", [H, M], F32, ph5)
            relm = sbuf("relm", [H, M], F32, ph5)
            rmax = sbuf("rmax", [H, 1], F32, ph5)
            negmax = sbuf("negmax", [H, 1], F32, ph5)
            wex = sbuf("wex", [H, M], F32, ph5)
            ssum = sbuf("ssum", [H, 1], F32, ph5)
            srq = sbuf("srq", [H, 1], F32, ph5)
            W_sb = sbuf("W_sb", [H, M], F32, ph5)
            gate_sb = sbuf("gate_sb", [1, H], F32, ph5)
            gsig = sbuf("gsig", [1, H], F32, ph5)
            gsig2 = sbuf("gsig2", [1, H], F32, ph5)
            maskcol = sbuf("maskcol", [128, 8], F32, ph5)

            P.wait("gpsimd", sCC, 1)

            def gld(sem, dst, src, **kw):
                P.op("gpsimd", lambda g, d=dst, s=src, sem=sem, kw=kw:
                     g.dma_start(out=d, in_=s, **kw).then_inc(sem.h, 16))
                return sem.inc(16)
            ccv = cc_out.rearrange("(r q) f -> r q f", q=CCR)
            ld_relg = gld(srelg, relg.ap().rearrange("a (r f) -> a r f", r=8),
                          ccv[:, H * 128:H * 128 + 1, 0:H * M].rearrange("r a f -> a r f"))
            ld_mnr = gld(smnr, mnr_sb[:], ap["mnrows"])
            ld_gate = gld(sgt2, gate_sb[:], ap["gate"])
            ld_msel = gld(smsl, msel_sb[:], ap["msel"])
            ld_sg = [0] * H
            pe_mkc = sPE.inc()
            pe_emp = sPE.inc()
            pe_eb = sPE.inc()
            pe_wb = sPE.inc()
            pe_gb = sPE.inc()
            pe_geff = sPE.inc()
            ac_wex = sAC.inc()
            ac_gsig = sAC.inc()
            dv_rel = sDV.inc()
            dv_mnred = sDV.inc()
            dv_mkc = sDV.inc()
            dv_e = sDV.inc()
            dv_esum = sDV.inc()
            dv_notall = sDV.inc()
            dv_sinit = [0] * H
            for h in range(H):
                dv_sinit[h] = sDV.inc()
            dv_srunb = sDV.inc()
            dv_ebm = sDV.inc()
            dv_relm = sDV.inc()
            dv_rmax = sDV.inc()
            dv_negmax = sDV.inc()
            dv_ssum = sDV.inc()
            dv_srq = sDV.inc()
            dv_wsb = sDV.inc()
            dv_wbev = sDV.inc()
            dv_gsig2 = sDV.inc()
            dv_gbev = sDV.inc()
            dv_geffev = sDV.inc()
            sy_relbuf = srel.inc(16)
            sy_wbuf = swbf.inc(16)

            for h in range(H):
                if h >= 2:
                    P.wait("gpsimd", sDV, dv_sinit[h - 2])
                ld_sg[h] = gld(
                    ssg[h % 2],
                    sgath[h % 2].ap().rearrange("p (r f) -> p r f", r=8),
                    ccv[:, h * 128:(h + 1) * 128, :].rearrange("r p f -> p r f"))
            ld_mem = [0] * H
            for h in range(H):
                ld_mem[h] = gld(
                    sGt,
                    mem_bf[:, h * 512:(h + 1) * 512].rearrange("p (m e) -> p m e", m=M),
                    ap["mem"][:, h].rearrange("m p e -> p m e"))
            ld_wo = [0] * DB
            for db in range(DB):
                ld_wo[db] = gld(sGt, w_oT[:, db * D:(db + 1) * D], ap["woT"][db])
            P.wait("gpsimd", srel, sy_relbuf)
            ld_relt = gld(srlt, rel_t[:], relbuf.rearrange("a (h m) -> (a h) m", h=H))
            P.wait("gpsimd", swbf, sy_wbuf)
            ld_wfl = gld(swfl, relg[:, 0:H * M], wbuf)

            P.op("tensor", lambda t: t.matmul(pB[:, 0:8], onesr[:], rmk[:],
                                              start=True, stop=True).then_inc(sPE.h, 1))
            P.wait("tensor", sDV, dv_mnred)
            P.wait("tensor", smsl, ld_msel)
            P.op("tensor", lambda t: t.matmul(pA[0:1, 0:4], mnr1[:], msel_sb[:],
                                              start=True, stop=True).then_inc(sPE.h, 1))
            P.wait("tensor", sDV, dv_e)
            P.op("tensor", lambda t: t.matmul(pA[0:H, 512:516], onesr[:, 0:H], e_sb[:],
                                              start=True, stop=True).then_inc(sPE.h, 1))
            P.wait("tensor", swfl, ld_wfl)
            P.op("tensor", lambda t: t.matmul(pB[:, 512:512 + H * M], onesr[:], relg[:, 0:H * M],
                                              start=True, stop=True).then_inc(sPE.h, 1))
            P.wait("tensor", sAC, ac_gsig)
            P.op("tensor", lambda t: t.matmul(pB[:, 1024:1024 + H], onesr[:], gsig[:],
                                              start=True, stop=True).then_inc(sPE.h, 1))
            P.wait("tensor", sDV, dv_gsig2)
            P.op("tensor", lambda t: t.matmul(pB[:, 1536:1536 + H], onesr[:], gsig2[:],
                                              start=True, stop=True).then_inc(sPE.h, 1))

            P.wait("vector", srelg, ld_relg)
            P.op("vector", lambda v: v.tensor_add(out=rel_all[:], in0=relg[:, 0:64],
                                                  in1=relg[:, 64:128]))
            P.op("vector", lambda v: v.drain())
            for r in range(2, 8):
                if r == 7:
                    P.op("vector", lambda v, r=r: _dinc(v, v.tensor_add(
                        out=rel_all[:], in0=rel_all[:],
                        in1=relg[:, r * 64:(r + 1) * 64]), sDV.h))
                else:
                    P.op("vector", lambda v, r=r: v.tensor_add(
                        out=rel_all[:], in0=rel_all[:], in1=relg[:, r * 64:(r + 1) * 64]))
                    P.op("vector", lambda v: v.drain())
            P.wait("vector", smnr, ld_mnr)
            P.op("vector", lambda v: _dinc(v, v.reduce_sum(
                out=mnr1[:], in_=mnr_sb[:], axis=AX.X), sDV.h))
            P.wait("vector", sPE, pe_mkc)
            P.op("vector", lambda v: _dinc(v, v.tensor_copy(out=maskcol[:], in_=pB[:, 0:8])
                 , sDV.h))
            P.wait("vector", sPE, pe_emp)
            P.op("vector", lambda v: _dinc(v, v.tensor_single_scalar(
                out=e_sb[:], in_=pA[0:1, 0:4], scalar=1e-6, op=ALU.is_lt), sDV.h))
            P.wait("vector", sDV, dv_e)
            P.op("vector", lambda v: _dinc(v, v.reduce_sum(out=esum[:], in_=e_sb[:], axis=AX.X)
                 , sDV.h))
            P.wait("vector", sDV, dv_esum)
            P.op("vector", lambda v: v.tensor_single_scalar(
                out=allf[:], in_=esum[:], scalar=3.5, op=ALU.is_ge))
            P.op("vector", lambda v: v.drain())
            P.op("vector", lambda v: _dinc(v, v.tensor_scalar(
                out=notall[:], in0=allf[:], scalar1=-1.0, scalar2=1.0,
                op0=ALU.mult, op1=ALU.add), sDV.h))
            for h in range(H):
                P.wait("vector", ssg[h % 2], ld_sg[h])
                P.wait("vector", sDV, dv_mkc)
                P.op("vector", lambda v, h=h: v.tensor_scalar_mul(
                    out=s_run[:, h * 129:(h + 1) * 129],
                    in0=sgath[h % 2][:, 0:129], scalar1=maskcol[:, 0:1]))
                for r in range(1, 8):
                    def _sin(v, h=h, r=r):
                        v.drain()
                        v.scalar_tensor_tensor(
                            out=s_run[:, h * 129:(h + 1) * 129],
                            in0=sgath[h % 2][:, r * 129:(r + 1) * 129],
                            scalar=maskcol[:, r:r + 1],
                            in1=s_run[:, h * 129:(h + 1) * 129],
                            op0=ALU.mult, op1=ALU.add)
                        if r == 7:
                            v.drain().then_inc(sDV.h, 1)
                    P.op("vector", _sin)
            P.op("vector", lambda v: _dinc(v, v.tensor_copy(out=s_runb[:], in_=s_run[:]), sDV.h))
            P.wait("vector", sPE, pe_eb)
            P.op("vector", lambda v: v.tensor_copy(out=e_b[:], in_=pA[0:H, 512:516]))
            P.op("vector", lambda v: v.drain())
            P.op("vector", lambda v: _dinc(v, v.tensor_scalar(
                out=one_m[:], in0=e_b[:], scalar1=-1.0, scalar2=1.0,
                op0=ALU.mult, op1=ALU.add), sDV.h))
            P.wait("vector", srlt, ld_relt)
            P.op("vector", lambda v: v.tensor_mul(out=relm[:], in0=rel_t[:], in1=one_m[:]))
            P.op("vector", lambda v: v.drain())
            P.op("vector", lambda v: _dinc(v, v.scalar_tensor_tensor(
                out=relm[:], in0=e_b[:], scalar=-1e30, in1=relm[:],
                op0=ALU.mult, op1=ALU.add), sDV.h))
            P.wait("vector", sDV, dv_relm)
            P.op("vector", lambda v: _dinc(v, v.reduce_max(out=rmax[:], in_=relm[:], axis=AX.X)
                 , sDV.h))
            P.wait("vector", sDV, dv_rmax)
            P.op("vector", lambda v: _dinc(v, v.tensor_scalar_mul(out=negmax[:], in0=rmax[:], scalar1=-1.0)
                 , sDV.h))
            P.wait("vector", sAC, ac_wex)
            P.op("vector", lambda v: _dinc(v, v.reduce_sum(out=ssum[:], in_=wex[:], axis=AX.X)
                 , sDV.h))
            P.wait("vector", sDV, dv_ssum)
            P.op("vector", lambda v: _dinc(v, v.reciprocal(out=srq[:], in_=ssum[:]), sDV.h))
            P.wait("vector", sDV, dv_srq)
            P.op("vector", lambda v: _dinc(v, v.tensor_scalar_mul(out=W_sb[:], in0=wex[:], scalar1=srq[:])
                 , sDV.h))
            P.wait("vector", sPE, pe_wb)
            P.op("vector", lambda v: _dinc(v, v.tensor_copy(out=W_b[:], in_=pB[:, 512:512 + H * M])
                 , sDV.h))
            P.wait("vector", sAC, ac_gsig)
            P.wait("vector", sDV, dv_notall)
            P.op("vector", lambda v: _dinc(v, v.tensor_scalar_mul(
                out=gsig2[:], in0=gsig[:], scalar1=notall[0:1, 0:1]), sDV.h))
            P.wait("vector", sPE, pe_gb)
            P.op("vector", lambda v: v.tensor_copy(out=G_sb[:], in_=pB[:, 1024:1024 + H]))
            P.op("vector", lambda v: v.drain())
            P.op("vector", lambda v: _dinc(v, v.tensor_scalar(
                out=Gc_sb[:], in0=G_sb[:], scalar1=-1.0, scalar2=1.0,
                op0=ALU.mult, op1=ALU.add), sDV.h))
            P.wait("vector", sPE, pe_geff)
            P.op("vector", lambda v: _dinc(v, v.tensor_copy(out=Geff[:], in_=pB[:, 1536:1536 + H])
                 , sDV.h))

            P.wait("scalar", sDV, dv_negmax)
            P.op("scalar", lambda s: _dinc(s, s.activation(out=wex[:], in_=relm[:], func=AF.Exp,
                                                  bias=negmax[:]), sAC.h))
            P.wait("scalar", sgt2, ld_gate)
            P.op("scalar", lambda s: _dinc(s, s.activation(out=gsig[:], in_=gate_sb[:], func=AF.Sigmoid)
                 , sAC.h))

            P.wait("sync", sDV, dv_rel)
            P.op("sync", lambda e: e.dma_start(out=relbuf, in_=rel_all[:]).then_inc(srel.h, 16))
            P.wait("sync", sDV, dv_wsb)
            P.op("sync", lambda e: e.dma_start(out=wbuf, in_=W_sb[:]).then_inc(swbf.h, 16))
            P.emit()
            barrier()
        dbg_dump("sinit", s_run[:])
        dbg_dump("wb", W_b[:])
        dbg_dump("ccout", cc_out)
        if "gg" in dbg:
            P.op("gpsimd", lambda g: g.dma_start(
                out=dbg["gg"][:, 0:H], in_=G_sb[:]).then_inc(sGt.h, 16)); sGt.inc(16)
            P.op("gpsimd", lambda g: g.dma_start(
                out=dbg["gg"][:, H:2 * H], in_=Gc_sb[:]).then_inc(sGt.h, 16)); sGt.inc(16)
            P.op("gpsimd", lambda g: g.dma_start(
                out=dbg["gg"][:, 2 * H:3 * H], in_=Geff[:]).then_inc(sGt.h, 16)); sGt.inc(16)
        P.emit()
        if upto <= 5:
            return nc

        with ExitStack() as ph:
            x_c = [sbuf(f"x_c{i}", [128, D], F32, ph) for i in range(2)]
            qp_sl = [sbuf(f"qp_sl{i}", [128, H * 128], BF16, ph) for i in range(2)]
            in_sl = [sbuf(f"in_sl{i}", [128, H * 129], BF16, ph) for i in range(2)]
            sl_sl = [sbuf(f"sl_sl{i}", [128, H * 129], BF16, ph) for i in range(2)]
            gf_c = [sbuf(f"gf_c{i}", [128, D], BF16, ph) for i in range(2)]
            saug = [sbuf(f"saug{i}", [128, H * 129], BF16, ph) for i in range(2)]
            att_tm = [sbuf(f"att_tm{i}", [128, D], BF16, ph) for i in range(2)]
            afm = [sbuf(f"afm{i}", [128, D], BF16, ph) for i in range(2)]
            out_sb = [sbuf(f"out_sb{i}", [128, D], F32, ph) for i in range(2)]
            tsb = sbuf("tsb", [128, D], F32, ph)
            dent = sbuf("dent", [128, 1], F32, ph)
            denc = sbuf("denc", [128, 1], F32, ph)
            dreq = [sbuf(f"dreq{i}", [128, 1], F32, ph) for i in range(4)]
            numt = [sbuf(f"numt{i}", [128, 128], F32, ph) for i in range(4)]
            loct = [sbuf(f"loct{i}", [128, 128], F32, ph) for i in range(4)]
            nrmc = sbuf("nrmc", [128, M], F32, ph)
            nrq = sbuf("nrq", [128, M], F32, ph)
            sc4 = [sbuf(f"sc4_{i}", [128, M], F32, ph) for i in range(4)]
            qm = [sbuf(f"qm{i}", [128, 4 * 128], F32, ph) for i in range(4)]
            s01 = sbuf("s01", [128, 128], F32, ph)
            s23 = sbuf("s23", [128, 128], F32, ph)
            macc = sbuf("macc", [128, 128], F32, ph)
            t1 = sbuf("t1", [128, 128], F32, ph)
            acfl6 = sbuf("acfl6", [128, 1], F32, ph)

            ldx = [0] * C; ldqp = [0] * C; ldin = [0] * C; ldsl = [0] * C; ldgf = [0] * C
            for c in range(C):
                ldx[c] = sx[c % 2].inc(16)
                ldqp[c] = sqp[c % 2].inc(16)
                ldin[c] = sin_[c % 2].inc(16)
                ldsl[c] = ssl[c % 2].inc(16)
                ldgf[c] = sgfc[c % 2].inc(16)
            pe_h = [[0] * H for _ in range(C)]
            wo_done = [[0, 0] for _ in range(C)]
            for c in range(C):
                for h in range(H):
                    pe_h[c][h] = sPE.inc()
                wo_done[c][0] = sPE.inc()
                wo_done[c][1] = sPE.inc()
            saug_dv = [0] * C
            e1_dv = [[0] * H for _ in range(C)]
            e2_dv = [[0] * H for _ in range(C)]
            e3_dv = [[0] * H for _ in range(C)]
            e4_dv = [[0] * H for _ in range(C)]
            att_dv = [[0] * H for _ in range(C)]
            a1_ac = [[0] * H for _ in range(C)]
            fin_dv = [[0, 0] for _ in range(C)]
            for c in range(C):
                saug_dv[c] = sDV.inc()
                for h in range(H):
                    e1_dv[c][h] = sDV.inc()
                    e2_dv[c][h] = sDV.inc()
                    e3_dv[c][h] = sDV.inc()
                    e4_dv[c][h] = sDV.inc()
                    att_dv[c][h] = sDV.inc()
                fin_dv[c][0] = sDV.inc()
                fin_dv[c][1] = sDV.inc()
            for c in range(C):
                for h in range(H):
                    a1_ac[c][h] = sAC.inc()
            att_tp = [satp[c % 2].inc(16 * H) for c in range(C)]
            out_wr = [sst.inc(16) for _ in range(C)]

            def _pslot(h):
                s = h % 4
                return (s % 2) * 512 + (s // 2) * 256

            stqp_v = st_qp.rearrange("h p t -> p h t")
            stin_v = st_intra.rearrange("h p t -> p h t")
            stsl_v = st_sloc.rearrange("h p t -> p h t")
            for c in range(C):
                if c >= 2:
                    P.wait("gpsimd", sDV, fin_dv[c - 2][1])
                    P.wait("gpsimd", sPE, pe_h[c - 2][H - 1])
                P.op("gpsimd", lambda g, c=c: g.dma_start(
                    out=x_c[c % 2][:], in_=ap["x"][c * 128:(c + 1) * 128, :]).then_inc(sx[c % 2].h, 16))
                P.op("gpsimd", lambda g, c=c: g.dma_start(
                    out=qp_sl[c % 2].ap().rearrange("p (h t) -> p h t", h=H),
                    in_=stqp_v[:, :, c * 128:(c + 1) * 128]).then_inc(sqp[c % 2].h, 16))
                P.op("gpsimd", lambda g, c=c: g.dma_start(
                    out=in_sl[c % 2].ap().rearrange("p (h t) -> p h t", h=H),
                    in_=stin_v[:, :, c * 129:(c + 1) * 129]).then_inc(sin_[c % 2].h, 16))
                P.op("gpsimd", lambda g, c=c: g.dma_start(
                    out=sl_sl[c % 2].ap().rearrange("p (h t) -> p h t", h=H),
                    in_=stsl_v[:, :, c * 129:(c + 1) * 129]).then_inc(ssl[c % 2].h, 16))
                P.op("gpsimd", lambda g, c=c: g.dma_start(
                    out=gf_c[c % 2][:], in_=st_gf[c]).then_inc(sgfc[c % 2].h, 16))
            P.wait("gpsimd", sst, out_wr[C - 1])

            flat = []
            for c in range(C):
                for h in range(H):
                    flat.append((c, h))
            for c in range(C):
                P.wait("vector", ssl[c % 2], ldsl[c])
                if c >= 2:
                    P.wait("vector", sPE, pe_h[c - 2][H - 1])
                P.op("vector", lambda v, c=c: _dinc(v, v.tensor_add(
                    out=saug[c % 2][:], in0=s_runb[:], in1=sl_sl[c % 2][:]), sDV.h))
                P.wait("vector", sin_[c % 2], ldin[c])
                for h in range(H):
                    pS = _pslot(h)
                    pM = (h % 4) * 512
                    gi = c * H + h
                    P.wait("vector", sPE, pe_h[c][h])
                    if gi >= 4:
                        c4, h4 = flat[gi - 4]
                        P.wait("vector", sAC, a1_ac[c4][h4])
                    P.op("vector", lambda v, c=c, h=h, pS=pS: v.tensor_add(
                        out=dent[:], in0=pB[:, pS + 128:pS + 129],
                        in1=in_sl[c % 2][:, h * 129 + 128:h * 129 + 129]))
                    P.op("vector", lambda v, c=c, h=h, pS=pS: v.tensor_add(
                        out=numt[h % 4][:], in0=pB[:, pS:pS + 128],
                        in1=in_sl[c % 2][:, h * 129:h * 129 + 128]))
                    P.op("vector", lambda v, pS=pS: _dinc(v, v.tensor_scalar_max(
                        out=nrmc[:], in0=pB[:, pS + 160:pS + 160 + M], scalar1=1e-6), sDV.h))
                    P.wait("vector", sDV, e1_dv[c][h])
                    P.op("vector", lambda v: v.tensor_scalar_max(
                        out=denc[:], in0=dent[:], scalar1=1e-6))
                    P.op("vector", lambda v: _dinc(v, v.reciprocal(
                        out=nrq[:], in_=nrmc[:]), sDV.h))
                    P.wait("vector", sDV, e2_dv[c][h])
                    P.op("vector", lambda v, h=h: v.reciprocal(out=dreq[h % 4][:], in_=denc[:]))
                    P.op("vector", lambda v, h=h: _dinc(v, v.tensor_mul(
                        out=sc4[h % 4][:], in0=nrq[:], in1=W_b[:, h * M:(h + 1) * M]), sDV.h))
                    P.wait("vector", sAC, a1_ac[c][h])
                    P.op("vector", lambda v, h=h: v.tensor_add(
                        out=s01[:], in0=qm[h % 4][:, 0:128], in1=qm[h % 4][:, 128:256]))
                    P.op("vector", lambda v, h=h: v.tensor_add(
                        out=s23[:], in0=qm[h % 4][:, 256:384], in1=qm[h % 4][:, 384:512]))
                    P.op("vector", lambda v, h=h: _dinc(v, v.tensor_scalar_mul(
                        out=t1[:], in0=loct[h % 4][:], scalar1=Gc_sb[:, h:h + 1]), sDV.h))
                    P.wait("vector", sDV, e4_dv[c][h])
                    P.op("vector", lambda v: v.tensor_add(out=macc[:], in0=s01[:], in1=s23[:]))
                    P.op("vector", lambda v: v.drain())
                    P.op("vector", lambda v, c=c, h=h: _dinc(v, v.scalar_tensor_tensor(
                        out=att_tm[c % 2][:, h * 128:(h + 1) * 128], in0=macc[:],
                        scalar=Geff[:, h:h + 1], in1=t1[:],
                        op0=ALU.mult, op1=ALU.add), sDV.h))
                P.wait("vector", sgfc[c % 2], ldgf[c])
                P.wait("vector", sx[c % 2], ldx[c])
                for half in range(2):
                    P.wait("vector", sPE, wo_done[c][half])
                    P.op("vector", lambda v, c=c, half=half: v.tensor_tensor(
                        out=tsb[:, 0:1024], in0=pB[:, 1024:2048],
                        in1=gf_c[c % 2][:, half * 1024:(half + 1) * 1024], op=ALU.mult))
                    P.op("vector", lambda v: v.drain())
                    P.op("vector", lambda v, c=c, half=half: _dinc(v, v.tensor_add(
                        out=out_sb[c % 2][:, half * 1024:(half + 1) * 1024],
                        in0=tsb[:, 0:1024],
                        in1=x_c[c % 2][:, half * 1024:(half + 1) * 1024]), sDV.h))

            for c in range(C):
                P.wait("tensor", sDV, saug_dv[c])
                P.wait("tensor", sqp[c % 2], ldqp[c])
                for h in range(H):
                    pS = _pslot(h)
                    pM = (h % 4) * 512
                    qp_t = qp_sl[c % 2][:, h * 128:(h + 1) * 128]
                    gi = c * H + h
                    if gi >= 2:
                        c2, h2 = flat[gi - 2]
                        P.wait("tensor", sDV, e1_dv[c2][h2])
                    if gi >= 4:
                        c4, h4 = flat[gi - 4]
                        P.wait("tensor", sAC, a1_ac[c4][h4])
                    P.op("tensor", lambda t, o=pB[:, pS:pS + 129], a=qp_t,
                         b=saug[c % 2][:, h * 129:(h + 1) * 129]:
                         t.matmul(o, a, b, start=True, stop=True))
                    P.op("tensor", lambda t, o=pA[:, pM:pM + 512], a=qp_t,
                         b=mem_bf[:, h * 512:(h + 1) * 512]:
                         t.matmul(o, a, b, start=True, stop=True))
                    P.op("tensor", lambda t, o=pB[:, pS + 160:pS + 160 + M], a=qp_t,
                         b=mnT_b[:, h * M:(h + 1) * M]:
                         t.matmul(o, a, b, start=True, stop=True).then_inc(sPE.h, 1))
                P.wait("tensor", satp[c % 2], att_tp[c])
                for half in range(2):
                    if c >= 1 or half >= 1:
                        P.wait("tensor", sDV, fin_dv[c - 1][1] if half == 0 else fin_dv[c][0])
                    for db in range(DB):
                        lhs = afm[c % 2][:, db * 128:(db + 1) * 128]
                        for sp in range(2):
                            o = pB[:, 1024 + sp * 512:1024 + (sp + 1) * 512]
                            rhs = w_oT[:, db * D + half * 1024 + sp * 512:
                                       db * D + half * 1024 + (sp + 1) * 512]
                            if db == DB - 1 and sp == 1:
                                P.op("tensor", lambda t, o=o, a=lhs, b=rhs:
                                     t.matmul(o, a, b, start=False, stop=True).then_inc(sPE.h, 1))
                            else:
                                P.op("tensor", lambda t, o=o, a=lhs, b=rhs, st=(db == 0),
                                     fi=(db == DB - 1): t.matmul(o, a, b, start=st, stop=fi))

            for c in range(C):
                for h in range(H):
                    pM = (h % 4) * 512
                    gi = c * H + h
                    P.wait("scalar", sDV, e3_dv[c][h])
                    if gi >= 4:
                        c4, h4 = flat[gi - 4]
                        P.wait("scalar", sDV, e4_dv[c4][h4])
                    P.op("scalar", lambda s, h=h: s.activation(
                        out=loct[h % 4][:], in_=numt[h % 4][:], func=AF.Copy,
                        scale=dreq[h % 4][:]))
                    for m in range(M):
                        P.op("scalar", lambda s, h=h, m=m, pM=pM: s.activation(
                            out=qm[h % 4][:, m * 128:(m + 1) * 128],
                            in_=pA[:, pM + m * 128:pM + (m + 1) * 128], func=AF.Copy,
                            scale=sc4[h % 4][:, m:m + 1]))
                    P.op("scalar", lambda s: _dinc(s, s.copy(
                        out=acfl6[:], in_=epsc[:]), sAC.h))
                    P.wait("scalar", sDV, att_dv[c][h])
                    if c >= 2 and h == 0:
                        P.wait("scalar", sPE, wo_done[c - 2][1])
                    P.op("scalar", lambda e, c=c, h=h: e.dma_start_transpose(
                        afm[c % 2][:, h * 128:(h + 1) * 128],
                        att_tm[c % 2][:, h * 128:(h + 1) * 128]).then_inc(satp[c % 2].h, 16))

            for c in range(C):
                P.wait("sync", sDV, fin_dv[c][1])
                P.op("sync", lambda e, c=c: e.dma_start(
                    out=out[c * 128:(c + 1) * 128, :], in_=out_sb[c % 2][:]).then_inc(sst.h, 16))
            if "att0" in dbg:
                P.wait("sync", satp[0], att_tp[0])
                P.op("sync", lambda e: e.dma_start(out=dbg["att0"], in_=afm[0][:])
                     .then_inc(sst.h, 16))
                sst.inc(16)
            P.emit()
    return nc


def _mk_inputs(inputs):
    f = np.float32
    x = np.asarray(inputs["L0_attn_hidden_states"], f).reshape(T * NCORE, D)
    wq = np.asarray(inputs["L0_attn_w_q"], f)
    wk = np.asarray(inputs["L0_attn_w_k"], f)
    wv = np.asarray(inputs["L0_attn_w_v"], f)
    wo = np.asarray(inputs["L0_attn_w_o"], f)
    mem = np.asarray(inputs["L0_attn_memories"], f)
    mn = np.asarray(inputs["L0_attn_memory_norms"], f)
    gate = np.asarray(inputs["L0_attn_gate"], f)
    gw = np.asarray(inputs["L0_ffn_gate_w"], np.int32)
    gs = np.asarray(inputs["L0_ffn_gate_scale"], f)
    uw = np.asarray(inputs["L0_ffn_up_w"], np.int32)
    us = np.asarray(inputs["L0_ffn_up_scale"], f)
    dw = np.asarray(inputs["L0_ffn_down_w"], np.int32)
    ds = np.asarray(inputs["L0_ffn_down_scale"], f)

    com = {
        "wqT": np.ascontiguousarray(wq.T.reshape(DB, 128, H, 128).transpose(2, 0, 1, 3)),
        "wkT": np.ascontiguousarray(wk.T.reshape(DB, 128, H, 128).transpose(2, 0, 1, 3)),
        "wvT": np.ascontiguousarray(wv.T.reshape(DB, 128, D)),
        "woT": np.ascontiguousarray(wo.T.reshape(DB, 128, D)),
        "gw": np.ascontiguousarray(gw.T.reshape(HB, 128, IB, 128).transpose(2, 0, 1, 3)),
        "uw": np.ascontiguousarray(uw.T.reshape(HB, 128, IB, 128).transpose(2, 0, 1, 3)),
        "dw": np.ascontiguousarray(dw.T.reshape(IB, 128, HB, 128).transpose(2, 0, 1, 3)),
        "gsT": np.ascontiguousarray(gs.reshape(IB, 128).T),
        "usT": np.ascontiguousarray(us.reshape(IB, 128).T),
        "dsT": np.ascontiguousarray(ds.reshape(HB, 128).T),
        "mem": np.ascontiguousarray(mem),
        "mnTh": np.ascontiguousarray(mn.transpose(2, 1, 0).reshape(128, H * M)),
        "mnrows": np.ascontiguousarray(mn.reshape(M * H, 128)),
        "gate": np.ascontiguousarray(gate.reshape(1, H)),
        "tril": np.triu(np.ones((128, 128))).astype(bf16),
        "i_epsc": np.full((128, 1), 1e-5, f),
        "i_sent": np.arange(700, 708, dtype=f)[None, :] + np.zeros((128, 1), f),
        "i_onesb": np.ones((128, 1), bf16),
        "i_onesr": np.ones((1, 128), f),
        "msel": np.kron(np.eye(M, dtype=f), np.ones((H, 1), f)),
    }
    in_maps = []
    for r in range(NCORE):
        m = dict(com)
        m["x"] = np.ascontiguousarray(x[r * T:(r + 1) * T])
        rm = np.zeros((1, 8), f)
        rm[0, :r] = 1.0
        m["rmask"] = rm
        in_maps.append(m)
    return in_maps


def kernel(**inputs):
    in_maps = _mk_inputs(inputs)
    nc = build()
    res = run_bass_kernel_spmd(nc, in_maps, core_ids=list(range(NCORE)))
    outs = [np.asarray(res.results[r]["out"]) for r in range(NCORE)]
    return np.concatenate(outs, axis=0).reshape(1, T * NCORE, D).astype(np.float32)


# version 06
# speedup vs baseline: 1.0704x; 1.0704x over previous
import numpy as np
import ml_dtypes
import concourse.bass as bass
import concourse.mybir as mybir
from concourse.bass_utils import run_bass_kernel_spmd
from contextlib import ExitStack

F32 = mybir.dt.float32
BF16 = mybir.dt.bfloat16
I32 = mybir.dt.int32
AF = mybir.ActivationFunctionType
ALU = mybir.AluOpType
AX = mybir.AxisListType

NCORE = 8
T = 2048
C = 16
H = 16
HD = 128
D = 2048
DB = 16
II = 8192
IB = 64
HB = 16
M = 4
SCALE = HD ** -0.5
TTN = 512
NTT = T // TTN
NSP = TTN // 512
NBW = 4
NWB = IB // NBW
CCR = H * 128 + 16

bf16 = ml_dtypes.bfloat16
ENGS = ("tensor", "vector", "scalar", "gpsimd", "sync")

def _dinc(eng, ins, sem, k=1):
    eng.drain().then_inc(sem, k)
    return ins




class Ctr:
    def __init__(self, handle, name):
        self.h = handle
        self.v = 0
        self.name = name

    def inc(self, k=1):
        self.v += k
        return self.v


class Phase:
    def __init__(self, block):
        self.block = block
        self.s = {k: [] for k in ENGS}

    def op(self, eng, fn):
        self.s[eng].append(fn)

    def wait(self, eng, ctr, val):
        if val > 0:
            self.s[eng].append(lambda e, h=ctr.h, v=val: e.wait_ge(h, v))

    def emit(self):
        for name in ENGS:
            ops = self.s[name]
            if not ops:
                continue

            def sec(e, ops=ops):
                for f in ops:
                    f(e)
            getattr(self.block, name)(sec)
        self.s = {k: [] for k in ENGS}


def build(dbg_names=(), upto=6, no_cc=False):
    nc = bass.Bass("TRN2", target_bir_lowering=False, detect_race_conditions=False)
    ap = {}

    def din(name, shape, dt):
        ap[name] = nc.dram_tensor(name, shape, dt, kind="ExternalInput").ap()

    din("x", [T, D], F32)
    din("wqT", [H, DB, 128, 128], F32)
    din("wkT", [H, DB, 128, 128], F32)
    din("wvT", [DB, 128, D], F32)
    din("woT", [DB, 128, D], F32)
    din("gw", [IB, HB, 128, 128], I32)
    din("uw", [IB, HB, 128, 128], I32)
    din("dw", [HB, IB, 128, 128], I32)
    din("gsT", [128, IB], F32)
    din("usT", [128, IB], F32)
    din("dsT", [128, HB], F32)
    din("mem", [M, H, 128, 128], F32)
    din("mnTh", [128, H * M], F32)
    din("mnrows", [M * H, 128], F32)
    din("gate", [1, H], F32)
    din("tril", [128, 128], BF16)
    din("i_epsc", [128, 1], F32)
    din("i_sent", [128, 8], F32)
    din("i_onesb", [128, 1], BF16)
    din("i_onesr", [1, 128], F32)
    din("rmask", [1, 8], F32)
    din("msel", [M * H, M], F32)
    din("i_ident", [128, 128], BF16)
    out = nc.dram_tensor("out", [T, D], F32, kind="ExternalOutput").ap()

    st_xn = nc.dram_tensor("st_xn", [DB, 128, T], BF16, kind="Internal").ap()
    st_qp = nc.dram_tensor("st_qp", [H, 128, T], BF16, kind="Internal").ap()
    st_intra = nc.dram_tensor("st_intra", [H, 128, C * 129], BF16, kind="Internal").ap()
    st_sloc = nc.dram_tensor("st_sloc", [H, 128, C * 129], BF16, kind="Internal").ap()
    st_gf = nc.dram_tensor("st_gf", [C, 128, D], BF16, kind="Internal").ap()
    scr_gw = nc.dram_tensor("scr_gw", [IB, 128, HB * 128], BF16, kind="Internal").ap()
    scr_uw = nc.dram_tensor("scr_uw", [IB, 128, HB * 128], BF16, kind="Internal").ap()
    scr_dw = nc.dram_tensor("scr_dw", [HB, 128, IB * 128], BF16, kind="Internal").ap()
    cc_in = nc.dram_tensor("cc_in", [CCR, 129], F32, kind="Internal").ap()
    cc_out = nc.dram_tensor("cc_out", [NCORE * CCR, 129], F32, kind="Internal").ap()
    relbuf = nc.dram_tensor("relbuf", [1, H * M], F32, kind="Internal").ap()
    wbuf = nc.dram_tensor("wbuf", [H, M], F32, kind="Internal").ap()
    dbg = {}
    _dbg_shapes = {
        "stqp": ([H, 128, T], BF16), "stintra": ([H, 128, C * 129], BF16),
        "stsloc": ([H, 128, C * 129], BF16), "stgf": ([C, 128, D], BF16),
        "stxn": ([DB, 128, T], BF16), "srun": ([128, H * 129], F32),
        "relp": ([1, H * M], F32), "sinit": ([128, H * 129], F32),
        "wb": ([128, H * M], F32), "gg": ([128, 3 * H], F32),
        "att0": ([128, D], BF16), "ccout": ([NCORE * CCR, 129], F32),
        "xnfm": ([128, DB * T], BF16),
        "stats": ([128, 8 * C], F32),
        "hbuf0": ([128, IB * TTN], BF16),
        "xnsl0": ([128, HB * TTN], BF16),
    }
    for name in dbg_names:
        sh, dt = _dbg_shapes[name]
        dbg[name] = nc.dram_tensor("dbg_" + name, sh, dt, kind="ExternalOutput").ap()

    with ExitStack() as Gs:
        def sbuf(name, shape, dt, stack):
            return stack.enter_context(nc.sbuf_tensor(name, shape, dt))

        s_run = sbuf("s_run", [128, H * 129], F32, Gs)
        qsum = sbuf("qsum", [128, H], F32, Gs)
        mnT_f = sbuf("mnT_f", [128, M * H], F32, Gs)
        mnT_b = sbuf("mnT_b", [128, M * H], BF16, Gs)
        trilb = sbuf("trilb", [128, 128], BF16, Gs)
        identb = sbuf("identb", [128, 128], BF16, Gs)
        onesb = sbuf("onesb", [128, 1], BF16, Gs)
        onesr = sbuf("onesr", [1, 128], F32, Gs)
        rmk = sbuf("rmk", [1, 8], F32, Gs)
        epsc = sbuf("epsc", [128, 1], F32, Gs)
        dvflush = sbuf("dvflush", [128, 1], F32, Gs)
        attS = Gs.enter_context(ExitStack())
        xn_fm = sbuf("xn_fm", [128, DB * T], BF16, attS)
        v_tm = sbuf("v_tm", [128, C * D], BF16, attS)

        pA = Gs.enter_context(nc.psum_tensor("pA", [128, 2048], F32))
        pB = Gs.enter_context(nc.psum_tensor("pB", [128, 2048], F32))

        _sems = {}

        def S(name):
            if name not in _sems:
                _sems[name] = Ctr(Gs.enter_context(nc.semaphore(name)), name)
            return _sems[name]

        sPE, sDV, sAC, sCC, sB = S("sPE"), S("sDV"), S("sAC"), S("sCC"), S("sB")
        sGt, sst = S("sGt"), S("sst")
        sx = [S("sx0"), S("sx1")]
        stpS = [S("stpS0"), S("stpS1")]
        stpC = [S("stpC0"), S("stpC1")]
        swv = S("swv")
        swq = [S("swq0"), S("swq1")]
        swk = [S("swk0"), S("swk1")]
        skp = [S("skp0"), S("skp1")]
        sqst = [S("sqst0"), S("sqst1")]
        sist, ssst = S("sist"), S("ssst")
        sxn, ssc = S("sxn"), S("ssc")
        sgw = [S("sgw0"), S("sgw1")]
        suw = [S("suw0"), S("suw1")]
        sdw = [S("sdw0"), S("sdw1")]
        sscw = [S("sscw0"), S("sscw1")]
        sdww = [S("sdww0"), S("sdww1")]
        sgf = [S("sgf0"), S("sgf1")]
        satp = [S("satp0"), S("satp1")]
        sqp = [S("sqp0"), S("sqp1")]
        sin_ = [S("sin0"), S("sin1")]
        ssl = [S("ssl0"), S("ssl1")]
        sgfc = [S("sgfc0"), S("sgfc1")]
        srelg, smnr, sgt2, smsl = S("srelg"), S("smnr"), S("sgt2"), S("smsl")
        ssg = [S("ssg0"), S("ssg1")]
        srlt, swfl, srel, swbf = S("srlt"), S("swfl"), S("srel"), S("swbf")

        block = Gs.enter_context(nc.Block())
        P = Phase(block)

        def barrier():
            vals = [(s, s.v) for nm, s in sorted(_sems.items())
                    if nm not in ("sCC", "sB")]
            bv = sB.inc()
            for s, v in vals:
                if v:
                    P.wait("gpsimd", s, v)
            P.op("gpsimd", lambda g: g.engine_nop().then_inc(sB.h, 1))
            for eng in ("tensor", "vector", "scalar", "sync"):
                P.wait(eng, sB, bv)
            P.emit()

        def dbg_dump(name, src):
            if name in dbg:
                P.op("gpsimd", lambda g, o=dbg[name], i=src: g.dma_start(out=o, in_=i)
                     .then_inc(sGt.h, 16))
                sGt.inc(16)

        for dst, src in ((trilb[:], ap["tril"]), (identb[:], ap["i_ident"]),
                         (mnT_f[:], ap["mnTh"]),
                         (mnT_b[:], ap["mnTh"]), (rmk[:], ap["rmask"]),
                         (epsc[:], ap["i_epsc"]), (onesb[:], ap["i_onesb"]),
                         (onesr[:], ap["i_onesr"])):
            P.op("gpsimd", lambda g, d=dst, s=src: g.dma_start(out=d, in_=s).then_inc(sGt.h, 16))
            sGt.inc(16)

        def _init(v):
            v.memset(s_run[:], 0.0)
            v.memset(qsum[:], 0.0)
            v.tensor_copy(out=dvflush[:], in_=s_run[:, 0:1])
            v.drain().then_inc(sDV.h, 1)
        P.op("vector", _init)
        sDV.inc()
        P.emit()
        barrier()

        with ExitStack() as ph:
            x_sb = [sbuf(f"x_sb{i}", [128, D], F32, ph) for i in range(2)]
            xn_tm = [sbuf(f"xn_tm{i}", [128, D], BF16, ph) for i in range(2)]
            scrap = sbuf("scrap", [128, D], BF16, ph)
            xsumt = sbuf("xsumt", [128, 1], F32, ph)
            xsqt = sbuf("xsqt", [128, 1], F32, ph)
            meant = sbuf("meant", [128, 1], F32, ph)
            m2t = sbuf("m2t", [128, 1], F32, ph)
            vart = sbuf("vart", [128, 1], F32, ph)
            stdt = sbuf("stdt", [128, 1], F32, ph)
            rstdt = sbuf("rstdt", [128, 1], F32, ph)
            negmr = sbuf("negmr", [128, 1], F32, ph)
            flushc = sbuf("flushc", [128, 1], F32, ph)
            xcen = sbuf("xcen", [128, D], F32, ph)
            acflush = sbuf("acflush", [128, 1], F32, ph)
            statlog = sbuf("statlog", [128, 8 * C], F32, ph)

            if "stats" in dbg:
                for ii, tl in enumerate((xsumt, xsqt, meant, m2t, vart, stdt, rstdt, flushc)):
                    P.op("gpsimd", lambda g, ii=ii, tl=tl: g.dma_start(
                        out=tl[:], in_=ap["i_sent"][:, ii:ii + 1],
                        allow_slow_non_contiguous=True).then_inc(sGt.h, 16))
                    sGt.inc(16)
                sent_ld = sGt.v
                P.wait("vector", sGt, sent_ld)
                P.wait("scalar", sGt, sent_ld)
            xload = [sx[c % 2].inc(16) for c in range(C)]
            ln_red = [0] * C; ln_var = [0] * C; ln_rcp = [0] * C; ln_xn = [0] * C
            for c in range(C):
                ln_red[c] = sDV.inc(); ln_var[c] = sDV.inc()
                ln_rcp[c] = sDV.inc(); ln_xn[c] = sDV.inc()
            ln_sq = [0] * C; ln_sqrt = [0] * C
            xn_tpS = [0] * C; xn_tpC = [0] * C
            for c in range(C):
                ln_sq[c] = sAC.inc(); ln_sqrt[c] = sAC.inc()
                xn_tpC[c] = stpC[c % 2].inc(16 * 8)
            for c in range(C):
                xn_tpS[c] = stpS[c % 2].inc(16 * 8)
            xn_stash = [sst.inc(16) for _ in range(DB)]

            for c in range(C):
                if c >= 2:
                    P.wait("gpsimd", sDV, ln_xn[c - 2])
                P.op("gpsimd", lambda g, c=c: g.dma_start(
                    out=x_sb[c % 2][:], in_=ap["x"][c * 128:(c + 1) * 128, :]).then_inc(sx[c % 2].h, 16))

            for c in range(C):
                if c >= 1:
                    P.wait("vector", sAC, ln_sqrt[c - 1])
                P.wait("vector", sx[c % 2], xload[c])
                P.op("vector", lambda v, c=c: _dinc(v, v.reduce_sum(
                    out=xsumt[:], in_=x_sb[c % 2][:], axis=AX.X), sDV.h))
                P.wait("vector", sDV, ln_red[c])
                P.wait("vector", sAC, ln_sq[c])
                P.op("vector", lambda v: v.tensor_scalar_mul(out=meant[:], in0=xsumt[:], scalar1=1.0 / D))
                P.op("vector", lambda v: v.drain())
                P.op("vector", lambda v: v.tensor_mul(out=m2t[:], in0=meant[:], in1=meant[:]))
                P.op("vector", lambda v: v.drain())
                P.op("vector", lambda v: v.scalar_tensor_tensor(
                    out=vart[:], in0=xsqt[:], scalar=1.0 / D, in1=m2t[:],
                    op0=ALU.mult, op1=ALU.subtract))
                P.op("vector", lambda v: _dinc(v, v.tensor_copy(out=dvflush[:], in_=vart[:])
                     , sDV.h))
                P.wait("vector", sAC, ln_sqrt[c])
                P.op("vector", lambda v: _dinc(v, v.reciprocal(out=rstdt[:], in_=stdt[:])
                     , sDV.h))
                P.wait("vector", sDV, ln_rcp[c])
                if c >= 2:
                    P.wait("vector", stpS[c % 2], xn_tpS[c - 2])
                    P.wait("vector", stpC[c % 2], xn_tpC[c - 2])
                P.op("vector", lambda v, c=c: v.tensor_scalar(
                    out=xcen[:], in0=x_sb[c % 2][:], scalar1=meant[:],
                    scalar2=None, op0=ALU.subtract))
                P.op("vector", lambda v: v.drain())
                P.op("vector", lambda v, c=c: v.tensor_scalar(
                    out=xn_tm[c % 2][:], in0=xcen[:], scalar1=rstdt[:],
                    scalar2=None, op0=ALU.mult))
                P.op("vector", lambda v: v.drain())
                if "stats" in dbg:
                    for ii, tl in enumerate((xsumt, xsqt, meant, m2t, vart, stdt, rstdt)):
                        P.op("vector", lambda v, c=c, ii=ii, tl=tl: v.tensor_copy(
                            out=statlog[:, c * 8 + ii:c * 8 + ii + 1], in_=tl[:]))
                P.op("vector", lambda v, c=c: _dinc(v, v.reduce_sum(
                    out=flushc[:], in_=xn_tm[c % 2][:], axis=AX.X), sDV.h))

            for c in range(C):
                P.wait("scalar", sx[c % 2], xload[c])
                if c >= 1:
                    P.wait("scalar", sDV, ln_var[c - 1])
                P.op("scalar", lambda s, c=c: s.activation(
                    out=scrap[:], in_=x_sb[c % 2][:], func=AF.Square,
                    accum_out=xsqt[:]))
                P.op("scalar", lambda s: _dinc(s, s.copy(out=acflush[:], in_=xsqt[:]),
                                               sAC.h))
                P.wait("scalar", sDV, ln_var[c])
                if c >= 1:
                    P.wait("scalar", sDV, ln_rcp[c - 1])
                P.op("scalar", lambda s: s.activation(
                    out=stdt[:], in_=vart[:], func=AF.Sqrt, bias=epsc[:]))
                P.op("scalar", lambda s: _dinc(s, s.copy(out=acflush[:], in_=stdt[:]),
                                               sAC.h))


            for c in range(C):
                P.wait("sync", sDV, ln_xn[c])
                for db in range(8):
                    P.op("sync", lambda e, c=c, db=db: e.dma_start_transpose(
                        xn_fm[:, db * T + c * 128: db * T + (c + 1) * 128],
                        xn_tm[c % 2][:, db * 128:(db + 1) * 128]).then_inc(stpS[c % 2].h, 16))
                for db in range(8, 16):
                    P.op("sync", lambda e, c=c, db=db: e.dma_start_transpose(
                        xn_fm[:, db * T + c * 128: db * T + (c + 1) * 128],
                        xn_tm[c % 2][:, db * 128:(db + 1) * 128]).then_inc(stpC[c % 2].h, 16))
            P.wait("sync", stpC[0], xn_tpC[C - 2])
            P.wait("sync", stpC[1], xn_tpC[C - 1])
            P.wait("sync", stpS[0], xn_tpS[C - 2])
            P.wait("sync", stpS[1], xn_tpS[C - 1])
            for db in range(DB):
                P.op("sync", lambda e, db=db: e.dma_start(
                    out=st_xn[db], in_=xn_fm[:, db * T:(db + 1) * T]).then_inc(sst.h, 16))
            P.emit()
            barrier()
        dbg_dump("stxn", st_xn)
        if "stats" in dbg:
            P.op("sync", lambda e: e.dma_start(out=dbg["stats"], in_=statlog[:])
                 .then_inc(sst.h, 16))
            sst.inc(16)
        if "xnfm" in dbg:
            P.op("sync", lambda e: e.dma_start(out=dbg["xnfm"], in_=xn_fm[:])
                 .then_inc(sst.h, 16))
            sst.inc(16)
            barrier()
        P.emit()
        if upto <= 1:
            return nc

        with ExitStack() as ph:
            wv_b = sbuf("wv_b", [128, DB * 1024], BF16, ph)
            wvl = [[swv.inc(16) for _ in range(DB)] for _ in range(2)]
            pe_v = [sPE.inc() for _ in range(2 * C)]
            vev = [sDV.inc() for _ in range(2 * C)]

            P.wait("gpsimd", sDV, ln_xn[C - 1])
            P.wait("gpsimd", sAC, sAC.v)
            for hf in range(2):
                for db in range(DB):
                    if hf == 1 and db == 0:
                        P.wait("gpsimd", sPE, pe_v[C - 1])
                    P.op("gpsimd", lambda g, hf=hf, db=db: g.dma_start(
                        out=wv_b[:, db * 1024:(db + 1) * 1024],
                        in_=ap["wvT"][db][:, hf * 1024:(hf + 1) * 1024]).then_inc(swv.h, 16))

            for hf in range(2):
                for tc in range(C):
                    i = hf * C + tc
                    P.wait("tensor", swv, wvl[hf][DB - 1])
                    if i >= 2:
                        P.wait("tensor", sDV, vev[i - 2])
                    base = (tc % 2) * 1024
                    for db in range(DB):
                        lhs = xn_fm[:, db * T + tc * 128: db * T + (tc + 1) * 128]
                        for sp in range(2):
                            o = pA[:, base + sp * 512: base + (sp + 1) * 512]
                            rhs = wv_b[:, db * 1024 + sp * 512: db * 1024 + (sp + 1) * 512]
                            if db == DB - 1 and sp == 1:
                                P.op("tensor", lambda t, o=o, a=lhs, b=rhs:
                                     t.matmul(o, a, b, start=False, stop=True).then_inc(sPE.h, 1))
                            else:
                                P.op("tensor", lambda t, o=o, a=lhs, b=rhs, st=(db == 0),
                                     fi=(db == DB - 1): t.matmul(o, a, b, start=st, stop=fi))

            for hf in range(2):
                for tc in range(C):
                    i = hf * C + tc
                    P.wait("vector", sPE, pe_v[i])
                    base = (tc % 2) * 1024
                    P.op("vector", lambda v, tc=tc, hf=hf, base=base: _dinc(v, v.tensor_copy(
                        out=v_tm[:, tc * D + hf * 1024: tc * D + (hf + 1) * 1024],
                        in_=pA[:, base: base + 1024]), sDV.h))
            P.emit()
            v_done_dv = sDV.v
        if upto <= 2:
            barrier()
            return nc

        with ExitStack() as ph:
            wq_b = [sbuf(f"wq_b{i}", [128, D], BF16, ph) for i in range(2)]
            wk_b = [sbuf(f"wk_b{i}", [128, D], BF16, ph) for i in range(2)]
            q_sb = sbuf("q_sb", [128, T], BF16, ph)
            k_sb = sbuf("k_sb", [128, T], BF16, ph)
            qp_h = [sbuf(f"qp_h{i}", [128, T], BF16, ph) for i in range(2)]
            kp_fm = [sbuf(f"kp_fm{i}", [128, T], BF16, ph) for i in range(2)]
            kp_tm = [sbuf(f"kp_tm{i}", [128, T], BF16, ph) for i in range(2)]
            tmin = sbuf("tmin", [128, T], BF16, ph)
            texp = sbuf("texp", [128, T], BF16, ph)
            scm = [sbuf(f"scm{i}", [128, 128], BF16, ph) for i in range(2)]
            intra_stg = [sbuf("instg", [128, C * 129], BF16, ph)] * 2
            sloc_stg = [sbuf("slstg", [128, C * 129], BF16, ph)] * 2
            relp = sbuf("relp", [16, 129], F32, ph)

            wq_l = [0] * H; wk_l = [0] * H
            q_done = [0] * H; k_done = [0] * H
            sc_done = [[0] * C for _ in range(H)]
            kv_done = [[0] * C for _ in range(H)]
            num_done = [[0] * C for _ in range(H)]
            for h in range(H):
                wq_l[h] = swq[h % 2].inc(16)
                wk_l[h] = swk[h % 2].inc(16)
            for h in range(H):
                q_done[h] = sPE.inc()
                k_done[h] = sPE.inc()
                for c in range(C):
                    sc_done[h][c] = sPE.inc()
                    kv_done[h][c] = sPE.inc()
                    num_done[h][c] = sPE.inc()
            rel_done = sPE.inc()
            q_ev = [0] * H; texp_qv = [0] * H; k_ev = [0] * H; texp_kv = [0] * H
            for h in range(H):
                q_ev[h] = sAC.inc()
                texp_qv[h] = sAC.inc()
                k_ev[h] = sAC.inc()
                texp_kv[h] = sAC.inc()
            qp_dv = [0] * H; kp_dv = [0] * H
            mask_dv = [[0] * C for _ in range(H)]
            sadd_dv = [[0] * C for _ in range(H)]
            nev_dv = [[0] * C for _ in range(H)]
            for h in range(H):
                sDV.inc()
                qp_dv[h] = sDV.inc()
                sDV.inc()
                kp_dv[h] = sDV.inc()
                for c in range(C):
                    mask_dv[h][c] = sDV.inc()
                    sadd_dv[h][c] = sDV.inc()
                    nev_dv[h][c] = sDV.inc()
            relp_dv = sDV.inc()
            kp_tp = [skp[h % 2].inc(16 * C) for h in range(H)]
            cc_wr = sst.inc(16 * (H + 1))
            qp_st = [0] * H; intra_st = [0] * H; sloc_st = [0] * H
            for h in range(H):
                qp_st[h] = sqst[h % 2].inc(16)
                intra_st[h] = sist.inc(16)
                sloc_st[h] = ssst.inc(16)
            cc_done = sCC.inc()

            P.wait("gpsimd", sPE, pe_v[2 * C - 1])
            P.wait("gpsimd", stpS[1], xn_tpS[C - 1])
            P.wait("gpsimd", stpC[1], xn_tpC[C - 1])
            P.wait("gpsimd", sst, xn_stash[DB - 1])
            for h in range(H):
                if h >= 2:
                    P.wait("gpsimd", sPE, q_done[h - 2])
                P.op("gpsimd", lambda g, h=h: g.dma_start(
                    out=wq_b[h % 2].ap().rearrange("p (a b) -> p a b", a=DB),
                    in_=ap["wqT"][h].rearrange("a p b -> p a b")).then_inc(swq[h % 2].h, 16))
                if h >= 2:
                    P.wait("gpsimd", sPE, k_done[h - 2])
                P.op("gpsimd", lambda g, h=h: g.dma_start(
                    out=wk_b[h % 2].ap().rearrange("p (a b) -> p a b", a=DB),
                    in_=ap["wkT"][h].rearrange("a p b -> p a b")).then_inc(swk[h % 2].h, 16))
            P.wait("gpsimd", sst, cc_wr)
            if no_cc:
                P.op("gpsimd", lambda g: g.engine_nop().then_inc(sCC.h))
            else:
                P.op("gpsimd", lambda g: g.collective_compute(
                    "AllGather", ALU.bypass, replica_groups=[list(range(NCORE))],
                    ins=[cc_in.opt()], outs=[cc_out.opt()]).then_inc(sCC.h))

            for h in range(H):
                P.wait("tensor", swq[h % 2], wq_l[h])
                if h == 0:
                    P.wait("tensor", sDV, v_done_dv)
                if h >= 1:
                    P.wait("tensor", sDV, nev_dv[h - 1][C - 1])
                    P.wait("tensor", sDV, mask_dv[h - 1][C - 1])
                for db in range(DB):
                    lhs = wq_b[h % 2][:, db * 128:(db + 1) * 128]
                    for sp in range(4):
                        o = pA[:, sp * 512:(sp + 1) * 512]
                        rhs = xn_fm[:, db * T + sp * 512: db * T + (sp + 1) * 512]
                        if db == DB - 1 and sp == 3:
                            P.op("tensor", lambda t, o=o, a=lhs, b=rhs:
                                 t.matmul(o, a, b, start=False, stop=True).then_inc(sPE.h, 1))
                        else:
                            P.op("tensor", lambda t, o=o, a=lhs, b=rhs, st=(db == 0),
                                 fi=(db == DB - 1): t.matmul(o, a, b, start=st, stop=fi))
                P.wait("tensor", swk[h % 2], wk_l[h])
                if h >= 1:
                    P.wait("tensor", sDV, sadd_dv[h - 1][C - 1])
                for db in range(DB):
                    lhs = wk_b[h % 2][:, db * 128:(db + 1) * 128]
                    for sp in range(4):
                        o = pB[:, sp * 512:(sp + 1) * 512]
                        rhs = xn_fm[:, db * T + sp * 512: db * T + (sp + 1) * 512]
                        if db == DB - 1 and sp == 3:
                            P.op("tensor", lambda t, o=o, a=lhs, b=rhs:
                                 t.matmul(o, a, b, start=False, stop=True).then_inc(sPE.h, 1))
                        else:
                            P.op("tensor", lambda t, o=o, a=lhs, b=rhs, st=(db == 0),
                                 fi=(db == DB - 1): t.matmul(o, a, b, start=st, stop=fi))
                P.wait("tensor", sDV, qp_dv[h])
                P.wait("tensor", sDV, kp_dv[h])
                P.wait("tensor", skp[h % 2], kp_tp[h])
                P.wait("tensor", sAC, q_ev[h])
                P.wait("tensor", sAC, k_ev[h])
                for c in range(C):
                    cs = slice(c * 128, (c + 1) * 128)
                    vt = v_tm[:, c * D + h * 128: c * D + (h + 1) * 128]
                    if c >= 2:
                        P.wait("tensor", sDV, mask_dv[h][c - 2])
                    P.op("tensor", lambda t, o=pA[:, (c % 2) * 512:(c % 2) * 512 + 128],
                         a=kp_fm[h % 2][:, cs], b=qp_h[h % 2][:, cs]:
                         t.matmul(o, a, b, start=True, stop=True).then_inc(sPE.h, 1))
                    if c >= 2:
                        P.wait("tensor", sDV, sadd_dv[h][c - 2])
                    P.op("tensor", lambda t, o=pB[:, (c % 2) * 512:(c % 2) * 512 + 128],
                         a=kp_tm[h % 2][:, cs], b=vt:
                         t.matmul(o, a, b, start=True, stop=True))
                    P.op("tensor", lambda t, o=pB[:, (c % 2) * 512 + 128:(c % 2) * 512 + 129],
                         a=kp_tm[h % 2][:, cs]:
                         t.matmul(o, a, onesb[:], start=True, stop=True).then_inc(sPE.h, 1))
                    P.wait("tensor", sDV, mask_dv[h][c])
                    if c >= 2:
                        P.wait("tensor", sDV, nev_dv[h][c - 2])
                    P.op("tensor", lambda t, o=pA[:, (2 + c % 2) * 512:(2 + c % 2) * 512 + 128],
                         a=scm[c % 2][:], b=vt:
                         t.matmul(o, a, b, start=True, stop=True))
                    P.op("tensor", lambda t, o=pA[:, (2 + c % 2) * 512 + 128:(2 + c % 2) * 512 + 129],
                         a=scm[c % 2][:]:
                         t.matmul(o, a, onesb[:], start=True, stop=True).then_inc(sPE.h, 1))
            P.wait("tensor", sAC, q_ev[H - 1])
            P.wait("tensor", sDV, sadd_dv[H - 1][C - 1])
            P.wait("tensor", sDV, nev_dv[H - 1][C - 1])
            P.wait("tensor", sDV, mask_dv[H - 1][C - 1])
            for h in range(H):
                o = pA[0:1, h * 4:(h + 1) * 4]
                rhs = mnT_f[:, h * 4:(h + 1) * 4]
                if h == H - 1:
                    P.op("tensor", lambda t, o=o, a=qsum[:, h:h + 1], b=rhs:
                         t.matmul(o, a, b, start=True, stop=True).then_inc(sPE.h, 1))
                else:
                    P.op("tensor", lambda t, o=o, a=qsum[:, h:h + 1], b=rhs:
                         t.matmul(o, a, b, start=True, stop=True))

            for h in range(H):
                P.wait("scalar", sPE, q_done[h])
                if h >= 1:
                    P.wait("scalar", sDV, qp_dv[h - 1])
                P.op("scalar", lambda s, h=h: _dinc(s, s.activation(
                    out=q_sb[:], in_=pA[:, 0:2048], func=AF.Copy,
                    accum_out=qsum[:, h:h + 1]), sAC.h))
                P.wait("scalar", sDV, qp_dv[h] - 1)
                P.op("scalar", lambda s: _dinc(s, s.activation(
                    out=texp[:], in_=tmin[:], func=AF.Exp), sAC.h))
                P.wait("scalar", sPE, k_done[h])
                if h >= 1:
                    P.wait("scalar", sDV, kp_dv[h - 1])
                P.op("scalar", lambda s, h=h: _dinc(s, s.activation(
                    out=k_sb[:], in_=pB[:, 0:2048], func=AF.Copy), sAC.h))
                P.wait("scalar", sDV, kp_dv[h] - 1)
                P.op("scalar", lambda s: _dinc(s, s.activation(
                    out=texp[:], in_=tmin[:], func=AF.Exp), sAC.h))
                P.wait("scalar", sDV, qp_dv[h])
                P.op("scalar", lambda e, h=h: e.dma_start(
                    out=st_qp[h], in_=qp_h[h % 2][:]).then_inc(sqst[h % 2].h, 16))
                P.wait("scalar", sDV, nev_dv[h][C - 1])
                P.op("scalar", lambda e, h=h: e.dma_start(
                    out=st_intra[h], in_=intra_stg[h % 2][:]).then_inc(sist.h, 16))
                P.wait("scalar", sDV, sadd_dv[h][C - 1])
                P.op("scalar", lambda e, h=h: e.dma_start(
                    out=st_sloc[h], in_=sloc_stg[h % 2][:]).then_inc(ssst.h, 16))

            P.op("vector", lambda v: v.memset(relp[:], 0.0))
            for h in range(H):
                P.wait("vector", sAC, q_ev[h])
                P.op("vector", lambda v: _dinc(v, v.tensor_scalar_min(
                    out=tmin[:], in0=q_sb[:], scalar1=0.0), sDV.h))
                P.wait("vector", sAC, texp_qv[h])
                if h >= 2:
                    P.wait("vector", sqst[h % 2], qp_st[h - 2])
                    P.wait("vector", sPE, sc_done[h - 2][C - 1])
                    P.wait("vector", sPE, num_done[h - 2][C - 1])
                P.op("vector", lambda v, h=h: _dinc(v, v.scalar_tensor_tensor(
                    out=qp_h[h % 2][:], in0=q_sb[:], scalar=0.0, in1=texp[:],
                    op0=ALU.max, op1=ALU.add), sDV.h))
                P.wait("vector", sAC, k_ev[h])
                P.op("vector", lambda v: _dinc(v, v.tensor_scalar_min(
                    out=tmin[:], in0=k_sb[:], scalar1=0.0), sDV.h))
                P.wait("vector", sAC, texp_kv[h])
                if h >= 2:
                    P.wait("vector", skp[h % 2], kp_tp[h - 2])
                P.op("vector", lambda v, h=h: _dinc(v, v.scalar_tensor_tensor(
                    out=kp_fm[h % 2][:], in0=k_sb[:], scalar=0.0, in1=texp[:],
                    op0=ALU.max, op1=ALU.add), sDV.h))
                if h >= 1:
                    P.wait("vector", sist, intra_st[h - 1])
                    P.wait("vector", ssst, sloc_st[h - 1])
                for c in range(C):
                    P.wait("vector", sPE, sc_done[h][c])
                    if c >= 2:
                        P.wait("vector", sPE, num_done[h][c - 2])
                    elif h >= 1:
                        P.wait("vector", sPE, num_done[h - 1][C - 2 + c])
                    P.op("vector", lambda v, c=c: _dinc(v, v.tensor_tensor(
                        out=scm[c % 2][:], in0=pA[:, (c % 2) * 512:(c % 2) * 512 + 128],
                        in1=trilb[:], op=ALU.mult), sDV.h))
                    P.op("vector", lambda v, h=h, c=c: v.tensor_copy(
                        out=sloc_stg[h % 2][:, c * 129:(c + 1) * 129],
                        in_=s_run[:, h * 129:(h + 1) * 129]))
                    P.op("vector", lambda v: v.drain())
                    P.wait("vector", sPE, kv_done[h][c])
                    P.op("vector", lambda v, h=h, c=c: _dinc(v, v.tensor_add(
                        out=s_run[:, h * 129:(h + 1) * 129],
                        in0=s_run[:, h * 129:(h + 1) * 129],
                        in1=pB[:, (c % 2) * 512:(c % 2) * 512 + 129]), sDV.h))
                    P.wait("vector", sPE, num_done[h][c])
                    P.op("vector", lambda v, h=h, c=c: _dinc(v, v.tensor_copy(
                        out=intra_stg[h % 2][:, c * 129:(c + 1) * 129],
                        in_=pA[:, (2 + c % 2) * 512:(2 + c % 2) * 512 + 129]), sDV.h))
            P.wait("vector", sPE, rel_done)
            P.op("vector", lambda v: _dinc(v, v.tensor_scalar_mul(
                out=relp[0:1, 0:H * M], in0=pA[0:1, 0:H * M],
                scalar1=SCALE / (T * NCORE)), sDV.h))

            for h in range(H):
                P.wait("sync", sDV, kp_dv[h])
                if h >= 2:
                    P.wait("sync", sPE, kv_done[h - 2][C - 1])
                for c in range(C):
                    P.op("sync", lambda e, h=h, c=c: e.dma_start_transpose(
                        kp_tm[h % 2][:, c * 128:(c + 1) * 128],
                        kp_fm[h % 2][:, c * 128:(c + 1) * 128]).then_inc(skp[h % 2].h, 16))
            for h in range(H):
                P.wait("sync", sDV, sadd_dv[h][C - 1])
                P.op("sync", lambda e, h=h: e.dma_start(
                    out=cc_in[h * 128:(h + 1) * 128, :],
                    in_=s_run[:, h * 129:(h + 1) * 129]).then_inc(sst.h, 16))
            P.wait("sync", sDV, relp_dv)
            P.op("sync", lambda e: e.dma_start(
                out=cc_in[H * 128:H * 128 + 16, :], in_=relp[:, 0:129]).then_inc(sst.h, 16))
            P.emit()
            barrier()
        dbg_dump("stqp", st_qp)
        dbg_dump("stintra", st_intra)
        dbg_dump("stsloc", st_sloc)
        dbg_dump("srun", s_run[:])
        P.emit()
        attS.close()
        if upto <= 3:
            return nc

        with ExitStack() as ph:
            h_buf = sbuf("h_buf", [128, IB * TTN], BF16, ph)
            xnsl = sbuf("xnsl", [128, HB * TTN], BF16, ph)
            gw_q = [sbuf(f"gw_q{i}", [128, NBW * HB * 128], BF16, ph) for i in range(2)]
            uw_q = [sbuf(f"uw_q{i}", [128, NBW * HB * 128], BF16, ph) for i in range(2)]
            dw_q = [sbuf(f"dw_q{i}", [128, IB * 128], BF16, ph) for i in range(2)]
            g_sb = [sbuf(f"g_sb{i}", [128, TTN], BF16, ph) for i in range(2)]
            u_sb = [sbuf(f"u_sb{i}", [128, TTN], BF16, ph) for i in range(2)]
            gfst = [sbuf(f"gfst{i}", [128, 512], BF16, ph) for i in range(2)]
            gftst = [sbuf(f"gftst{i}", [128, 512], BF16, ph) for i in range(2)]
            gst = sbuf("gst", [128, IB], F32, ph)
            ust = sbuf("ust", [128, IB], F32, ph)
            dsc = sbuf("dsc", [128, HB], F32, ph)

            for dst, src in ((gst[:], ap["gsT"]), (ust[:], ap["usT"]), (dsc[:], ap["dsT"])):
                P.op("gpsimd", lambda g, d=dst, s=src: g.dma_start(out=d, in_=s).then_inc(ssc.h, 16))
            ld_scales = ssc.inc(48)

            prev = None
            for tt in range(NTT):
                xnl = [sxn.inc(16) for _ in range(HB)]
                gwl = [0] * NWB; uwl = [0] * NWB
                for b in range(NWB):
                    gwl[b] = sgw[b % 2].inc(16)
                    uwl[b] = suw[b % 2].inc(16)
                dwl = [sdw[hb % 2].inc(16) for hb in range(HB)]
                g_done = [0] * IB; u_done = [0] * IB
                for ib in range(IB):
                    g_done[ib] = sPE.inc()
                    u_done[ib] = sPE.inc()
                dn_done = [[sPE.inc() for _ in range(NSP)] for _ in range(HB)]
                gwsh = [0] * NWB; uwsh = [0] * NWB
                uev = [0] * IB; h_done = [0] * IB
                for ib in range(IB):
                    if tt == 0 and ib % NBW == 0:
                        gwsh[ib // NBW] = sDV.inc()
                        uwsh[ib // NBW] = sDV.inc()
                    uev[ib] = sDV.inc()
                    h_done[ib] = sDV.inc()
                dwsh = [sDV.inc() if tt == 0 else 0 for _ in range(HB)]
                g_ev = [sAC.inc() for _ in range(IB)]
                scw_g = [0] * NWB; scw_u = [0] * NWB
                if tt == 0:
                    for b in range(NWB):
                        scw_g[b] = sscw[b % 2].inc(16)
                        scw_u[b] = sscw[b % 2].inc(16)
                gf_ev = [[0] * NSP for _ in range(HB)]
                scw_d = [0] * HB
                for hb in range(HB):
                    if tt == 0:
                        scw_d[hb] = sdww[hb % 2].inc(16)
                    for ts in range(NSP):
                        gf_ev[hb][ts] = sAC.inc()
                gf_tpa = [[0] * NSP for _ in range(HB)]
                gf_tp = [[0] * NSP for _ in range(HB)]
                for hb in range(HB):
                    for ts in range(NSP):
                        par = (hb * NSP + ts) % 2
                        gf_tpa[hb][ts] = sgf[par].inc(16 * 4)
                        gf_tp[hb][ts] = sgf[par].inc(16 * 4)

                if prev is not None:
                    P.wait("gpsimd", sPE, prev["u_done"][IB - 1])
                for hb in range(HB):
                    P.op("gpsimd", lambda g, hb=hb, tt=tt: g.dma_start(
                        out=xnsl[:, hb * TTN:(hb + 1) * TTN],
                        in_=st_xn[hb][:, tt * TTN:(tt + 1) * TTN]).then_inc(sxn.h, 16))
                for b in range(NWB):
                    if b >= 2:
                        P.wait("gpsimd", sPE, g_done[(b - 2) * NBW + NBW - 1])
                        if tt == 0:
                            P.wait("gpsimd", sscw[b % 2], scw_g[b - 2])
                    elif prev is not None:
                        P.wait("gpsimd", sPE, prev["g_done"][IB - 1])
                        if tt == 1:
                            P.wait("gpsimd", sscw[b % 2], prev["scw_g"][NWB - 2 + b])
                    if tt == 0:
                        P.op("gpsimd", lambda g, b=b: g.dma_start(
                            out=gw_q[b % 2].ap().rearrange("p (n a b) -> p n a b", n=NBW, a=HB),
                            in_=ap["gw"][b * NBW:(b + 1) * NBW].rearrange("n a p b -> p n a b"))
                            .then_inc(sgw[b % 2].h, 16))
                    else:
                        P.op("gpsimd", lambda g, b=b: g.dma_start(
                            out=gw_q[b % 2].ap().rearrange("p (n f) -> p n f", n=NBW),
                            in_=scr_gw[b * NBW:(b + 1) * NBW].rearrange("n p f -> p n f"))
                            .then_inc(sgw[b % 2].h, 16))
                    if b >= 2:
                        P.wait("gpsimd", sPE, u_done[(b - 2) * NBW + NBW - 1])
                        if tt == 0:
                            P.wait("gpsimd", sscw[b % 2], scw_u[b - 2])
                    elif prev is not None:
                        P.wait("gpsimd", sPE, prev["u_done"][IB - 1])
                        if tt == 1:
                            P.wait("gpsimd", sscw[b % 2], prev["scw_u"][NWB - 2 + b])
                    if tt == 0:
                        P.op("gpsimd", lambda g, b=b: g.dma_start(
                            out=uw_q[b % 2].ap().rearrange("p (n a b) -> p n a b", n=NBW, a=HB),
                            in_=ap["uw"][b * NBW:(b + 1) * NBW].rearrange("n a p b -> p n a b"))
                            .then_inc(suw[b % 2].h, 16))
                    else:
                        P.op("gpsimd", lambda g, b=b: g.dma_start(
                            out=uw_q[b % 2].ap().rearrange("p (n f) -> p n f", n=NBW),
                            in_=scr_uw[b * NBW:(b + 1) * NBW].rearrange("n p f -> p n f"))
                            .then_inc(suw[b % 2].h, 16))
                for hb in range(HB):
                    if hb >= 2:
                        P.wait("gpsimd", sPE, dn_done[hb - 2][NSP - 1])
                        if tt == 0:
                            P.wait("gpsimd", sdww[hb % 2], scw_d[hb - 2])
                    elif prev is not None:
                        P.wait("gpsimd", sPE, prev["dn_done"][HB - 1][NSP - 1])
                        if tt == 1:
                            P.wait("gpsimd", sdww[hb % 2], prev["scw_d"][HB - 2 + hb])
                    if tt == 0:
                        P.op("gpsimd", lambda g, hb=hb: g.dma_start(
                            out=dw_q[hb % 2].ap().rearrange("p (a b) -> p a b", a=IB),
                            in_=ap["dw"][hb].rearrange("a p b -> p a b")).then_inc(sdw[hb % 2].h, 16))
                    else:
                        P.op("gpsimd", lambda g, hb=hb: g.dma_start(
                            out=dw_q[hb % 2][:], in_=scr_dw[hb]).then_inc(sdw[hb % 2].h, 16))

                for ib in range(IB):
                    if tt == 0 and ib % NBW == 0:
                        b = ib // NBW
                        P.wait("vector", sgw[b % 2], gwl[b])
                        P.op("vector", lambda v, b=b: _dinc(v, v.tensor_scalar_add(
                            out=gw_q[b % 2][:], in0=gw_q[b % 2][:], scalar1=-128.0)
                            , sDV.h))
                        P.wait("vector", suw[b % 2], uwl[b])
                        P.op("vector", lambda v, b=b: _dinc(v, v.tensor_scalar_add(
                            out=uw_q[b % 2][:], in0=uw_q[b % 2][:], scalar1=-128.0)
                            , sDV.h))
                    if ib == 0 and prev is not None:
                        P.wait("vector", sPE, prev["dn_done"][HB - 1][NSP - 1])
                    P.wait("vector", sPE, u_done[ib])
                    ubase = TTN * (ib % 2)
                    P.op("vector", lambda v, ib=ib, ubase=ubase: _dinc(v, v.tensor_scalar_mul(
                        out=u_sb[ib % 2][:], in0=pB[:, ubase:ubase + TTN],
                        scalar1=ust[:, ib:ib + 1]), sDV.h))
                    P.wait("vector", sAC, g_ev[ib])
                    P.op("vector", lambda v, ib=ib: _dinc(v, v.tensor_mul(
                        out=h_buf[:, ib * TTN:(ib + 1) * TTN],
                        in0=g_sb[ib % 2][:], in1=u_sb[ib % 2][:]), sDV.h))
                if tt == 0:
                    for hb in range(HB):
                        P.wait("vector", sdw[hb % 2], dwl[hb])
                        P.op("vector", lambda v, hb=hb: _dinc(v, v.tensor_scalar_add(
                            out=dw_q[hb % 2][:], in0=dw_q[hb % 2][:], scalar1=-128.0)
                            , sDV.h))

                for ib in range(IB):
                    gbase = TTN * (ib % 2)
                    bi, bo = ib // NBW, ib % NBW
                    if ib == 0:
                        P.wait("tensor", sxn, xnl[HB - 1])
                        P.wait("tensor", ssc, ld_scales)
                        if prev is not None:
                            P.wait("tensor", sAC, prev["gf_ev"][HB - 1][NSP - 1])
                            P.wait("tensor", sDV, prev["uev"][IB - 1])
                    if bo == 0:
                        if tt == 0:
                            P.wait("tensor", sDV, gwsh[bi])
                        else:
                            P.wait("tensor", sgw[bi % 2], gwl[bi])
                    if ib >= 2:
                        P.wait("tensor", sAC, g_ev[ib - 2])
                    for hb in range(HB):
                        lhs = gw_q[bi % 2][:, (bo * HB + hb) * 128:(bo * HB + hb + 1) * 128]
                        for sp in range(NSP):
                            o = pA[:, gbase + sp * 512: gbase + (sp + 1) * 512]
                            rhs = xnsl[:, hb * TTN + sp * 512: hb * TTN + (sp + 1) * 512]
                            if hb == HB - 1 and sp == NSP - 1:
                                P.op("tensor", lambda t, o=o, a=lhs, b=rhs:
                                     t.matmul(o, a, b, start=False, stop=True).then_inc(sPE.h, 1))
                            else:
                                P.op("tensor", lambda t, o=o, a=lhs, b=rhs, st=(hb == 0),
                                     fi=(hb == HB - 1): t.matmul(o, a, b, start=st, stop=fi))
                    if bo == 0:
                        if tt == 0:
                            P.wait("tensor", sDV, uwsh[bi])
                        else:
                            P.wait("tensor", suw[bi % 2], uwl[bi])
                    if ib >= 2:
                        P.wait("tensor", sDV, uev[ib - 2])
                    for hb in range(HB):
                        lhs = uw_q[bi % 2][:, (bo * HB + hb) * 128:(bo * HB + hb + 1) * 128]
                        for sp in range(NSP):
                            o = pB[:, gbase + sp * 512: gbase + (sp + 1) * 512]
                            rhs = xnsl[:, hb * TTN + sp * 512: hb * TTN + (sp + 1) * 512]
                            if hb == HB - 1 and sp == NSP - 1:
                                P.op("tensor", lambda t, o=o, a=lhs, b=rhs:
                                     t.matmul(o, a, b, start=False, stop=True).then_inc(sPE.h, 1))
                            else:
                                P.op("tensor", lambda t, o=o, a=lhs, b=rhs, st=(hb == 0),
                                     fi=(hb == HB - 1): t.matmul(o, a, b, start=st, stop=fi))
                if tt == 0 and "hbuf0" in dbg:
                    P.wait("sync", sDV, h_done[IB - 1])
                    P.op("sync", lambda e: e.dma_start(out=dbg["hbuf0"], in_=h_buf[:])
                         .then_inc(sst.h, 16))
                    sst.inc(16)
                if tt == 0 and "xnsl0" in dbg:
                    P.wait("sync", sxn, xnl[HB - 1])
                    P.op("sync", lambda e: e.dma_start(out=dbg["xnsl0"], in_=xnsl[:])
                         .then_inc(sst.h, 16))
                    sst.inc(16)
                P.wait("tensor", sDV, h_done[IB - 1])
                P.wait("tensor", sAC, g_ev[IB - 1])
                for hb in range(HB):
                    for ts in range(NSP):
                        idx = hb * NSP + ts
                        par = idx % 2
                        if ts == 0:
                            if tt == 0:
                                P.wait("tensor", sDV, dwsh[hb])
                            else:
                                P.wait("tensor", sdw[hb % 2], dwl[hb])
                        if idx >= 2:
                            P.wait("tensor", sAC, gf_ev[(idx - 2) // NSP][(idx - 2) % NSP])
                        for ib in range(IB):
                            lhs = dw_q[hb % 2][:, ib * 128:(ib + 1) * 128]
                            rhs = h_buf[:, ib * TTN + ts * 512: ib * TTN + (ts + 1) * 512]
                            o = pA[:, par * 512:(par + 1) * 512]
                            if ib == IB - 1:
                                P.op("tensor", lambda t, o=o, a=lhs, b=rhs:
                                     t.matmul(o, a, b, start=False, stop=True).then_inc(sPE.h, 1))
                            else:
                                P.op("tensor", lambda t, o=o, a=lhs, b=rhs, st=(ib == 0):
                                     t.matmul(o, a, b, start=st, stop=False))

                for ib in range(IB):
                    P.wait("scalar", sPE, g_done[ib])
                    if ib >= 2:
                        P.wait("scalar", sDV, h_done[ib - 2])
                    elif prev is not None:
                        P.wait("scalar", sDV, prev["h_done"][IB - 1])
                    gbase = TTN * (ib % 2)
                    P.op("scalar", lambda s, ib=ib, gbase=gbase: _dinc(s, s.activation(
                        out=g_sb[ib % 2][:], in_=pA[:, gbase:gbase + TTN], func=AF.Silu,
                        scale=gst[:, ib:ib + 1]), sAC.h))
                    if tt == 0 and ib % NBW == 0:
                        b = ib // NBW
                        P.wait("scalar", sDV, gwsh[b])
                        P.op("scalar", lambda e, b=b: e.dma_start(
                            out=scr_gw[b * NBW:(b + 1) * NBW].rearrange("n p f -> p n f"),
                            in_=gw_q[b % 2].ap().rearrange("p (n f) -> p n f", n=NBW))
                            .then_inc(sscw[b % 2].h, 16))
                        P.wait("scalar", sDV, uwsh[b])
                        P.op("scalar", lambda e, b=b: e.dma_start(
                            out=scr_uw[b * NBW:(b + 1) * NBW].rearrange("n p f -> p n f"),
                            in_=uw_q[b % 2].ap().rearrange("p (n f) -> p n f", n=NBW))
                            .then_inc(sscw[b % 2].h, 16))
                for hb in range(HB):
                    if tt == 0:
                        P.wait("scalar", sDV, dwsh[hb])
                        P.op("scalar", lambda e, hb=hb: e.dma_start(
                            out=scr_dw[hb], in_=dw_q[hb % 2][:]).then_inc(sdww[hb % 2].h, 16))
                    for ts in range(NSP):
                        idx = hb * NSP + ts
                        par = idx % 2
                        P.wait("scalar", sPE, dn_done[hb][ts])
                        if idx >= 2:
                            P.wait("scalar", sgf[idx % 2], gf_tp[(idx - 2) // NSP][(idx - 2) % NSP])
                        elif prev is not None:
                            P.wait("scalar", sgf[idx % 2], prev["gf_tp"][HB - 1][NSP - 1])
                        P.op("scalar", lambda s, hb=hb, par=par: _dinc(s, s.activation(
                            out=gfst[par][:], in_=pA[:, par * 512:(par + 1) * 512],
                            func=AF.Silu, scale=dsc[:, hb:hb + 1]), sAC.h))

                for hb in range(HB):
                    for ts in range(NSP):
                        idx = hb * NSP + ts
                        par = idx % 2
                        P.wait("sync", sAC, gf_ev[hb][ts])
                        if idx >= 2:
                            P.wait("sync", sgf[par], gf_tp[(idx - 2) // NSP][(idx - 2) % NSP])
                        elif prev is not None:
                            P.wait("sync", sgf[par], prev["gf_tp"][HB - 1][NSP - 1])
                        for tc4 in range(4):
                            P.op("sync", lambda e, par=par, tc4=tc4: e.dma_start_transpose(
                                gftst[par][:, tc4 * 128:(tc4 + 1) * 128],
                                gfst[par][:, tc4 * 128:(tc4 + 1) * 128]).then_inc(sgf[par].h, 16))
                        P.wait("sync", sgf[par], gf_tpa[hb][ts])
                        for tc4 in range(4):
                            tchunk = tt * (TTN // 128) + ts * 4 + tc4
                            P.op("sync", lambda e, par=par, tc4=tc4, tchunk=tchunk, hb=hb:
                                 e.dma_start(
                                     out=st_gf[tchunk][:, hb * 128:(hb + 1) * 128],
                                     in_=gftst[par][:, tc4 * 128:(tc4 + 1) * 128])
                                 .then_inc(sgf[par].h, 16))
                P.emit()
                prev = dict(g_done=g_done, u_done=u_done, dn_done=dn_done,
                            gf_ev=gf_ev, gf_tp=gf_tp, h_done=h_done, uev=uev,
                            scw_g=scw_g, scw_u=scw_u, scw_d=scw_d)
            barrier()
        dbg_dump("stgf", st_gf)
        P.emit()
        if upto <= 4:
            return nc

        with ExitStack() as ph5:
            mem_bf = sbuf("mem_bf", [128, H * 512], BF16, Gs)
            s_runb = sbuf("s_runb", [128, H * 129], BF16, Gs)
            w_oT = sbuf("w_oT", [128, DB * D], BF16, Gs)
            W_b = sbuf("W_b", [128, H * M], F32, Gs)
            G_sb = sbuf("G_sb", [128, H], F32, Gs)
            Gc_sb = sbuf("Gc_sb", [128, H], F32, Gs)
            Geff = sbuf("Geff", [128, H], F32, Gs)

            sgath = [sbuf(f"sgath{i}", [128, 8 * 129], F32, ph5) for i in range(2)]
            relg = sbuf("relg", [1, 8 * H * M], F32, ph5)
            rel_all = sbuf("rel_all", [1, H * M], F32, ph5)
            rel_t = sbuf("rel_t", [H, M], F32, ph5)
            mnr_sb = sbuf("mnr_sb", [M * H, 128], F32, ph5)
            mnr1 = sbuf("mnr1", [M * H, 1], F32, ph5)
            msel_sb = sbuf("msel_sb", [M * H, M], F32, ph5)
            e_sb = sbuf("e_sb", [1, M], F32, ph5)
            esum = sbuf("esum", [1, 1], F32, ph5)
            allf = sbuf("allf", [1, 1], F32, ph5)
            notall = sbuf("notall", [1, 1], F32, ph5)
            e_b = sbuf("e_b", [H, M], F32, ph5)
            one_m = sbuf("one_m", [H, M], F32, ph5)
            relm = sbuf("relm", [H, M], F32, ph5)
            rmax = sbuf("rmax", [H, 1], F32, ph5)
            negmax = sbuf("negmax", [H, 1], F32, ph5)
            wex = sbuf("wex", [H, M], F32, ph5)
            ssum = sbuf("ssum", [H, 1], F32, ph5)
            srq = sbuf("srq", [H, 1], F32, ph5)
            W_sb = sbuf("W_sb", [H, M], F32, ph5)
            gate_sb = sbuf("gate_sb", [1, H], F32, ph5)
            gsig = sbuf("gsig", [1, H], F32, ph5)
            gsig2 = sbuf("gsig2", [1, H], F32, ph5)
            maskcol = sbuf("maskcol", [128, 8], F32, ph5)

            P.wait("gpsimd", sCC, 1)

            def gld(sem, dst, src, **kw):
                P.op("gpsimd", lambda g, d=dst, s=src, sem=sem, kw=kw:
                     g.dma_start(out=d, in_=s, **kw).then_inc(sem.h, 16))
                return sem.inc(16)
            ccv = cc_out.rearrange("(r q) f -> r q f", q=CCR)
            ld_relg = gld(srelg, relg.ap().rearrange("a (r f) -> a r f", r=8),
                          ccv[:, H * 128:H * 128 + 1, 0:H * M].rearrange("r a f -> a r f"))
            ld_mnr = gld(smnr, mnr_sb[:], ap["mnrows"])
            ld_gate = gld(sgt2, gate_sb[:], ap["gate"])
            ld_msel = gld(smsl, msel_sb[:], ap["msel"])
            ld_sg = [0] * H
            pe_mkc = sPE.inc()
            pe_emp = sPE.inc()
            pe_eb = sPE.inc()
            pe_wb = sPE.inc()
            pe_gb = sPE.inc()
            pe_geff = sPE.inc()
            ac_wex = sAC.inc()
            ac_gsig = sAC.inc()
            dv_rel = sDV.inc()
            dv_mnred = sDV.inc()
            dv_mkc = sDV.inc()
            dv_e = sDV.inc()
            dv_esum = sDV.inc()
            dv_notall = sDV.inc()
            dv_sinit = [0] * H
            for h in range(H):
                dv_sinit[h] = sDV.inc()
            dv_srunb = sDV.inc()
            dv_ebm = sDV.inc()
            dv_relm = sDV.inc()
            dv_rmax = sDV.inc()
            dv_negmax = sDV.inc()
            dv_ssum = sDV.inc()
            dv_srq = sDV.inc()
            dv_wsb = sDV.inc()
            dv_wbev = sDV.inc()
            dv_gsig2 = sDV.inc()
            dv_gbev = sDV.inc()
            dv_geffev = sDV.inc()
            sy_relbuf = srel.inc(16)
            sy_wbuf = swbf.inc(16)

            for h in range(H):
                if h >= 2:
                    P.wait("gpsimd", sDV, dv_sinit[h - 2])
                ld_sg[h] = gld(
                    ssg[h % 2],
                    sgath[h % 2].ap().rearrange("p (r f) -> p r f", r=8),
                    ccv[:, h * 128:(h + 1) * 128, :].rearrange("r p f -> p r f"))
            ld_mem = [0] * H
            for h in range(H):
                ld_mem[h] = gld(
                    sGt,
                    mem_bf[:, h * 512:(h + 1) * 512].rearrange("p (m e) -> p m e", m=M),
                    ap["mem"][:, h].rearrange("m p e -> p m e"))
            ld_wo = [0] * DB
            for db in range(DB):
                ld_wo[db] = gld(sGt, w_oT[:, db * D:(db + 1) * D], ap["woT"][db])
            P.wait("gpsimd", srel, sy_relbuf)
            ld_relt = gld(srlt, rel_t[:], relbuf.rearrange("a (h m) -> (a h) m", h=H))
            P.wait("gpsimd", swbf, sy_wbuf)
            ld_wfl = gld(swfl, relg[:, 0:H * M], wbuf)

            P.op("tensor", lambda t: t.matmul(pB[:, 0:8], onesr[:], rmk[:],
                                              start=True, stop=True).then_inc(sPE.h, 1))
            P.wait("tensor", sDV, dv_mnred)
            P.wait("tensor", smsl, ld_msel)
            P.op("tensor", lambda t: t.matmul(pA[0:1, 0:4], mnr1[:], msel_sb[:],
                                              start=True, stop=True).then_inc(sPE.h, 1))
            P.wait("tensor", sDV, dv_e)
            P.op("tensor", lambda t: t.matmul(pA[0:H, 512:516], onesr[:, 0:H], e_sb[:],
                                              start=True, stop=True).then_inc(sPE.h, 1))
            P.wait("tensor", swfl, ld_wfl)
            P.op("tensor", lambda t: t.matmul(pB[:, 512:512 + H * M], onesr[:], relg[:, 0:H * M],
                                              start=True, stop=True).then_inc(sPE.h, 1))
            P.wait("tensor", sAC, ac_gsig)
            P.op("tensor", lambda t: t.matmul(pB[:, 1024:1024 + H], onesr[:], gsig[:],
                                              start=True, stop=True).then_inc(sPE.h, 1))
            P.wait("tensor", sDV, dv_gsig2)
            P.op("tensor", lambda t: t.matmul(pB[:, 1536:1536 + H], onesr[:], gsig2[:],
                                              start=True, stop=True).then_inc(sPE.h, 1))

            P.wait("vector", srelg, ld_relg)
            P.op("vector", lambda v: v.tensor_add(out=rel_all[:], in0=relg[:, 0:64],
                                                  in1=relg[:, 64:128]))
            P.op("vector", lambda v: v.drain())
            for r in range(2, 8):
                if r == 7:
                    P.op("vector", lambda v, r=r: _dinc(v, v.tensor_add(
                        out=rel_all[:], in0=rel_all[:],
                        in1=relg[:, r * 64:(r + 1) * 64]), sDV.h))
                else:
                    P.op("vector", lambda v, r=r: v.tensor_add(
                        out=rel_all[:], in0=rel_all[:], in1=relg[:, r * 64:(r + 1) * 64]))
                    P.op("vector", lambda v: v.drain())
            P.wait("vector", smnr, ld_mnr)
            P.op("vector", lambda v: _dinc(v, v.reduce_sum(
                out=mnr1[:], in_=mnr_sb[:], axis=AX.X), sDV.h))
            P.wait("vector", sPE, pe_mkc)
            P.op("vector", lambda v: _dinc(v, v.tensor_copy(out=maskcol[:], in_=pB[:, 0:8])
                 , sDV.h))
            P.wait("vector", sPE, pe_emp)
            P.op("vector", lambda v: _dinc(v, v.tensor_single_scalar(
                out=e_sb[:], in_=pA[0:1, 0:4], scalar=1e-6, op=ALU.is_lt), sDV.h))
            P.wait("vector", sDV, dv_e)
            P.op("vector", lambda v: _dinc(v, v.reduce_sum(out=esum[:], in_=e_sb[:], axis=AX.X)
                 , sDV.h))
            P.wait("vector", sDV, dv_esum)
            P.op("vector", lambda v: v.tensor_single_scalar(
                out=allf[:], in_=esum[:], scalar=3.5, op=ALU.is_ge))
            P.op("vector", lambda v: v.drain())
            P.op("vector", lambda v: _dinc(v, v.tensor_scalar(
                out=notall[:], in0=allf[:], scalar1=-1.0, scalar2=1.0,
                op0=ALU.mult, op1=ALU.add), sDV.h))
            for h in range(H):
                P.wait("vector", ssg[h % 2], ld_sg[h])
                P.wait("vector", sDV, dv_mkc)
                P.op("vector", lambda v, h=h: v.tensor_scalar_mul(
                    out=s_run[:, h * 129:(h + 1) * 129],
                    in0=sgath[h % 2][:, 0:129], scalar1=maskcol[:, 0:1]))
                for r in range(1, 8):
                    def _sin(v, h=h, r=r):
                        v.drain()
                        v.scalar_tensor_tensor(
                            out=s_run[:, h * 129:(h + 1) * 129],
                            in0=sgath[h % 2][:, r * 129:(r + 1) * 129],
                            scalar=maskcol[:, r:r + 1],
                            in1=s_run[:, h * 129:(h + 1) * 129],
                            op0=ALU.mult, op1=ALU.add)
                        if r == 7:
                            v.drain().then_inc(sDV.h, 1)
                    P.op("vector", _sin)
            P.op("vector", lambda v: _dinc(v, v.tensor_copy(out=s_runb[:], in_=s_run[:]), sDV.h))
            P.wait("vector", sPE, pe_eb)
            P.op("vector", lambda v: v.tensor_copy(out=e_b[:], in_=pA[0:H, 512:516]))
            P.op("vector", lambda v: v.drain())
            P.op("vector", lambda v: _dinc(v, v.tensor_scalar(
                out=one_m[:], in0=e_b[:], scalar1=-1.0, scalar2=1.0,
                op0=ALU.mult, op1=ALU.add), sDV.h))
            P.wait("vector", srlt, ld_relt)
            P.op("vector", lambda v: v.tensor_mul(out=relm[:], in0=rel_t[:], in1=one_m[:]))
            P.op("vector", lambda v: v.drain())
            P.op("vector", lambda v: _dinc(v, v.scalar_tensor_tensor(
                out=relm[:], in0=e_b[:], scalar=-1e30, in1=relm[:],
                op0=ALU.mult, op1=ALU.add), sDV.h))
            P.wait("vector", sDV, dv_relm)
            P.op("vector", lambda v: _dinc(v, v.reduce_max(out=rmax[:], in_=relm[:], axis=AX.X)
                 , sDV.h))
            P.wait("vector", sDV, dv_rmax)
            P.op("vector", lambda v: _dinc(v, v.tensor_scalar_mul(out=negmax[:], in0=rmax[:], scalar1=-1.0)
                 , sDV.h))
            P.wait("vector", sAC, ac_wex)
            P.op("vector", lambda v: _dinc(v, v.reduce_sum(out=ssum[:], in_=wex[:], axis=AX.X)
                 , sDV.h))
            P.wait("vector", sDV, dv_ssum)
            P.op("vector", lambda v: _dinc(v, v.reciprocal(out=srq[:], in_=ssum[:]), sDV.h))
            P.wait("vector", sDV, dv_srq)
            P.op("vector", lambda v: _dinc(v, v.tensor_scalar_mul(out=W_sb[:], in0=wex[:], scalar1=srq[:])
                 , sDV.h))
            P.wait("vector", sPE, pe_wb)
            P.op("vector", lambda v: _dinc(v, v.tensor_copy(out=W_b[:], in_=pB[:, 512:512 + H * M])
                 , sDV.h))
            P.wait("vector", sAC, ac_gsig)
            P.wait("vector", sDV, dv_notall)
            P.op("vector", lambda v: _dinc(v, v.tensor_scalar_mul(
                out=gsig2[:], in0=gsig[:], scalar1=notall[0:1, 0:1]), sDV.h))
            P.wait("vector", sPE, pe_gb)
            P.op("vector", lambda v: v.tensor_copy(out=G_sb[:], in_=pB[:, 1024:1024 + H]))
            P.op("vector", lambda v: v.drain())
            P.op("vector", lambda v: _dinc(v, v.tensor_scalar(
                out=Gc_sb[:], in0=G_sb[:], scalar1=-1.0, scalar2=1.0,
                op0=ALU.mult, op1=ALU.add), sDV.h))
            P.wait("vector", sPE, pe_geff)
            P.op("vector", lambda v: _dinc(v, v.tensor_copy(out=Geff[:], in_=pB[:, 1536:1536 + H])
                 , sDV.h))

            P.wait("scalar", sDV, dv_negmax)
            P.op("scalar", lambda s: _dinc(s, s.activation(out=wex[:], in_=relm[:], func=AF.Exp,
                                                  bias=negmax[:]), sAC.h))
            P.wait("scalar", sgt2, ld_gate)
            P.op("scalar", lambda s: _dinc(s, s.activation(out=gsig[:], in_=gate_sb[:], func=AF.Sigmoid)
                 , sAC.h))

            P.wait("sync", sDV, dv_rel)
            P.op("sync", lambda e: e.dma_start(out=relbuf, in_=rel_all[:]).then_inc(srel.h, 16))
            P.wait("sync", sDV, dv_wsb)
            P.op("sync", lambda e: e.dma_start(out=wbuf, in_=W_sb[:]).then_inc(swbf.h, 16))
            P.emit()
            barrier()
        dbg_dump("sinit", s_run[:])
        dbg_dump("wb", W_b[:])
        dbg_dump("ccout", cc_out)
        if "gg" in dbg:
            P.op("gpsimd", lambda g: g.dma_start(
                out=dbg["gg"][:, 0:H], in_=G_sb[:]).then_inc(sGt.h, 16)); sGt.inc(16)
            P.op("gpsimd", lambda g: g.dma_start(
                out=dbg["gg"][:, H:2 * H], in_=Gc_sb[:]).then_inc(sGt.h, 16)); sGt.inc(16)
            P.op("gpsimd", lambda g: g.dma_start(
                out=dbg["gg"][:, 2 * H:3 * H], in_=Geff[:]).then_inc(sGt.h, 16)); sGt.inc(16)
        P.emit()
        if upto <= 5:
            return nc

        with ExitStack() as ph:
            x_c = [sbuf(f"x_c{i}", [128, D], F32, ph) for i in range(2)]
            qp_sl = [sbuf(f"qp_sl{i}", [128, H * 128], BF16, ph) for i in range(2)]
            in_sl = [sbuf(f"in_sl{i}", [128, H * 129], BF16, ph) for i in range(2)]
            sl_sl = [sbuf(f"sl_sl{i}", [128, H * 129], BF16, ph) for i in range(2)]
            gf_c = [sbuf(f"gf_c{i}", [128, D], BF16, ph) for i in range(2)]
            saug = [sbuf(f"saug{i}", [128, H * 129], BF16, ph) for i in range(2)]
            att_tm = [sbuf(f"att_tm{i}", [128, D], BF16, ph) for i in range(2)]
            afm = [sbuf(f"afm{i}", [128, D], BF16, ph) for i in range(2)]
            out_sb = [sbuf(f"out_sb{i}", [128, D], F32, ph) for i in range(2)]
            tsb = sbuf("tsb", [128, D], F32, ph)
            denc = [sbuf(f"denc{i}", [128, 1], F32, ph) for i in range(4)]
            dreq = [sbuf(f"dreq{i}", [128, 1], F32, ph) for i in range(4)]
            loct = [sbuf(f"loct{i}", [128, 128], BF16, ph) for i in range(4)]
            nrmc = [sbuf(f"nrmc{i}", [128, M], F32, ph) for i in range(4)]
            nrq = [sbuf(f"nrq{i}", [128, M], F32, ph) for i in range(4)]
            sc4 = [sbuf(f"sc4_{i}", [128, M], F32, ph) for i in range(4)]
            mem_e = [sbuf(f"mem_e{i}", [128, 512], BF16, ph) for i in range(4)]
            numc = [sbuf(f"numc{i}", [128, 128], BF16, ph) for i in range(4)]
            qm = [sbuf(f"qm{i}", [128, 4 * 128], BF16, ph) for i in range(4)]
            s01 = [sbuf(f"s01_{i}", [128, 128], BF16, ph) for i in range(4)]
            s23 = [sbuf(f"s23_{i}", [128, 128], BF16, ph) for i in range(4)]
            macc = [sbuf(f"macc{i}", [128, 128], BF16, ph) for i in range(4)]
            t1 = [sbuf(f"t1_{i}", [128, 128], BF16, ph) for i in range(4)]

            ldx = [0] * C; ldqp = [0] * C; ldin = [0] * C; ldsl = [0] * C; ldgf = [0] * C
            for c in range(C):
                ldx[c] = sx[c % 2].inc(16)
                ldqp[c] = sqp[c % 2].inc(16)
                ldin[c] = sin_[c % 2].inc(16)
                ldsl[c] = ssl[c % 2].inc(16)
                ldgf[c] = sgfc[c % 2].inc(16)
            pe_h = [[0] * H for _ in range(C)]
            wo_done = [[0, 0] for _ in range(C)]
            for c in range(C):
                for h in range(H):
                    pe_h[c][h] = sPE.inc()
                wo_done[c][0] = sPE.inc()
                wo_done[c][1] = sPE.inc()
            NHG = 4
            saug_dv = [0] * C
            st1h = [[0] * H for _ in range(C)]
            att_dv = [[0] * H for _ in range(C)]
            fin_dv = [[0, 0] for _ in range(C)]
            for c in range(C):
                saug_dv[c] = sDV.inc()
                for hg in range(H // NHG):
                    for hh in range(NHG):
                        st1h[c][hg * NHG + hh] = sDV.inc()
                    for hh in range(NHG):
                        att_dv[c][hg * NHG + hh] = sDV.inc()
                fin_dv[c][0] = sDV.inc()
                fin_dv[c][1] = sDV.inc()
            att_tp = [satp[c % 2].inc(16 * H) for c in range(C)]
            out_wr = [sst.inc(16) for _ in range(C)]

            def _pslot(h):
                s = h % 4
                return (s % 2) * 512 + (s // 2) * 256

            stqp_v = st_qp.rearrange("h p t -> p h t")
            stin_v = st_intra.rearrange("h p t -> p h t")
            stsl_v = st_sloc.rearrange("h p t -> p h t")
            for c in range(C):
                if c >= 2:
                    P.wait("gpsimd", sDV, fin_dv[c - 2][1])
                    P.wait("gpsimd", sPE, pe_h[c - 2][H - 1])
                P.op("gpsimd", lambda g, c=c: g.dma_start(
                    out=x_c[c % 2][:], in_=ap["x"][c * 128:(c + 1) * 128, :]).then_inc(sx[c % 2].h, 16))
                P.op("gpsimd", lambda g, c=c: g.dma_start(
                    out=qp_sl[c % 2].ap().rearrange("p (h t) -> p h t", h=H),
                    in_=stqp_v[:, :, c * 128:(c + 1) * 128]).then_inc(sqp[c % 2].h, 16))
                P.op("gpsimd", lambda g, c=c: g.dma_start(
                    out=in_sl[c % 2].ap().rearrange("p (h t) -> p h t", h=H),
                    in_=stin_v[:, :, c * 129:(c + 1) * 129]).then_inc(sin_[c % 2].h, 16))
                P.op("gpsimd", lambda g, c=c: g.dma_start(
                    out=sl_sl[c % 2].ap().rearrange("p (h t) -> p h t", h=H),
                    in_=stsl_v[:, :, c * 129:(c + 1) * 129]).then_inc(ssl[c % 2].h, 16))
                P.op("gpsimd", lambda g, c=c: g.dma_start(
                    out=gf_c[c % 2][:], in_=st_gf[c]).then_inc(sgfc[c % 2].h, 16))
            P.wait("gpsimd", sst, out_wr[C - 1])

            flat = []
            for c in range(C):
                for h in range(H):
                    flat.append((c, h))
            NHG = 4
            for c in range(C):
                P.wait("vector", ssl[c % 2], ldsl[c])
                if c >= 2:
                    P.wait("vector", sPE, pe_h[c - 2][H - 1])
                    P.wait("vector", satp[c % 2], att_tp[c - 2])
                    P.wait("vector", sst, out_wr[c - 2])
                P.op("vector", lambda v, c=c: _dinc(v, v.tensor_add(
                    out=saug[c % 2][:], in0=s_runb[:], in1=sl_sl[c % 2][:]), sDV.h))
                for hg in range(H // NHG):
                    h0 = hg * NHG
                    gi0 = c * H + h0
                    for hh in range(NHG):
                        h = h0 + hh
                        pS = _pslot(h)
                        pM = (h % 4) * 512
                        P.wait("vector", sPE, pe_h[c][h])
                        P.op("vector", lambda v, h=h, pM=pM: v.tensor_copy(
                            out=mem_e[h % 4][:], in_=pA[:, pM:pM + 512]))
                        P.op("vector", lambda v, h=h, pS=pS: v.tensor_copy(
                            out=numc[h % 4][:], in_=pB[:, pS:pS + 128]))
                        P.op("vector", lambda v, h=h, pS=pS: v.tensor_scalar_max(
                            out=denc[h % 4][:], in0=pB[:, pS + 128:pS + 129], scalar1=1e-6))
                        P.op("vector", lambda v, h=h, pS=pS: v.tensor_scalar_max(
                            out=nrmc[h % 4][:], in0=pB[:, pS + 160:pS + 160 + M],
                            scalar1=1e-6).then_inc(sDV.h, 1))
                    P.op("vector", lambda v: v.drain())
                    for hh in range(NHG):
                        h = h0 + hh
                        P.op("vector", lambda v, h=h: v.reciprocal(
                            out=dreq[h % 4][:], in_=denc[h % 4][:]))
                        P.op("vector", lambda v, h=h: v.reciprocal(
                            out=nrq[h % 4][:], in_=nrmc[h % 4][:]))
                    P.op("vector", lambda v: v.drain())
                    for hh in range(NHG):
                        h = h0 + hh
                        P.op("vector", lambda v, h=h: v.tensor_mul(
                            out=sc4[h % 4][:], in0=nrq[h % 4][:], in1=W_b[:, h * M:(h + 1) * M]))
                    P.op("vector", lambda v: v.drain())
                    for hh in range(NHG):
                        h = h0 + hh
                        pS = _pslot(h)
                        P.op("vector", lambda v, h=h: v.tensor_scalar_mul(
                            out=loct[h % 4][:], in0=numc[h % 4][:], scalar1=dreq[h % 4][:]))
                        for m in range(M):
                            P.op("vector", lambda v, h=h, m=m: v.tensor_scalar_mul(
                                out=qm[h % 4][:, m * 128:(m + 1) * 128],
                                in0=mem_e[h % 4][:, m * 128:(m + 1) * 128],
                                scalar1=sc4[h % 4][:, m:m + 1]))
                    P.op("vector", lambda v: v.drain())
                    for hh in range(NHG):
                        h = h0 + hh
                        P.op("vector", lambda v, h=h: v.tensor_add(
                            out=s01[h % 4][:], in0=qm[h % 4][:, 0:128], in1=qm[h % 4][:, 128:256]))
                        P.op("vector", lambda v, h=h: v.tensor_add(
                            out=s23[h % 4][:], in0=qm[h % 4][:, 256:384], in1=qm[h % 4][:, 384:512]))
                        P.op("vector", lambda v, h=h: v.tensor_scalar_mul(
                            out=t1[h % 4][:], in0=loct[h % 4][:], scalar1=Gc_sb[:, h:h + 1]))
                    P.op("vector", lambda v: v.drain())
                    for hh in range(NHG):
                        h = h0 + hh
                        P.op("vector", lambda v, h=h: v.tensor_add(
                            out=macc[h % 4][:], in0=s01[h % 4][:], in1=s23[h % 4][:]))
                    P.op("vector", lambda v: v.drain())
                    for hh in range(NHG):
                        h = h0 + hh
                        P.op("vector", lambda v, c=c, h=h: v.scalar_tensor_tensor(
                            out=att_tm[c % 2][:, h * 128:(h + 1) * 128], in0=macc[h % 4][:],
                            scalar=Geff[:, h:h + 1], in1=t1[h % 4][:],
                            op0=ALU.mult, op1=ALU.add))
                    P.op("vector", lambda v: v.drain().then_inc(sDV.h, NHG))
                P.wait("vector", sgfc[c % 2], ldgf[c])
                P.wait("vector", sx[c % 2], ldx[c])
                for half in range(2):
                    P.wait("vector", sPE, wo_done[c][half])
                    P.op("vector", lambda v, c=c, half=half: v.tensor_tensor(
                        out=tsb[:, 0:1024], in0=pB[:, 1024:2048],
                        in1=gf_c[c % 2][:, half * 1024:(half + 1) * 1024], op=ALU.mult))
                    P.op("vector", lambda v: v.drain())
                    P.op("vector", lambda v, c=c, half=half: _dinc(v, v.tensor_add(
                        out=out_sb[c % 2][:, half * 1024:(half + 1) * 1024],
                        in0=tsb[:, 0:1024],
                        in1=x_c[c % 2][:, half * 1024:(half + 1) * 1024]), sDV.h))

            for c in range(C):
                P.wait("tensor", sDV, saug_dv[c])
                P.wait("tensor", sqp[c % 2], ldqp[c])
                P.wait("tensor", sin_[c % 2], ldin[c])
                for h in range(H):
                    pS = _pslot(h)
                    pM = (h % 4) * 512
                    qp_t = qp_sl[c % 2][:, h * 128:(h + 1) * 128]
                    gi = c * H + h
                    if gi >= 2:
                        c2, h2 = flat[gi - 2]
                        P.wait("tensor", sDV, st1h[c2][h2])
                    P.op("tensor", lambda t, o=pB[:, pS:pS + 129], a=qp_t,
                         b=saug[c % 2][:, h * 129:(h + 1) * 129]:
                         t.matmul(o, a, b, start=True, stop=False))
                    P.op("tensor", lambda t, o=pB[:, pS:pS + 129],
                         b=in_sl[c % 2][:, h * 129:(h + 1) * 129]:
                         t.matmul(o, identb[:], b, start=False, stop=True))
                    P.op("tensor", lambda t, o=pA[:, pM:pM + 512], a=qp_t,
                         b=mem_bf[:, h * 512:(h + 1) * 512]:
                         t.matmul(o, a, b, start=True, stop=True))
                    P.op("tensor", lambda t, o=pB[:, pS + 160:pS + 160 + M], a=qp_t,
                         b=mnT_b[:, h * M:(h + 1) * M]:
                         t.matmul(o, a, b, start=True, stop=True).then_inc(sPE.h, 1))
                P.wait("tensor", satp[c % 2], att_tp[c])
                for half in range(2):
                    if c >= 1 or half >= 1:
                        P.wait("tensor", sDV, fin_dv[c - 1][1] if half == 0 else fin_dv[c][0])
                    for db in range(DB):
                        lhs = afm[c % 2][:, db * 128:(db + 1) * 128]
                        for sp in range(2):
                            o = pB[:, 1024 + sp * 512:1024 + (sp + 1) * 512]
                            rhs = w_oT[:, db * D + half * 1024 + sp * 512:
                                       db * D + half * 1024 + (sp + 1) * 512]
                            if db == DB - 1 and sp == 1:
                                P.op("tensor", lambda t, o=o, a=lhs, b=rhs:
                                     t.matmul(o, a, b, start=False, stop=True).then_inc(sPE.h, 1))
                            else:
                                P.op("tensor", lambda t, o=o, a=lhs, b=rhs, st=(db == 0),
                                     fi=(db == DB - 1): t.matmul(o, a, b, start=st, stop=fi))

            for c in range(C):
                for h in range(H):
                    P.wait("scalar", sDV, att_dv[c][h])
                    if c >= 2 and h == 0:
                        P.wait("scalar", sPE, wo_done[c - 2][1])
                    P.op("scalar", lambda e, c=c, h=h: e.dma_start_transpose(
                        afm[c % 2][:, h * 128:(h + 1) * 128],
                        att_tm[c % 2][:, h * 128:(h + 1) * 128]).then_inc(satp[c % 2].h, 16))

            for c in range(C):
                P.wait("sync", sDV, fin_dv[c][1])
                P.op("sync", lambda e, c=c: e.dma_start(
                    out=out[c * 128:(c + 1) * 128, :], in_=out_sb[c % 2][:]).then_inc(sst.h, 16))
            if "att0" in dbg:
                P.wait("sync", satp[0], att_tp[0])
                P.op("sync", lambda e: e.dma_start(out=dbg["att0"], in_=afm[0][:])
                     .then_inc(sst.h, 16))
                sst.inc(16)
            P.emit()
    return nc


def _mk_inputs(inputs):
    f = np.float32
    x = np.asarray(inputs["L0_attn_hidden_states"], f).reshape(T * NCORE, D)
    wq = np.asarray(inputs["L0_attn_w_q"], f)
    wk = np.asarray(inputs["L0_attn_w_k"], f)
    wv = np.asarray(inputs["L0_attn_w_v"], f)
    wo = np.asarray(inputs["L0_attn_w_o"], f)
    mem = np.asarray(inputs["L0_attn_memories"], f)
    mn = np.asarray(inputs["L0_attn_memory_norms"], f)
    gate = np.asarray(inputs["L0_attn_gate"], f)
    gw = np.asarray(inputs["L0_ffn_gate_w"], np.int32)
    gs = np.asarray(inputs["L0_ffn_gate_scale"], f)
    uw = np.asarray(inputs["L0_ffn_up_w"], np.int32)
    us = np.asarray(inputs["L0_ffn_up_scale"], f)
    dw = np.asarray(inputs["L0_ffn_down_w"], np.int32)
    ds = np.asarray(inputs["L0_ffn_down_scale"], f)

    com = {
        "wqT": np.ascontiguousarray(wq.T.reshape(DB, 128, H, 128).transpose(2, 0, 1, 3)),
        "wkT": np.ascontiguousarray(wk.T.reshape(DB, 128, H, 128).transpose(2, 0, 1, 3)),
        "wvT": np.ascontiguousarray(wv.T.reshape(DB, 128, D)),
        "woT": np.ascontiguousarray(wo.T.reshape(DB, 128, D)),
        "gw": np.ascontiguousarray(gw.T.reshape(HB, 128, IB, 128).transpose(2, 0, 1, 3)),
        "uw": np.ascontiguousarray(uw.T.reshape(HB, 128, IB, 128).transpose(2, 0, 1, 3)),
        "dw": np.ascontiguousarray(dw.T.reshape(IB, 128, HB, 128).transpose(2, 0, 1, 3)),
        "gsT": np.ascontiguousarray(gs.reshape(IB, 128).T),
        "usT": np.ascontiguousarray(us.reshape(IB, 128).T),
        "dsT": np.ascontiguousarray(ds.reshape(HB, 128).T),
        "mem": np.ascontiguousarray(mem),
        "mnTh": np.ascontiguousarray(mn.transpose(2, 1, 0).reshape(128, H * M)),
        "mnrows": np.ascontiguousarray(mn.reshape(M * H, 128)),
        "gate": np.ascontiguousarray(gate.reshape(1, H)),
        "tril": np.triu(np.ones((128, 128))).astype(bf16),
        "i_epsc": np.full((128, 1), 1e-5, f),
        "i_sent": np.arange(700, 708, dtype=f)[None, :] + np.zeros((128, 1), f),
        "i_onesb": np.ones((128, 1), bf16),
        "i_onesr": np.ones((1, 128), f),
        "msel": np.kron(np.eye(M, dtype=f), np.ones((H, 1), f)),
        "i_ident": np.eye(128).astype(bf16),
    }
    in_maps = []
    for r in range(NCORE):
        m = dict(com)
        m["x"] = np.ascontiguousarray(x[r * T:(r + 1) * T])
        rm = np.zeros((1, 8), f)
        rm[0, :r] = 1.0
        m["rmask"] = rm
        in_maps.append(m)
    return in_maps


def kernel(**inputs):
    in_maps = _mk_inputs(inputs)
    nc = build()
    res = run_bass_kernel_spmd(nc, in_maps, core_ids=list(range(NCORE)))
    outs = [np.asarray(res.results[r]["out"]) for r in range(NCORE)]
    return np.concatenate(outs, axis=0).reshape(1, T * NCORE, D).astype(np.float32)


# version 07
# speedup vs baseline: 1.0731x; 1.0026x over previous
import numpy as np
import ml_dtypes
import concourse.bass as bass
import concourse.mybir as mybir
from concourse.bass_utils import run_bass_kernel_spmd
from contextlib import ExitStack

F32 = mybir.dt.float32
BF16 = mybir.dt.bfloat16
I32 = mybir.dt.int32
AF = mybir.ActivationFunctionType
ALU = mybir.AluOpType
AX = mybir.AxisListType

NCORE = 8
T = 2048
C = 16
H = 16
HD = 128
D = 2048
DB = 16
II = 8192
IB = 64
HB = 16
M = 4
SCALE = HD ** -0.5
TTN = 512
NTT = T // TTN
NSP = TTN // 512
NBW = 4
NWB = IB // NBW
CCR = H * 128 + 16

bf16 = ml_dtypes.bfloat16
ENGS = ("tensor", "vector", "scalar", "gpsimd", "sync")

def _dinc(eng, ins, sem, k=1):
    eng.drain().then_inc(sem, k)
    return ins




class Ctr:
    def __init__(self, handle, name):
        self.h = handle
        self.v = 0
        self.name = name

    def inc(self, k=1):
        self.v += k
        return self.v


class Phase:
    def __init__(self, block):
        self.block = block
        self.s = {k: [] for k in ENGS}

    def op(self, eng, fn):
        self.s[eng].append(fn)

    def wait(self, eng, ctr, val):
        if val > 0:
            self.s[eng].append(lambda e, h=ctr.h, v=val: e.wait_ge(h, v))

    def emit(self):
        for name in ENGS:
            ops = self.s[name]
            if not ops:
                continue

            def sec(e, ops=ops):
                for f in ops:
                    f(e)
            getattr(self.block, name)(sec)
        self.s = {k: [] for k in ENGS}


def build(dbg_names=(), upto=6, no_cc=False):
    nc = bass.Bass("TRN2", target_bir_lowering=False, detect_race_conditions=False)
    ap = {}

    def din(name, shape, dt):
        ap[name] = nc.dram_tensor(name, shape, dt, kind="ExternalInput").ap()

    din("x", [T, D], F32)
    din("wqT", [H, DB, 128, 128], F32)
    din("wkT", [H, DB, 128, 128], F32)
    din("wvT", [DB, 128, D], F32)
    din("woT", [DB, 128, D], F32)
    din("gw", [IB, HB, 128, 128], I32)
    din("uw", [IB, HB, 128, 128], I32)
    din("dw", [HB, IB, 128, 128], I32)
    din("gsT", [128, IB], F32)
    din("usT", [128, IB], F32)
    din("dsT", [128, HB], F32)
    din("mem", [M, H, 128, 128], F32)
    din("mnTh", [128, H * M], F32)
    din("mnrows", [M * H, 128], F32)
    din("gate", [1, H], F32)
    din("tril", [128, 128], BF16)
    din("i_epsc", [128, 1], F32)
    din("i_sent", [128, 8], F32)
    din("i_onesb", [128, 1], BF16)
    din("i_onesr", [1, 128], F32)
    din("rmask", [1, 8], F32)
    din("msel", [M * H, M], F32)
    din("i_ident", [128, 128], BF16)
    out = nc.dram_tensor("out", [T, D], F32, kind="ExternalOutput").ap()

    st_xn = nc.dram_tensor("st_xn", [DB, 128, T], BF16, kind="Internal").ap()
    st_qp = nc.dram_tensor("st_qp", [H, 128, T], BF16, kind="Internal").ap()
    st_intra = nc.dram_tensor("st_intra", [H, 128, C * 129], BF16, kind="Internal").ap()
    st_sloc = nc.dram_tensor("st_sloc", [H, 128, C * 129], BF16, kind="Internal").ap()
    st_gf = nc.dram_tensor("st_gf", [C, 128, D], BF16, kind="Internal").ap()
    scr_gw = nc.dram_tensor("scr_gw", [IB, 128, HB * 128], BF16, kind="Internal").ap()
    scr_uw = nc.dram_tensor("scr_uw", [IB, 128, HB * 128], BF16, kind="Internal").ap()
    scr_dw = nc.dram_tensor("scr_dw", [HB, 128, IB * 128], BF16, kind="Internal").ap()
    cc_in = nc.dram_tensor("cc_in", [CCR, 129], F32, kind="Internal").ap()
    cc_out = nc.dram_tensor("cc_out", [NCORE * CCR, 129], F32, kind="Internal").ap()
    relbuf = nc.dram_tensor("relbuf", [1, H * M], F32, kind="Internal").ap()
    wbuf = nc.dram_tensor("wbuf", [H, M], F32, kind="Internal").ap()
    dbg = {}
    _dbg_shapes = {
        "stqp": ([H, 128, T], BF16), "stintra": ([H, 128, C * 129], BF16),
        "stsloc": ([H, 128, C * 129], BF16), "stgf": ([C, 128, D], BF16),
        "stxn": ([DB, 128, T], BF16), "srun": ([128, H * 129], F32),
        "relp": ([1, H * M], F32), "sinit": ([128, H * 129], F32),
        "wb": ([128, H * M], F32), "gg": ([128, 3 * H], F32),
        "att0": ([128, D], BF16), "ccout": ([NCORE * CCR, 129], F32),
        "xnfm": ([128, DB * T], BF16),
        "stats": ([128, 8 * C], F32),
        "hbuf0": ([128, IB * TTN], BF16),
        "xnsl0": ([128, HB * TTN], BF16),
    }
    for name in dbg_names:
        sh, dt = _dbg_shapes[name]
        dbg[name] = nc.dram_tensor("dbg_" + name, sh, dt, kind="ExternalOutput").ap()

    with ExitStack() as Gs:
        def sbuf(name, shape, dt, stack):
            return stack.enter_context(nc.sbuf_tensor(name, shape, dt))

        s_run = sbuf("s_run", [128, H * 129], F32, Gs)
        qsum = sbuf("qsum", [128, H], F32, Gs)
        mnT_f = sbuf("mnT_f", [128, M * H], F32, Gs)
        mnT_b = sbuf("mnT_b", [128, M * H], BF16, Gs)
        trilb = sbuf("trilb", [128, 128], BF16, Gs)
        identb = sbuf("identb", [128, 128], BF16, Gs)
        onesb = sbuf("onesb", [128, 1], BF16, Gs)
        onesr = sbuf("onesr", [1, 128], F32, Gs)
        rmk = sbuf("rmk", [1, 8], F32, Gs)
        epsc = sbuf("epsc", [128, 1], F32, Gs)
        dvflush = sbuf("dvflush", [128, 1], F32, Gs)
        attS = Gs.enter_context(ExitStack())
        xn_fm = sbuf("xn_fm", [128, DB * T], BF16, attS)
        v_tm = sbuf("v_tm", [128, C * D], BF16, attS)

        pA = Gs.enter_context(nc.psum_tensor("pA", [128, 2048], F32))
        pB = Gs.enter_context(nc.psum_tensor("pB", [128, 2048], F32))

        _sems = {}

        def S(name):
            if name not in _sems:
                _sems[name] = Ctr(Gs.enter_context(nc.semaphore(name)), name)
            return _sems[name]

        sPE, sDV, sAC, sCC, sB = S("sPE"), S("sDV"), S("sAC"), S("sCC"), S("sB")
        sGt, sst = S("sGt"), S("sst")
        sx = [S("sx0"), S("sx1")]
        stpS = [S("stpS0"), S("stpS1")]
        stpC = [S("stpC0"), S("stpC1")]
        swv = S("swv")
        swq = [S("swq0"), S("swq1")]
        swk = [S("swk0"), S("swk1")]
        skp = [S("skp0"), S("skp1")]
        sqst = [S("sqst0"), S("sqst1")]
        sist, ssst = S("sist"), S("ssst")
        sxn, ssc = S("sxn"), S("ssc")
        sgw = [S("sgw0"), S("sgw1")]
        suw = [S("suw0"), S("suw1")]
        sdw = [S("sdw0"), S("sdw1")]
        sscw = [S("sscw0"), S("sscw1")]
        sdww = [S("sdww0"), S("sdww1")]
        sgf = [S("sgf0"), S("sgf1")]
        satp = [S("satp0"), S("satp1")]
        sqp = [S("sqp0"), S("sqp1")]
        sin_ = [S("sin0"), S("sin1")]
        ssl = [S("ssl0"), S("ssl1")]
        sgfc = [S("sgfc0"), S("sgfc1")]
        srelg, smnr, sgt2, smsl = S("srelg"), S("smnr"), S("sgt2"), S("smsl")
        ssg = [S("ssg0"), S("ssg1")]
        srlt, swfl, srel, swbf = S("srlt"), S("swfl"), S("srel"), S("swbf")

        block = Gs.enter_context(nc.Block())
        P = Phase(block)

        def barrier():
            vals = [(s, s.v) for nm, s in sorted(_sems.items())
                    if nm not in ("sCC", "sB")]
            bv = sB.inc()
            for s, v in vals:
                if v:
                    P.wait("gpsimd", s, v)
            P.op("gpsimd", lambda g: g.engine_nop().then_inc(sB.h, 1))
            for eng in ("tensor", "vector", "scalar", "sync"):
                P.wait(eng, sB, bv)
            P.emit()

        def dbg_dump(name, src):
            if name in dbg:
                P.op("gpsimd", lambda g, o=dbg[name], i=src: g.dma_start(out=o, in_=i)
                     .then_inc(sGt.h, 16))
                sGt.inc(16)

        for dst, src in ((trilb[:], ap["tril"]), (identb[:], ap["i_ident"]),
                         (mnT_f[:], ap["mnTh"]),
                         (mnT_b[:], ap["mnTh"]), (rmk[:], ap["rmask"]),
                         (epsc[:], ap["i_epsc"]), (onesb[:], ap["i_onesb"]),
                         (onesr[:], ap["i_onesr"])):
            P.op("gpsimd", lambda g, d=dst, s=src: g.dma_start(out=d, in_=s).then_inc(sGt.h, 16))
            sGt.inc(16)

        def _init(v):
            v.memset(s_run[:], 0.0)
            v.memset(qsum[:], 0.0)
            v.tensor_copy(out=dvflush[:], in_=s_run[:, 0:1])
            v.drain().then_inc(sDV.h, 1)
        P.op("vector", _init)
        sDV.inc()
        P.emit()
        barrier()

        with ExitStack() as ph:
            x_sb = [sbuf(f"x_sb{i}", [128, D], F32, ph) for i in range(2)]
            xn_tm = [sbuf(f"xn_tm{i}", [128, D], BF16, ph) for i in range(2)]
            scrap = sbuf("scrap", [128, D], BF16, ph)
            xsumt = sbuf("xsumt", [128, 1], F32, ph)
            xsqt = sbuf("xsqt", [128, 1], F32, ph)
            meant = sbuf("meant", [128, 1], F32, ph)
            m2t = sbuf("m2t", [128, 1], F32, ph)
            vart = sbuf("vart", [128, 1], F32, ph)
            stdt = sbuf("stdt", [128, 1], F32, ph)
            rstdt = sbuf("rstdt", [128, 1], F32, ph)
            negmr = sbuf("negmr", [128, 1], F32, ph)
            flushc = sbuf("flushc", [128, 1], F32, ph)
            xcen = sbuf("xcen", [128, D], F32, ph)
            acflush = sbuf("acflush", [128, 1], F32, ph)
            statlog = sbuf("statlog", [128, 8 * C], F32, ph)

            if "stats" in dbg:
                for ii, tl in enumerate((xsumt, xsqt, meant, m2t, vart, stdt, rstdt, flushc)):
                    P.op("gpsimd", lambda g, ii=ii, tl=tl: g.dma_start(
                        out=tl[:], in_=ap["i_sent"][:, ii:ii + 1],
                        allow_slow_non_contiguous=True).then_inc(sGt.h, 16))
                    sGt.inc(16)
                sent_ld = sGt.v
                P.wait("vector", sGt, sent_ld)
                P.wait("scalar", sGt, sent_ld)
            xload = [sx[c % 2].inc(16) for c in range(C)]
            ln_red = [0] * C; ln_var = [0] * C; ln_rcp = [0] * C; ln_xn = [0] * C
            for c in range(C):
                ln_red[c] = sDV.inc(); ln_var[c] = sDV.inc()
                ln_rcp[c] = sDV.inc(); ln_xn[c] = sDV.inc()
            ln_sq = [0] * C; ln_sqrt = [0] * C
            xn_tpS = [0] * C; xn_tpC = [0] * C
            for c in range(C):
                ln_sq[c] = sAC.inc(); ln_sqrt[c] = sAC.inc()
                xn_tpC[c] = stpC[c % 2].inc(16 * 8)
            for c in range(C):
                xn_tpS[c] = stpS[c % 2].inc(16 * 8)
            xn_stash = [sst.inc(16) for _ in range(DB)]

            for c in range(C):
                if c >= 2:
                    P.wait("gpsimd", sDV, ln_xn[c - 2])
                P.op("gpsimd", lambda g, c=c: g.dma_start(
                    out=x_sb[c % 2][:], in_=ap["x"][c * 128:(c + 1) * 128, :]).then_inc(sx[c % 2].h, 16))

            for c in range(C):
                if c >= 1:
                    P.wait("vector", sAC, ln_sqrt[c - 1])
                P.wait("vector", sx[c % 2], xload[c])
                P.op("vector", lambda v, c=c: _dinc(v, v.reduce_sum(
                    out=xsumt[:], in_=x_sb[c % 2][:], axis=AX.X), sDV.h))
                P.wait("vector", sDV, ln_red[c])
                P.wait("vector", sAC, ln_sq[c])
                P.op("vector", lambda v: v.tensor_scalar_mul(out=meant[:], in0=xsumt[:], scalar1=1.0 / D))
                P.op("vector", lambda v: v.drain())
                P.op("vector", lambda v: v.tensor_mul(out=m2t[:], in0=meant[:], in1=meant[:]))
                P.op("vector", lambda v: v.drain())
                P.op("vector", lambda v: v.scalar_tensor_tensor(
                    out=vart[:], in0=xsqt[:], scalar=1.0 / D, in1=m2t[:],
                    op0=ALU.mult, op1=ALU.subtract))
                P.op("vector", lambda v: _dinc(v, v.tensor_copy(out=dvflush[:], in_=vart[:])
                     , sDV.h))
                P.wait("vector", sAC, ln_sqrt[c])
                P.op("vector", lambda v: _dinc(v, v.reciprocal(out=rstdt[:], in_=stdt[:])
                     , sDV.h))
                P.wait("vector", sDV, ln_rcp[c])
                if c >= 2:
                    P.wait("vector", stpS[c % 2], xn_tpS[c - 2])
                    P.wait("vector", stpC[c % 2], xn_tpC[c - 2])
                P.op("vector", lambda v, c=c: v.tensor_scalar(
                    out=xcen[:], in0=x_sb[c % 2][:], scalar1=meant[:],
                    scalar2=None, op0=ALU.subtract))
                P.op("vector", lambda v: v.drain())
                P.op("vector", lambda v, c=c: v.tensor_scalar(
                    out=xn_tm[c % 2][:], in0=xcen[:], scalar1=rstdt[:],
                    scalar2=None, op0=ALU.mult))
                P.op("vector", lambda v: v.drain())
                if "stats" in dbg:
                    for ii, tl in enumerate((xsumt, xsqt, meant, m2t, vart, stdt, rstdt)):
                        P.op("vector", lambda v, c=c, ii=ii, tl=tl: v.tensor_copy(
                            out=statlog[:, c * 8 + ii:c * 8 + ii + 1], in_=tl[:]))
                P.op("vector", lambda v, c=c: _dinc(v, v.reduce_sum(
                    out=flushc[:], in_=xn_tm[c % 2][:], axis=AX.X), sDV.h))

            for c in range(C):
                P.wait("scalar", sx[c % 2], xload[c])
                if c >= 1:
                    P.wait("scalar", sDV, ln_var[c - 1])
                P.op("scalar", lambda s, c=c: s.activation(
                    out=scrap[:], in_=x_sb[c % 2][:], func=AF.Square,
                    accum_out=xsqt[:]))
                P.op("scalar", lambda s: _dinc(s, s.copy(out=acflush[:], in_=xsqt[:]),
                                               sAC.h))
                P.wait("scalar", sDV, ln_var[c])
                if c >= 1:
                    P.wait("scalar", sDV, ln_rcp[c - 1])
                P.op("scalar", lambda s: s.activation(
                    out=stdt[:], in_=vart[:], func=AF.Sqrt, bias=epsc[:]))
                P.op("scalar", lambda s: _dinc(s, s.copy(out=acflush[:], in_=stdt[:]),
                                               sAC.h))


            for c in range(C):
                P.wait("sync", sDV, ln_xn[c])
                for db in range(8):
                    P.op("sync", lambda e, c=c, db=db: e.dma_start_transpose(
                        xn_fm[:, db * T + c * 128: db * T + (c + 1) * 128],
                        xn_tm[c % 2][:, db * 128:(db + 1) * 128]).then_inc(stpS[c % 2].h, 16))
                for db in range(8, 16):
                    P.op("sync", lambda e, c=c, db=db: e.dma_start_transpose(
                        xn_fm[:, db * T + c * 128: db * T + (c + 1) * 128],
                        xn_tm[c % 2][:, db * 128:(db + 1) * 128]).then_inc(stpC[c % 2].h, 16))
            P.wait("sync", stpC[0], xn_tpC[C - 2])
            P.wait("sync", stpC[1], xn_tpC[C - 1])
            P.wait("sync", stpS[0], xn_tpS[C - 2])
            P.wait("sync", stpS[1], xn_tpS[C - 1])
            for db in range(DB):
                P.op("sync", lambda e, db=db: e.dma_start(
                    out=st_xn[db], in_=xn_fm[:, db * T:(db + 1) * T]).then_inc(sst.h, 16))
            P.emit()
            barrier()
        dbg_dump("stxn", st_xn)
        if "stats" in dbg:
            P.op("sync", lambda e: e.dma_start(out=dbg["stats"], in_=statlog[:])
                 .then_inc(sst.h, 16))
            sst.inc(16)
        if "xnfm" in dbg:
            P.op("sync", lambda e: e.dma_start(out=dbg["xnfm"], in_=xn_fm[:])
                 .then_inc(sst.h, 16))
            sst.inc(16)
            barrier()
        P.emit()
        if upto <= 1:
            return nc

        with ExitStack() as ph:
            wv_b = sbuf("wv_b", [128, DB * 1024], BF16, ph)
            wvl = [[swv.inc(16) for _ in range(DB)] for _ in range(2)]
            pe_v = [sPE.inc() for _ in range(2 * C)]
            vev = [sDV.inc() for _ in range(2 * C)]

            P.wait("gpsimd", sDV, ln_xn[C - 1])
            P.wait("gpsimd", sAC, sAC.v)
            for hf in range(2):
                for db in range(DB):
                    if hf == 1 and db == 0:
                        P.wait("gpsimd", sPE, pe_v[C - 1])
                    P.op("gpsimd", lambda g, hf=hf, db=db: g.dma_start(
                        out=wv_b[:, db * 1024:(db + 1) * 1024],
                        in_=ap["wvT"][db][:, hf * 1024:(hf + 1) * 1024]).then_inc(swv.h, 16))

            for hf in range(2):
                for tc in range(C):
                    i = hf * C + tc
                    P.wait("tensor", swv, wvl[hf][DB - 1])
                    if i >= 2:
                        P.wait("tensor", sDV, vev[i - 2])
                    base = (tc % 2) * 1024
                    for db in range(DB):
                        lhs = xn_fm[:, db * T + tc * 128: db * T + (tc + 1) * 128]
                        for sp in range(2):
                            o = pA[:, base + sp * 512: base + (sp + 1) * 512]
                            rhs = wv_b[:, db * 1024 + sp * 512: db * 1024 + (sp + 1) * 512]
                            if db == DB - 1 and sp == 1:
                                P.op("tensor", lambda t, o=o, a=lhs, b=rhs:
                                     t.matmul(o, a, b, start=False, stop=True).then_inc(sPE.h, 1))
                            else:
                                P.op("tensor", lambda t, o=o, a=lhs, b=rhs, st=(db == 0),
                                     fi=(db == DB - 1): t.matmul(o, a, b, start=st, stop=fi))

            for hf in range(2):
                for tc in range(C):
                    i = hf * C + tc
                    P.wait("vector", sPE, pe_v[i])
                    base = (tc % 2) * 1024
                    P.op("vector", lambda v, tc=tc, hf=hf, base=base: _dinc(v, v.tensor_copy(
                        out=v_tm[:, tc * D + hf * 1024: tc * D + (hf + 1) * 1024],
                        in_=pA[:, base: base + 1024]), sDV.h))
            P.emit()
            v_done_dv = sDV.v
        if upto <= 2:
            barrier()
            return nc

        with ExitStack() as ph:
            wq_b = [sbuf(f"wq_b{i}", [128, D], BF16, ph) for i in range(2)]
            wk_b = [sbuf(f"wk_b{i}", [128, D], BF16, ph) for i in range(2)]
            q_sb = sbuf("q_sb", [128, T], BF16, ph)
            k_sb = sbuf("k_sb", [128, T], BF16, ph)
            qp_h = [sbuf(f"qp_h{i}", [128, T], BF16, ph) for i in range(2)]
            kp_fm = [sbuf(f"kp_fm{i}", [128, T], BF16, ph) for i in range(2)]
            kp_tm = [sbuf(f"kp_tm{i}", [128, T], BF16, ph) for i in range(2)]
            tmin = sbuf("tmin", [128, T], BF16, ph)
            texp = sbuf("texp", [128, T], BF16, ph)
            scm = [sbuf(f"scm{i}", [128, 128], BF16, ph) for i in range(2)]
            intra_stg = [sbuf("instg", [128, C * 129], BF16, ph)] * 2
            sloc_stg = [sbuf("slstg", [128, C * 129], BF16, ph)] * 2
            relp = sbuf("relp", [16, 129], F32, ph)

            wq_l = [0] * H; wk_l = [0] * H
            q_done = [0] * H; k_done = [0] * H
            sc_done = [[0] * C for _ in range(H)]
            kv_done = [[0] * C for _ in range(H)]
            num_done = [[0] * C for _ in range(H)]
            for h in range(H):
                wq_l[h] = swq[h % 2].inc(16)
                wk_l[h] = swk[h % 2].inc(16)
            for h in range(H):
                q_done[h] = sPE.inc()
                k_done[h] = sPE.inc()
                for c in range(C):
                    sc_done[h][c] = sPE.inc()
                    kv_done[h][c] = sPE.inc()
                    num_done[h][c] = sPE.inc()
            rel_done = sPE.inc()
            q_ev = [0] * H; texp_qv = [0] * H; k_ev = [0] * H; texp_kv = [0] * H
            for h in range(H):
                q_ev[h] = sAC.inc()
                texp_qv[h] = sAC.inc()
                k_ev[h] = sAC.inc()
                texp_kv[h] = sAC.inc()
            qp_dv = [0] * H; kp_dv = [0] * H
            mask_dv = [[0] * C for _ in range(H)]
            sadd_dv = [[0] * C for _ in range(H)]
            nev_dv = [[0] * C for _ in range(H)]
            for h in range(H):
                sDV.inc()
                qp_dv[h] = sDV.inc()
                sDV.inc()
                kp_dv[h] = sDV.inc()
                for c in range(C):
                    mask_dv[h][c] = sDV.inc()
                    sadd_dv[h][c] = sDV.inc()
                    nev_dv[h][c] = sDV.inc()
            relp_dv = sDV.inc()
            kp_tp = [skp[h % 2].inc(16 * C) for h in range(H)]
            cc_wr = sst.inc(16 * (H + 1))
            qp_st = [0] * H; intra_st = [0] * H; sloc_st = [0] * H
            for h in range(H):
                qp_st[h] = sqst[h % 2].inc(16)
                intra_st[h] = sist.inc(16)
                sloc_st[h] = ssst.inc(16)
            cc_done = sCC.inc()

            P.wait("gpsimd", sPE, pe_v[2 * C - 1])
            P.wait("gpsimd", stpS[1], xn_tpS[C - 1])
            P.wait("gpsimd", stpC[1], xn_tpC[C - 1])
            P.wait("gpsimd", sst, xn_stash[DB - 1])
            for h in range(H):
                if h >= 2:
                    P.wait("gpsimd", sPE, q_done[h - 2])
                P.op("gpsimd", lambda g, h=h: g.dma_start(
                    out=wq_b[h % 2].ap().rearrange("p (a b) -> p a b", a=DB),
                    in_=ap["wqT"][h].rearrange("a p b -> p a b")).then_inc(swq[h % 2].h, 16))
                if h >= 2:
                    P.wait("gpsimd", sPE, k_done[h - 2])
                P.op("gpsimd", lambda g, h=h: g.dma_start(
                    out=wk_b[h % 2].ap().rearrange("p (a b) -> p a b", a=DB),
                    in_=ap["wkT"][h].rearrange("a p b -> p a b")).then_inc(swk[h % 2].h, 16))
            P.wait("gpsimd", sst, cc_wr)
            if no_cc:
                P.op("gpsimd", lambda g: g.engine_nop().then_inc(sCC.h))
            else:
                P.op("gpsimd", lambda g: g.collective_compute(
                    "AllGather", ALU.bypass, replica_groups=[list(range(NCORE))],
                    ins=[cc_in.opt()], outs=[cc_out.opt()]).then_inc(sCC.h))

            for h in range(H):
                P.wait("tensor", swq[h % 2], wq_l[h])
                if h == 0:
                    P.wait("tensor", sDV, v_done_dv)
                if h >= 1:
                    P.wait("tensor", sDV, nev_dv[h - 1][C - 1])
                    P.wait("tensor", sDV, mask_dv[h - 1][C - 1])
                for db in range(DB):
                    lhs = wq_b[h % 2][:, db * 128:(db + 1) * 128]
                    for sp in range(4):
                        o = pA[:, sp * 512:(sp + 1) * 512]
                        rhs = xn_fm[:, db * T + sp * 512: db * T + (sp + 1) * 512]
                        if db == DB - 1 and sp == 3:
                            P.op("tensor", lambda t, o=o, a=lhs, b=rhs:
                                 t.matmul(o, a, b, start=False, stop=True).then_inc(sPE.h, 1))
                        else:
                            P.op("tensor", lambda t, o=o, a=lhs, b=rhs, st=(db == 0),
                                 fi=(db == DB - 1): t.matmul(o, a, b, start=st, stop=fi))
                P.wait("tensor", swk[h % 2], wk_l[h])
                if h >= 1:
                    P.wait("tensor", sDV, sadd_dv[h - 1][C - 1])
                for db in range(DB):
                    lhs = wk_b[h % 2][:, db * 128:(db + 1) * 128]
                    for sp in range(4):
                        o = pB[:, sp * 512:(sp + 1) * 512]
                        rhs = xn_fm[:, db * T + sp * 512: db * T + (sp + 1) * 512]
                        if db == DB - 1 and sp == 3:
                            P.op("tensor", lambda t, o=o, a=lhs, b=rhs:
                                 t.matmul(o, a, b, start=False, stop=True).then_inc(sPE.h, 1))
                        else:
                            P.op("tensor", lambda t, o=o, a=lhs, b=rhs, st=(db == 0),
                                 fi=(db == DB - 1): t.matmul(o, a, b, start=st, stop=fi))
                P.wait("tensor", sDV, qp_dv[h])
                P.wait("tensor", sDV, kp_dv[h])
                P.wait("tensor", skp[h % 2], kp_tp[h])
                P.wait("tensor", sAC, q_ev[h])
                P.wait("tensor", sAC, k_ev[h])
                for c in range(C):
                    cs = slice(c * 128, (c + 1) * 128)
                    vt = v_tm[:, c * D + h * 128: c * D + (h + 1) * 128]
                    if c >= 2:
                        P.wait("tensor", sDV, mask_dv[h][c - 2])
                    P.op("tensor", lambda t, o=pA[:, (c % 2) * 512:(c % 2) * 512 + 128],
                         a=kp_fm[h % 2][:, cs], b=qp_h[h % 2][:, cs]:
                         t.matmul(o, a, b, start=True, stop=True).then_inc(sPE.h, 1))
                    if c >= 2:
                        P.wait("tensor", sDV, sadd_dv[h][c - 2])
                    P.op("tensor", lambda t, o=pB[:, (c % 2) * 512:(c % 2) * 512 + 128],
                         a=kp_tm[h % 2][:, cs], b=vt:
                         t.matmul(o, a, b, start=True, stop=True))
                    P.op("tensor", lambda t, o=pB[:, (c % 2) * 512 + 128:(c % 2) * 512 + 129],
                         a=kp_tm[h % 2][:, cs]:
                         t.matmul(o, a, onesb[:], start=True, stop=True).then_inc(sPE.h, 1))
                    P.wait("tensor", sDV, mask_dv[h][c])
                    if c >= 2:
                        P.wait("tensor", sDV, nev_dv[h][c - 2])
                    P.op("tensor", lambda t, o=pA[:, (2 + c % 2) * 512:(2 + c % 2) * 512 + 128],
                         a=scm[c % 2][:], b=vt:
                         t.matmul(o, a, b, start=True, stop=True))
                    P.op("tensor", lambda t, o=pA[:, (2 + c % 2) * 512 + 128:(2 + c % 2) * 512 + 129],
                         a=scm[c % 2][:]:
                         t.matmul(o, a, onesb[:], start=True, stop=True).then_inc(sPE.h, 1))
            P.wait("tensor", sAC, q_ev[H - 1])
            P.wait("tensor", sDV, sadd_dv[H - 1][C - 1])
            P.wait("tensor", sDV, nev_dv[H - 1][C - 1])
            P.wait("tensor", sDV, mask_dv[H - 1][C - 1])
            for h in range(H):
                o = pA[0:1, h * 4:(h + 1) * 4]
                rhs = mnT_f[:, h * 4:(h + 1) * 4]
                if h == H - 1:
                    P.op("tensor", lambda t, o=o, a=qsum[:, h:h + 1], b=rhs:
                         t.matmul(o, a, b, start=True, stop=True).then_inc(sPE.h, 1))
                else:
                    P.op("tensor", lambda t, o=o, a=qsum[:, h:h + 1], b=rhs:
                         t.matmul(o, a, b, start=True, stop=True))

            for h in range(H):
                P.wait("scalar", sPE, q_done[h])
                if h >= 1:
                    P.wait("scalar", sDV, qp_dv[h - 1])
                P.op("scalar", lambda s, h=h: _dinc(s, s.activation(
                    out=q_sb[:], in_=pA[:, 0:2048], func=AF.Copy,
                    accum_out=qsum[:, h:h + 1]), sAC.h))
                P.wait("scalar", sDV, qp_dv[h] - 1)
                P.op("scalar", lambda s: _dinc(s, s.activation(
                    out=texp[:], in_=tmin[:], func=AF.Exp), sAC.h))
                P.wait("scalar", sPE, k_done[h])
                if h >= 1:
                    P.wait("scalar", sDV, kp_dv[h - 1])
                P.op("scalar", lambda s, h=h: _dinc(s, s.activation(
                    out=k_sb[:], in_=pB[:, 0:2048], func=AF.Copy), sAC.h))
                P.wait("scalar", sDV, kp_dv[h] - 1)
                P.op("scalar", lambda s: _dinc(s, s.activation(
                    out=texp[:], in_=tmin[:], func=AF.Exp), sAC.h))
                P.wait("scalar", sDV, qp_dv[h])
                P.op("scalar", lambda e, h=h: e.dma_start(
                    out=st_qp[h], in_=qp_h[h % 2][:]).then_inc(sqst[h % 2].h, 16))
                P.wait("scalar", sDV, nev_dv[h][C - 1])
                P.op("scalar", lambda e, h=h: e.dma_start(
                    out=st_intra[h], in_=intra_stg[h % 2][:]).then_inc(sist.h, 16))
                P.wait("scalar", sDV, sadd_dv[h][C - 1])
                P.op("scalar", lambda e, h=h: e.dma_start(
                    out=st_sloc[h], in_=sloc_stg[h % 2][:]).then_inc(ssst.h, 16))

            P.op("vector", lambda v: v.memset(relp[:], 0.0))
            for h in range(H):
                P.wait("vector", sAC, q_ev[h])
                P.op("vector", lambda v: _dinc(v, v.tensor_scalar_min(
                    out=tmin[:], in0=q_sb[:], scalar1=0.0), sDV.h))
                P.wait("vector", sAC, texp_qv[h])
                if h >= 2:
                    P.wait("vector", sqst[h % 2], qp_st[h - 2])
                    P.wait("vector", sPE, sc_done[h - 2][C - 1])
                    P.wait("vector", sPE, num_done[h - 2][C - 1])
                P.op("vector", lambda v, h=h: _dinc(v, v.scalar_tensor_tensor(
                    out=qp_h[h % 2][:], in0=q_sb[:], scalar=0.0, in1=texp[:],
                    op0=ALU.max, op1=ALU.add), sDV.h))
                P.wait("vector", sAC, k_ev[h])
                P.op("vector", lambda v: _dinc(v, v.tensor_scalar_min(
                    out=tmin[:], in0=k_sb[:], scalar1=0.0), sDV.h))
                P.wait("vector", sAC, texp_kv[h])
                if h >= 2:
                    P.wait("vector", skp[h % 2], kp_tp[h - 2])
                P.op("vector", lambda v, h=h: _dinc(v, v.scalar_tensor_tensor(
                    out=kp_fm[h % 2][:], in0=k_sb[:], scalar=0.0, in1=texp[:],
                    op0=ALU.max, op1=ALU.add), sDV.h))
                if h >= 1:
                    P.wait("vector", sist, intra_st[h - 1])
                    P.wait("vector", ssst, sloc_st[h - 1])
                for c in range(C):
                    P.wait("vector", sPE, sc_done[h][c])
                    if c >= 2:
                        P.wait("vector", sPE, num_done[h][c - 2])
                    elif h >= 1:
                        P.wait("vector", sPE, num_done[h - 1][C - 2 + c])
                    P.op("vector", lambda v, c=c: _dinc(v, v.tensor_tensor(
                        out=scm[c % 2][:], in0=pA[:, (c % 2) * 512:(c % 2) * 512 + 128],
                        in1=trilb[:], op=ALU.mult), sDV.h))
                    P.op("vector", lambda v, h=h, c=c: v.tensor_copy(
                        out=sloc_stg[h % 2][:, c * 129:(c + 1) * 129],
                        in_=s_run[:, h * 129:(h + 1) * 129]))
                    P.op("vector", lambda v: v.drain())
                    P.wait("vector", sPE, kv_done[h][c])
                    P.op("vector", lambda v, h=h, c=c: _dinc(v, v.tensor_add(
                        out=s_run[:, h * 129:(h + 1) * 129],
                        in0=s_run[:, h * 129:(h + 1) * 129],
                        in1=pB[:, (c % 2) * 512:(c % 2) * 512 + 129]), sDV.h))
                    P.wait("vector", sPE, num_done[h][c])
                    P.op("vector", lambda v, h=h, c=c: _dinc(v, v.tensor_copy(
                        out=intra_stg[h % 2][:, c * 129:(c + 1) * 129],
                        in_=pA[:, (2 + c % 2) * 512:(2 + c % 2) * 512 + 129]), sDV.h))
            P.wait("vector", sPE, rel_done)
            P.op("vector", lambda v: _dinc(v, v.tensor_scalar_mul(
                out=relp[0:1, 0:H * M], in0=pA[0:1, 0:H * M],
                scalar1=SCALE / (T * NCORE)), sDV.h))

            for h in range(H):
                P.wait("sync", sDV, kp_dv[h])
                if h >= 2:
                    P.wait("sync", sPE, kv_done[h - 2][C - 1])
                for c in range(C):
                    P.op("sync", lambda e, h=h, c=c: e.dma_start_transpose(
                        kp_tm[h % 2][:, c * 128:(c + 1) * 128],
                        kp_fm[h % 2][:, c * 128:(c + 1) * 128]).then_inc(skp[h % 2].h, 16))
            for h in range(H):
                P.wait("sync", sDV, sadd_dv[h][C - 1])
                P.op("sync", lambda e, h=h: e.dma_start(
                    out=cc_in[h * 128:(h + 1) * 128, :],
                    in_=s_run[:, h * 129:(h + 1) * 129]).then_inc(sst.h, 16))
            P.wait("sync", sDV, relp_dv)
            P.op("sync", lambda e: e.dma_start(
                out=cc_in[H * 128:H * 128 + 16, :], in_=relp[:, 0:129]).then_inc(sst.h, 16))
            P.emit()
            barrier()
        dbg_dump("stqp", st_qp)
        dbg_dump("stintra", st_intra)
        dbg_dump("stsloc", st_sloc)
        dbg_dump("srun", s_run[:])
        P.emit()
        attS.close()
        if upto <= 3:
            return nc

        with ExitStack() as ph:
            h_buf = sbuf("h_buf", [128, IB * TTN], BF16, ph)
            xnsl = sbuf("xnsl", [128, HB * TTN], BF16, ph)
            gw_q = [sbuf(f"gw_q{i}", [128, NBW * HB * 128], BF16, ph) for i in range(2)]
            uw_q = [sbuf(f"uw_q{i}", [128, NBW * HB * 128], BF16, ph) for i in range(2)]
            dw_q = [sbuf(f"dw_q{i}", [128, IB * 128], BF16, ph) for i in range(2)]
            g_sb = [sbuf(f"g_sb{i}", [128, TTN], BF16, ph) for i in range(2)]
            u_sb = [sbuf(f"u_sb{i}", [128, TTN], BF16, ph) for i in range(2)]
            gfst = [sbuf(f"gfst{i}", [128, 512], BF16, ph) for i in range(2)]
            gftst = [sbuf(f"gftst{i}", [128, 512], BF16, ph) for i in range(2)]
            gst = sbuf("gst", [128, IB], F32, ph)
            ust = sbuf("ust", [128, IB], F32, ph)
            dsc = sbuf("dsc", [128, HB], F32, ph)

            for dst, src in ((gst[:], ap["gsT"]), (ust[:], ap["usT"]), (dsc[:], ap["dsT"])):
                P.op("gpsimd", lambda g, d=dst, s=src: g.dma_start(out=d, in_=s).then_inc(ssc.h, 16))
            ld_scales = ssc.inc(48)

            prev = None
            for tt in range(NTT):
                xnl = [sxn.inc(16) for _ in range(HB)]
                gwl = [0] * NWB; uwl = [0] * NWB
                for b in range(NWB):
                    gwl[b] = sgw[b % 2].inc(16)
                    uwl[b] = suw[b % 2].inc(16)
                dwl = [sdw[hb % 2].inc(16) for hb in range(HB)]
                g_done = [0] * IB; u_done = [0] * IB
                for ib in range(IB):
                    g_done[ib] = sPE.inc()
                    u_done[ib] = sPE.inc()
                dn_done = [[sPE.inc() for _ in range(NSP)] for _ in range(HB)]
                gwsh = [0] * NWB; uwsh = [0] * NWB
                uev = [0] * IB; h_done = [0] * IB
                for ib in range(IB):
                    if tt == 0 and ib % NBW == 0:
                        gwsh[ib // NBW] = sDV.inc()
                        uwsh[ib // NBW] = sDV.inc()
                    uev[ib] = sDV.inc()
                    h_done[ib] = sDV.inc()
                dwsh = [sDV.inc() if tt == 0 else 0 for _ in range(HB)]
                g_ev = [sAC.inc() for _ in range(IB)]
                scw_g = [0] * NWB; scw_u = [0] * NWB
                if tt == 0:
                    for b in range(NWB):
                        scw_g[b] = sscw[b % 2].inc(16)
                        scw_u[b] = sscw[b % 2].inc(16)
                gf_ev = [[0] * NSP for _ in range(HB)]
                scw_d = [0] * HB
                for hb in range(HB):
                    if tt == 0:
                        scw_d[hb] = sdww[hb % 2].inc(16)
                    for ts in range(NSP):
                        gf_ev[hb][ts] = sAC.inc()
                gf_tpa = [[0] * NSP for _ in range(HB)]
                gf_tp = [[0] * NSP for _ in range(HB)]
                for hb in range(HB):
                    for ts in range(NSP):
                        par = (hb * NSP + ts) % 2
                        gf_tpa[hb][ts] = sgf[par].inc(16 * 4)
                        gf_tp[hb][ts] = sgf[par].inc(16 * 4)

                if prev is not None:
                    P.wait("gpsimd", sPE, prev["u_done"][IB - 1])
                for hb in range(HB):
                    P.op("gpsimd", lambda g, hb=hb, tt=tt: g.dma_start(
                        out=xnsl[:, hb * TTN:(hb + 1) * TTN],
                        in_=st_xn[hb][:, tt * TTN:(tt + 1) * TTN]).then_inc(sxn.h, 16))
                for b in range(NWB):
                    if b >= 2:
                        P.wait("gpsimd", sPE, g_done[(b - 2) * NBW + NBW - 1])
                        if tt == 0:
                            P.wait("gpsimd", sscw[b % 2], scw_g[b - 2])
                    elif prev is not None:
                        P.wait("gpsimd", sPE, prev["g_done"][IB - 1])
                        if tt == 1:
                            P.wait("gpsimd", sscw[b % 2], prev["scw_g"][NWB - 2 + b])
                    if tt == 0:
                        P.op("gpsimd", lambda g, b=b: g.dma_start(
                            out=gw_q[b % 2].ap().rearrange("p (n a b) -> p n a b", n=NBW, a=HB),
                            in_=ap["gw"][b * NBW:(b + 1) * NBW].rearrange("n a p b -> p n a b"))
                            .then_inc(sgw[b % 2].h, 16))
                    else:
                        P.op("gpsimd", lambda g, b=b: g.dma_start(
                            out=gw_q[b % 2].ap().rearrange("p (n f) -> p n f", n=NBW),
                            in_=scr_gw[b * NBW:(b + 1) * NBW].rearrange("n p f -> p n f"))
                            .then_inc(sgw[b % 2].h, 16))
                    if b >= 2:
                        P.wait("gpsimd", sPE, u_done[(b - 2) * NBW + NBW - 1])
                        if tt == 0:
                            P.wait("gpsimd", sscw[b % 2], scw_u[b - 2])
                    elif prev is not None:
                        P.wait("gpsimd", sPE, prev["u_done"][IB - 1])
                        if tt == 1:
                            P.wait("gpsimd", sscw[b % 2], prev["scw_u"][NWB - 2 + b])
                    if tt == 0:
                        P.op("gpsimd", lambda g, b=b: g.dma_start(
                            out=uw_q[b % 2].ap().rearrange("p (n a b) -> p n a b", n=NBW, a=HB),
                            in_=ap["uw"][b * NBW:(b + 1) * NBW].rearrange("n a p b -> p n a b"))
                            .then_inc(suw[b % 2].h, 16))
                    else:
                        P.op("gpsimd", lambda g, b=b: g.dma_start(
                            out=uw_q[b % 2].ap().rearrange("p (n f) -> p n f", n=NBW),
                            in_=scr_uw[b * NBW:(b + 1) * NBW].rearrange("n p f -> p n f"))
                            .then_inc(suw[b % 2].h, 16))
                for hb in range(HB):
                    if hb >= 2:
                        P.wait("gpsimd", sPE, dn_done[hb - 2][NSP - 1])
                        if tt == 0:
                            P.wait("gpsimd", sdww[hb % 2], scw_d[hb - 2])
                    elif prev is not None:
                        P.wait("gpsimd", sPE, prev["dn_done"][HB - 1][NSP - 1])
                        if tt == 1:
                            P.wait("gpsimd", sdww[hb % 2], prev["scw_d"][HB - 2 + hb])
                    if tt == 0:
                        P.op("gpsimd", lambda g, hb=hb: g.dma_start(
                            out=dw_q[hb % 2].ap().rearrange("p (a b) -> p a b", a=IB),
                            in_=ap["dw"][hb].rearrange("a p b -> p a b")).then_inc(sdw[hb % 2].h, 16))
                    else:
                        P.op("gpsimd", lambda g, hb=hb: g.dma_start(
                            out=dw_q[hb % 2][:], in_=scr_dw[hb]).then_inc(sdw[hb % 2].h, 16))

                for ib in range(IB):
                    if tt == 0 and ib % NBW == 0:
                        b = ib // NBW
                        P.wait("vector", sgw[b % 2], gwl[b])
                        P.op("vector", lambda v, b=b: _dinc(v, v.tensor_scalar_add(
                            out=gw_q[b % 2][:], in0=gw_q[b % 2][:], scalar1=-128.0)
                            , sDV.h))
                        P.wait("vector", suw[b % 2], uwl[b])
                        P.op("vector", lambda v, b=b: _dinc(v, v.tensor_scalar_add(
                            out=uw_q[b % 2][:], in0=uw_q[b % 2][:], scalar1=-128.0)
                            , sDV.h))
                    if ib == 0 and prev is not None:
                        P.wait("vector", sPE, prev["dn_done"][HB - 1][NSP - 1])
                    P.wait("vector", sPE, u_done[ib])
                    ubase = TTN * (ib % 2)
                    P.op("vector", lambda v, ib=ib, ubase=ubase: _dinc(v, v.tensor_scalar_mul(
                        out=u_sb[ib % 2][:], in0=pB[:, ubase:ubase + TTN],
                        scalar1=ust[:, ib:ib + 1]), sDV.h))
                    P.wait("vector", sAC, g_ev[ib])
                    P.op("vector", lambda v, ib=ib: _dinc(v, v.tensor_mul(
                        out=h_buf[:, ib * TTN:(ib + 1) * TTN],
                        in0=g_sb[ib % 2][:], in1=u_sb[ib % 2][:]), sDV.h))
                if tt == 0:
                    for hb in range(HB):
                        P.wait("vector", sdw[hb % 2], dwl[hb])
                        P.op("vector", lambda v, hb=hb: _dinc(v, v.tensor_scalar_add(
                            out=dw_q[hb % 2][:], in0=dw_q[hb % 2][:], scalar1=-128.0)
                            , sDV.h))

                for ib in range(IB):
                    gbase = TTN * (ib % 2)
                    bi, bo = ib // NBW, ib % NBW
                    if ib == 0:
                        P.wait("tensor", sxn, xnl[HB - 1])
                        P.wait("tensor", ssc, ld_scales)
                        if prev is not None:
                            P.wait("tensor", sAC, prev["gf_ev"][HB - 1][NSP - 1])
                            P.wait("tensor", sDV, prev["uev"][IB - 1])
                    if bo == 0:
                        if tt == 0:
                            P.wait("tensor", sDV, gwsh[bi])
                        else:
                            P.wait("tensor", sgw[bi % 2], gwl[bi])
                    if ib >= 2:
                        P.wait("tensor", sAC, g_ev[ib - 2])
                    for hb in range(HB):
                        lhs = gw_q[bi % 2][:, (bo * HB + hb) * 128:(bo * HB + hb + 1) * 128]
                        for sp in range(NSP):
                            o = pA[:, gbase + sp * 512: gbase + (sp + 1) * 512]
                            rhs = xnsl[:, hb * TTN + sp * 512: hb * TTN + (sp + 1) * 512]
                            if hb == HB - 1 and sp == NSP - 1:
                                P.op("tensor", lambda t, o=o, a=lhs, b=rhs:
                                     t.matmul(o, a, b, start=False, stop=True).then_inc(sPE.h, 1))
                            else:
                                P.op("tensor", lambda t, o=o, a=lhs, b=rhs, st=(hb == 0),
                                     fi=(hb == HB - 1): t.matmul(o, a, b, start=st, stop=fi))
                    if bo == 0:
                        if tt == 0:
                            P.wait("tensor", sDV, uwsh[bi])
                        else:
                            P.wait("tensor", suw[bi % 2], uwl[bi])
                    if ib >= 2:
                        P.wait("tensor", sDV, uev[ib - 2])
                    for hb in range(HB):
                        lhs = uw_q[bi % 2][:, (bo * HB + hb) * 128:(bo * HB + hb + 1) * 128]
                        for sp in range(NSP):
                            o = pB[:, gbase + sp * 512: gbase + (sp + 1) * 512]
                            rhs = xnsl[:, hb * TTN + sp * 512: hb * TTN + (sp + 1) * 512]
                            if hb == HB - 1 and sp == NSP - 1:
                                P.op("tensor", lambda t, o=o, a=lhs, b=rhs:
                                     t.matmul(o, a, b, start=False, stop=True).then_inc(sPE.h, 1))
                            else:
                                P.op("tensor", lambda t, o=o, a=lhs, b=rhs, st=(hb == 0),
                                     fi=(hb == HB - 1): t.matmul(o, a, b, start=st, stop=fi))
                if tt == 0 and "hbuf0" in dbg:
                    P.wait("sync", sDV, h_done[IB - 1])
                    P.op("sync", lambda e: e.dma_start(out=dbg["hbuf0"], in_=h_buf[:])
                         .then_inc(sst.h, 16))
                    sst.inc(16)
                if tt == 0 and "xnsl0" in dbg:
                    P.wait("sync", sxn, xnl[HB - 1])
                    P.op("sync", lambda e: e.dma_start(out=dbg["xnsl0"], in_=xnsl[:])
                         .then_inc(sst.h, 16))
                    sst.inc(16)
                P.wait("tensor", sDV, h_done[IB - 1])
                P.wait("tensor", sAC, g_ev[IB - 1])
                for hb in range(HB):
                    for ts in range(NSP):
                        idx = hb * NSP + ts
                        par = idx % 2
                        if ts == 0:
                            if tt == 0:
                                P.wait("tensor", sDV, dwsh[hb])
                            else:
                                P.wait("tensor", sdw[hb % 2], dwl[hb])
                        if idx >= 2:
                            P.wait("tensor", sAC, gf_ev[(idx - 2) // NSP][(idx - 2) % NSP])
                        for ib in range(IB):
                            lhs = dw_q[hb % 2][:, ib * 128:(ib + 1) * 128]
                            rhs = h_buf[:, ib * TTN + ts * 512: ib * TTN + (ts + 1) * 512]
                            o = pA[:, par * 512:(par + 1) * 512]
                            if ib == IB - 1:
                                P.op("tensor", lambda t, o=o, a=lhs, b=rhs:
                                     t.matmul(o, a, b, start=False, stop=True).then_inc(sPE.h, 1))
                            else:
                                P.op("tensor", lambda t, o=o, a=lhs, b=rhs, st=(ib == 0):
                                     t.matmul(o, a, b, start=st, stop=False))

                for ib in range(IB):
                    P.wait("scalar", sPE, g_done[ib])
                    if ib >= 2:
                        P.wait("scalar", sDV, h_done[ib - 2])
                    elif prev is not None:
                        P.wait("scalar", sDV, prev["h_done"][IB - 1])
                    gbase = TTN * (ib % 2)
                    P.op("scalar", lambda s, ib=ib, gbase=gbase: _dinc(s, s.activation(
                        out=g_sb[ib % 2][:], in_=pA[:, gbase:gbase + TTN], func=AF.Silu,
                        scale=gst[:, ib:ib + 1]), sAC.h))
                    if tt == 0 and ib % NBW == 0:
                        b = ib // NBW
                        P.wait("scalar", sDV, gwsh[b])
                        P.op("scalar", lambda e, b=b: e.dma_start(
                            out=scr_gw[b * NBW:(b + 1) * NBW].rearrange("n p f -> p n f"),
                            in_=gw_q[b % 2].ap().rearrange("p (n f) -> p n f", n=NBW))
                            .then_inc(sscw[b % 2].h, 16))
                        P.wait("scalar", sDV, uwsh[b])
                        P.op("scalar", lambda e, b=b: e.dma_start(
                            out=scr_uw[b * NBW:(b + 1) * NBW].rearrange("n p f -> p n f"),
                            in_=uw_q[b % 2].ap().rearrange("p (n f) -> p n f", n=NBW))
                            .then_inc(sscw[b % 2].h, 16))
                for hb in range(HB):
                    if tt == 0:
                        P.wait("scalar", sDV, dwsh[hb])
                        P.op("scalar", lambda e, hb=hb: e.dma_start(
                            out=scr_dw[hb], in_=dw_q[hb % 2][:]).then_inc(sdww[hb % 2].h, 16))
                    for ts in range(NSP):
                        idx = hb * NSP + ts
                        par = idx % 2
                        P.wait("scalar", sPE, dn_done[hb][ts])
                        if idx >= 2:
                            P.wait("scalar", sgf[idx % 2], gf_tp[(idx - 2) // NSP][(idx - 2) % NSP])
                        elif prev is not None:
                            P.wait("scalar", sgf[idx % 2], prev["gf_tp"][HB - 1][NSP - 1])
                        P.op("scalar", lambda s, hb=hb, par=par: _dinc(s, s.activation(
                            out=gfst[par][:], in_=pA[:, par * 512:(par + 1) * 512],
                            func=AF.Silu, scale=dsc[:, hb:hb + 1]), sAC.h))

                for hb in range(HB):
                    for ts in range(NSP):
                        idx = hb * NSP + ts
                        par = idx % 2
                        P.wait("sync", sAC, gf_ev[hb][ts])
                        if idx >= 2:
                            P.wait("sync", sgf[par], gf_tp[(idx - 2) // NSP][(idx - 2) % NSP])
                        elif prev is not None:
                            P.wait("sync", sgf[par], prev["gf_tp"][HB - 1][NSP - 1])
                        for tc4 in range(4):
                            P.op("sync", lambda e, par=par, tc4=tc4: e.dma_start_transpose(
                                gftst[par][:, tc4 * 128:(tc4 + 1) * 128],
                                gfst[par][:, tc4 * 128:(tc4 + 1) * 128]).then_inc(sgf[par].h, 16))
                        P.wait("sync", sgf[par], gf_tpa[hb][ts])
                        for tc4 in range(4):
                            tchunk = tt * (TTN // 128) + ts * 4 + tc4
                            P.op("sync", lambda e, par=par, tc4=tc4, tchunk=tchunk, hb=hb:
                                 e.dma_start(
                                     out=st_gf[tchunk][:, hb * 128:(hb + 1) * 128],
                                     in_=gftst[par][:, tc4 * 128:(tc4 + 1) * 128])
                                 .then_inc(sgf[par].h, 16))
                P.emit()
                prev = dict(g_done=g_done, u_done=u_done, dn_done=dn_done,
                            gf_ev=gf_ev, gf_tp=gf_tp, h_done=h_done, uev=uev,
                            scw_g=scw_g, scw_u=scw_u, scw_d=scw_d)
            barrier()
        dbg_dump("stgf", st_gf)
        P.emit()
        if upto <= 4:
            return nc

        with ExitStack() as ph5:
            mem_bf = sbuf("mem_bf", [128, H * 512], BF16, Gs)
            s_runb = sbuf("s_runb", [128, H * 129], BF16, Gs)
            w_oT = sbuf("w_oT", [128, DB * D], BF16, Gs)
            W_b = sbuf("W_b", [128, H * M], F32, Gs)
            G_sb = sbuf("G_sb", [128, H], F32, Gs)
            Gc_sb = sbuf("Gc_sb", [128, H], F32, Gs)
            Geff = sbuf("Geff", [128, H], F32, Gs)

            sgath = [sbuf(f"sgath{i}", [128, 8 * 129], F32, ph5) for i in range(2)]
            relg = sbuf("relg", [1, 8 * H * M], F32, ph5)
            rel_all = sbuf("rel_all", [1, H * M], F32, ph5)
            rel_t = sbuf("rel_t", [H, M], F32, ph5)
            mnr_sb = sbuf("mnr_sb", [M * H, 128], F32, ph5)
            mnr1 = sbuf("mnr1", [M * H, 1], F32, ph5)
            msel_sb = sbuf("msel_sb", [M * H, M], F32, ph5)
            e_sb = sbuf("e_sb", [1, M], F32, ph5)
            esum = sbuf("esum", [1, 1], F32, ph5)
            allf = sbuf("allf", [1, 1], F32, ph5)
            notall = sbuf("notall", [1, 1], F32, ph5)
            e_b = sbuf("e_b", [H, M], F32, ph5)
            one_m = sbuf("one_m", [H, M], F32, ph5)
            relm = sbuf("relm", [H, M], F32, ph5)
            rmax = sbuf("rmax", [H, 1], F32, ph5)
            negmax = sbuf("negmax", [H, 1], F32, ph5)
            wex = sbuf("wex", [H, M], F32, ph5)
            ssum = sbuf("ssum", [H, 1], F32, ph5)
            srq = sbuf("srq", [H, 1], F32, ph5)
            W_sb = sbuf("W_sb", [H, M], F32, ph5)
            gate_sb = sbuf("gate_sb", [1, H], F32, ph5)
            gsig = sbuf("gsig", [1, H], F32, ph5)
            gsig2 = sbuf("gsig2", [1, H], F32, ph5)
            maskcol = sbuf("maskcol", [128, 8], F32, ph5)

            P.wait("gpsimd", sCC, 1)

            def gld(sem, dst, src, **kw):
                P.op("gpsimd", lambda g, d=dst, s=src, sem=sem, kw=kw:
                     g.dma_start(out=d, in_=s, **kw).then_inc(sem.h, 16))
                return sem.inc(16)
            ccv = cc_out.rearrange("(r q) f -> r q f", q=CCR)
            ld_relg = gld(srelg, relg.ap().rearrange("a (r f) -> a r f", r=8),
                          ccv[:, H * 128:H * 128 + 1, 0:H * M].rearrange("r a f -> a r f"))
            ld_mnr = gld(smnr, mnr_sb[:], ap["mnrows"])
            ld_gate = gld(sgt2, gate_sb[:], ap["gate"])
            ld_msel = gld(smsl, msel_sb[:], ap["msel"])
            ld_sg = [0] * H
            pe_mkc = sPE.inc()
            pe_emp = sPE.inc()
            pe_eb = sPE.inc()
            pe_wb = sPE.inc()
            pe_gb = sPE.inc()
            pe_geff = sPE.inc()
            ac_wex = sAC.inc()
            ac_gsig = sAC.inc()
            dv_rel = sDV.inc()
            dv_mnred = sDV.inc()
            dv_mkc = sDV.inc()
            dv_e = sDV.inc()
            dv_esum = sDV.inc()
            dv_notall = sDV.inc()
            dv_sinit = [0] * H
            for h in range(H):
                dv_sinit[h] = sDV.inc()
            dv_srunb = sDV.inc()
            dv_ebm = sDV.inc()
            dv_relm = sDV.inc()
            dv_rmax = sDV.inc()
            dv_negmax = sDV.inc()
            dv_ssum = sDV.inc()
            dv_srq = sDV.inc()
            dv_wsb = sDV.inc()
            dv_wbev = sDV.inc()
            dv_gsig2 = sDV.inc()
            dv_gbev = sDV.inc()
            dv_geffev = sDV.inc()
            sy_relbuf = srel.inc(16)
            sy_wbuf = swbf.inc(16)

            for h in range(H):
                if h >= 2:
                    P.wait("gpsimd", sDV, dv_sinit[h - 2])
                ld_sg[h] = gld(
                    ssg[h % 2],
                    sgath[h % 2].ap().rearrange("p (r f) -> p r f", r=8),
                    ccv[:, h * 128:(h + 1) * 128, :].rearrange("r p f -> p r f"))
            ld_mem = [0] * H
            for h in range(H):
                ld_mem[h] = gld(
                    sGt,
                    mem_bf[:, h * 512:(h + 1) * 512].rearrange("p (m e) -> p m e", m=M),
                    ap["mem"][:, h].rearrange("m p e -> p m e"))
            ld_wo = [0] * DB
            for db in range(DB):
                ld_wo[db] = gld(sGt, w_oT[:, db * D:(db + 1) * D], ap["woT"][db])
            P.wait("gpsimd", srel, sy_relbuf)
            ld_relt = gld(srlt, rel_t[:], relbuf.rearrange("a (h m) -> (a h) m", h=H))
            P.wait("gpsimd", swbf, sy_wbuf)
            ld_wfl = gld(swfl, relg[:, 0:H * M], wbuf)

            P.op("tensor", lambda t: t.matmul(pB[:, 0:8], onesr[:], rmk[:],
                                              start=True, stop=True).then_inc(sPE.h, 1))
            P.wait("tensor", sDV, dv_mnred)
            P.wait("tensor", smsl, ld_msel)
            P.op("tensor", lambda t: t.matmul(pA[0:1, 0:4], mnr1[:], msel_sb[:],
                                              start=True, stop=True).then_inc(sPE.h, 1))
            P.wait("tensor", sDV, dv_e)
            P.op("tensor", lambda t: t.matmul(pA[0:H, 512:516], onesr[:, 0:H], e_sb[:],
                                              start=True, stop=True).then_inc(sPE.h, 1))
            P.wait("tensor", swfl, ld_wfl)
            P.op("tensor", lambda t: t.matmul(pB[:, 512:512 + H * M], onesr[:], relg[:, 0:H * M],
                                              start=True, stop=True).then_inc(sPE.h, 1))
            P.wait("tensor", sAC, ac_gsig)
            P.op("tensor", lambda t: t.matmul(pB[:, 1024:1024 + H], onesr[:], gsig[:],
                                              start=True, stop=True).then_inc(sPE.h, 1))
            P.wait("tensor", sDV, dv_gsig2)
            P.op("tensor", lambda t: t.matmul(pB[:, 1536:1536 + H], onesr[:], gsig2[:],
                                              start=True, stop=True).then_inc(sPE.h, 1))

            P.wait("vector", srelg, ld_relg)
            P.op("vector", lambda v: v.tensor_add(out=rel_all[:], in0=relg[:, 0:64],
                                                  in1=relg[:, 64:128]))
            P.op("vector", lambda v: v.drain())
            for r in range(2, 8):
                if r == 7:
                    P.op("vector", lambda v, r=r: _dinc(v, v.tensor_add(
                        out=rel_all[:], in0=rel_all[:],
                        in1=relg[:, r * 64:(r + 1) * 64]), sDV.h))
                else:
                    P.op("vector", lambda v, r=r: v.tensor_add(
                        out=rel_all[:], in0=rel_all[:], in1=relg[:, r * 64:(r + 1) * 64]))
                    P.op("vector", lambda v: v.drain())
            P.wait("vector", smnr, ld_mnr)
            P.op("vector", lambda v: _dinc(v, v.reduce_sum(
                out=mnr1[:], in_=mnr_sb[:], axis=AX.X), sDV.h))
            P.wait("vector", sPE, pe_mkc)
            P.op("vector", lambda v: _dinc(v, v.tensor_copy(out=maskcol[:], in_=pB[:, 0:8])
                 , sDV.h))
            P.wait("vector", sPE, pe_emp)
            P.op("vector", lambda v: _dinc(v, v.tensor_single_scalar(
                out=e_sb[:], in_=pA[0:1, 0:4], scalar=1e-6, op=ALU.is_lt), sDV.h))
            P.wait("vector", sDV, dv_e)
            P.op("vector", lambda v: _dinc(v, v.reduce_sum(out=esum[:], in_=e_sb[:], axis=AX.X)
                 , sDV.h))
            P.wait("vector", sDV, dv_esum)
            P.op("vector", lambda v: v.tensor_single_scalar(
                out=allf[:], in_=esum[:], scalar=3.5, op=ALU.is_ge))
            P.op("vector", lambda v: v.drain())
            P.op("vector", lambda v: _dinc(v, v.tensor_scalar(
                out=notall[:], in0=allf[:], scalar1=-1.0, scalar2=1.0,
                op0=ALU.mult, op1=ALU.add), sDV.h))
            for h in range(H):
                P.wait("vector", ssg[h % 2], ld_sg[h])
                P.wait("vector", sDV, dv_mkc)
                P.op("vector", lambda v, h=h: v.tensor_scalar_mul(
                    out=s_run[:, h * 129:(h + 1) * 129],
                    in0=sgath[h % 2][:, 0:129], scalar1=maskcol[:, 0:1]))
                for r in range(1, 8):
                    def _sin(v, h=h, r=r):
                        v.drain()
                        v.scalar_tensor_tensor(
                            out=s_run[:, h * 129:(h + 1) * 129],
                            in0=sgath[h % 2][:, r * 129:(r + 1) * 129],
                            scalar=maskcol[:, r:r + 1],
                            in1=s_run[:, h * 129:(h + 1) * 129],
                            op0=ALU.mult, op1=ALU.add)
                        if r == 7:
                            v.drain().then_inc(sDV.h, 1)
                    P.op("vector", _sin)
            P.op("vector", lambda v: _dinc(v, v.tensor_copy(out=s_runb[:], in_=s_run[:]), sDV.h))
            P.wait("vector", sPE, pe_eb)
            P.op("vector", lambda v: v.tensor_copy(out=e_b[:], in_=pA[0:H, 512:516]))
            P.op("vector", lambda v: v.drain())
            P.op("vector", lambda v: _dinc(v, v.tensor_scalar(
                out=one_m[:], in0=e_b[:], scalar1=-1.0, scalar2=1.0,
                op0=ALU.mult, op1=ALU.add), sDV.h))
            P.wait("vector", srlt, ld_relt)
            P.op("vector", lambda v: v.tensor_mul(out=relm[:], in0=rel_t[:], in1=one_m[:]))
            P.op("vector", lambda v: v.drain())
            P.op("vector", lambda v: _dinc(v, v.scalar_tensor_tensor(
                out=relm[:], in0=e_b[:], scalar=-1e30, in1=relm[:],
                op0=ALU.mult, op1=ALU.add), sDV.h))
            P.wait("vector", sDV, dv_relm)
            P.op("vector", lambda v: _dinc(v, v.reduce_max(out=rmax[:], in_=relm[:], axis=AX.X)
                 , sDV.h))
            P.wait("vector", sDV, dv_rmax)
            P.op("vector", lambda v: _dinc(v, v.tensor_scalar_mul(out=negmax[:], in0=rmax[:], scalar1=-1.0)
                 , sDV.h))
            P.wait("vector", sAC, ac_wex)
            P.op("vector", lambda v: _dinc(v, v.reduce_sum(out=ssum[:], in_=wex[:], axis=AX.X)
                 , sDV.h))
            P.wait("vector", sDV, dv_ssum)
            P.op("vector", lambda v: _dinc(v, v.reciprocal(out=srq[:], in_=ssum[:]), sDV.h))
            P.wait("vector", sDV, dv_srq)
            P.op("vector", lambda v: _dinc(v, v.tensor_scalar_mul(out=W_sb[:], in0=wex[:], scalar1=srq[:])
                 , sDV.h))
            P.wait("vector", sPE, pe_wb)
            P.op("vector", lambda v: _dinc(v, v.tensor_copy(out=W_b[:], in_=pB[:, 512:512 + H * M])
                 , sDV.h))
            P.wait("vector", sAC, ac_gsig)
            P.wait("vector", sDV, dv_notall)
            P.op("vector", lambda v: _dinc(v, v.tensor_scalar_mul(
                out=gsig2[:], in0=gsig[:], scalar1=notall[0:1, 0:1]), sDV.h))
            P.wait("vector", sPE, pe_gb)
            P.op("vector", lambda v: v.tensor_copy(out=G_sb[:], in_=pB[:, 1024:1024 + H]))
            P.op("vector", lambda v: v.drain())
            P.op("vector", lambda v: _dinc(v, v.tensor_scalar(
                out=Gc_sb[:], in0=G_sb[:], scalar1=-1.0, scalar2=1.0,
                op0=ALU.mult, op1=ALU.add), sDV.h))
            P.wait("vector", sPE, pe_geff)
            P.op("vector", lambda v: _dinc(v, v.tensor_copy(out=Geff[:], in_=pB[:, 1536:1536 + H])
                 , sDV.h))

            P.wait("scalar", sDV, dv_negmax)
            P.op("scalar", lambda s: _dinc(s, s.activation(out=wex[:], in_=relm[:], func=AF.Exp,
                                                  bias=negmax[:]), sAC.h))
            P.wait("scalar", sgt2, ld_gate)
            P.op("scalar", lambda s: _dinc(s, s.activation(out=gsig[:], in_=gate_sb[:], func=AF.Sigmoid)
                 , sAC.h))

            P.wait("sync", sDV, dv_rel)
            P.op("sync", lambda e: e.dma_start(out=relbuf, in_=rel_all[:]).then_inc(srel.h, 16))
            P.wait("sync", sDV, dv_wsb)
            P.op("sync", lambda e: e.dma_start(out=wbuf, in_=W_sb[:]).then_inc(swbf.h, 16))
            P.emit()
            barrier()
        dbg_dump("sinit", s_run[:])
        dbg_dump("wb", W_b[:])
        dbg_dump("ccout", cc_out)
        if "gg" in dbg:
            P.op("gpsimd", lambda g: g.dma_start(
                out=dbg["gg"][:, 0:H], in_=G_sb[:]).then_inc(sGt.h, 16)); sGt.inc(16)
            P.op("gpsimd", lambda g: g.dma_start(
                out=dbg["gg"][:, H:2 * H], in_=Gc_sb[:]).then_inc(sGt.h, 16)); sGt.inc(16)
            P.op("gpsimd", lambda g: g.dma_start(
                out=dbg["gg"][:, 2 * H:3 * H], in_=Geff[:]).then_inc(sGt.h, 16)); sGt.inc(16)
        P.emit()
        if upto <= 5:
            return nc

        with ExitStack() as ph:
            x_c = [sbuf(f"x_c{i}", [128, D], F32, ph) for i in range(2)]
            qp_sl = [sbuf(f"qp_sl{i}", [128, H * 128], BF16, ph) for i in range(2)]
            in_sl = [sbuf(f"in_sl{i}", [128, H * 129], BF16, ph) for i in range(2)]
            sl_sl = [sbuf(f"sl_sl{i}", [128, H * 129], BF16, ph) for i in range(2)]
            gf_c = [sbuf(f"gf_c{i}", [128, D], BF16, ph) for i in range(2)]
            saug = [sbuf(f"saug{i}", [128, H * 129], BF16, ph) for i in range(2)]
            att_tm = [sbuf(f"att_tm{i}", [128, D], BF16, ph) for i in range(2)]
            afm = [sbuf(f"afm{i}", [128, D], BF16, ph) for i in range(2)]
            out_sb = [sbuf(f"out_sb{i}", [128, D], F32, ph) for i in range(2)]
            tsb = sbuf("tsb", [128, D], F32, ph)
            denc = [sbuf(f"denc{i}", [128, 1], F32, ph) for i in range(4)]
            dreq = [sbuf(f"dreq{i}", [128, 1], F32, ph) for i in range(4)]
            loct = [sbuf(f"loct{i}", [128, 128], BF16, ph) for i in range(4)]
            nrmc = [sbuf(f"nrmc{i}", [128, M], F32, ph) for i in range(4)]
            nrq = [sbuf(f"nrq{i}", [128, M], F32, ph) for i in range(4)]
            sc4 = [sbuf(f"sc4_{i}", [128, M], F32, ph) for i in range(4)]
            mem_e = [sbuf(f"mem_e{i}", [128, 512], BF16, ph) for i in range(4)]
            numc = [sbuf(f"numc{i}", [128, 128], BF16, ph) for i in range(4)]
            qm = [sbuf(f"qm{i}", [128, 4 * 128], BF16, ph) for i in range(4)]
            s0123 = [sbuf(f"s0123_{i}", [128, 256], BF16, ph) for i in range(4)]
            macc = [sbuf(f"macc{i}", [128, 128], BF16, ph) for i in range(4)]
            t1 = [sbuf(f"t1_{i}", [128, 128], BF16, ph) for i in range(4)]

            ldx = [0] * C; ldqp = [0] * C; ldin = [0] * C; ldsl = [0] * C; ldgf = [0] * C
            for c in range(C):
                ldx[c] = sx[c % 2].inc(16)
                ldqp[c] = sqp[c % 2].inc(16)
                ldin[c] = sin_[c % 2].inc(16)
                ldsl[c] = ssl[c % 2].inc(16)
                ldgf[c] = sgfc[c % 2].inc(16)
            pe_h = [[0] * H for _ in range(C)]
            wo_done = [[0, 0] for _ in range(C)]
            for c in range(C):
                for h in range(H):
                    pe_h[c][h] = sPE.inc()
                wo_done[c][0] = sPE.inc()
                wo_done[c][1] = sPE.inc()
            NHG = 4
            saug_dv = [0] * C
            st1h = [[0] * H for _ in range(C)]
            att_dv = [[0] * H for _ in range(C)]
            fin_dv = [[0, 0] for _ in range(C)]
            for c in range(C):
                saug_dv[c] = sDV.inc()
                for hg in range(H // NHG):
                    for hh in range(NHG):
                        st1h[c][hg * NHG + hh] = sDV.inc()
                    for hh in range(NHG):
                        att_dv[c][hg * NHG + hh] = sDV.inc()
                fin_dv[c][0] = sDV.inc()
                fin_dv[c][1] = sDV.inc()
            att_tp = [satp[c % 2].inc(16 * H) for c in range(C)]
            out_wr = [sst.inc(16) for _ in range(C)]

            def _pslot(h):
                s = h % 4
                return (s % 2) * 512 + (s // 2) * 256

            stqp_v = st_qp.rearrange("h p t -> p h t")
            stin_v = st_intra.rearrange("h p t -> p h t")
            stsl_v = st_sloc.rearrange("h p t -> p h t")
            for c in range(C):
                if c >= 2:
                    P.wait("gpsimd", sDV, fin_dv[c - 2][1])
                    P.wait("gpsimd", sPE, pe_h[c - 2][H - 1])
                P.op("gpsimd", lambda g, c=c: g.dma_start(
                    out=x_c[c % 2][:], in_=ap["x"][c * 128:(c + 1) * 128, :]).then_inc(sx[c % 2].h, 16))
                P.op("gpsimd", lambda g, c=c: g.dma_start(
                    out=qp_sl[c % 2].ap().rearrange("p (h t) -> p h t", h=H),
                    in_=stqp_v[:, :, c * 128:(c + 1) * 128]).then_inc(sqp[c % 2].h, 16))
                P.op("gpsimd", lambda g, c=c: g.dma_start(
                    out=in_sl[c % 2].ap().rearrange("p (h t) -> p h t", h=H),
                    in_=stin_v[:, :, c * 129:(c + 1) * 129]).then_inc(sin_[c % 2].h, 16))
                P.op("gpsimd", lambda g, c=c: g.dma_start(
                    out=sl_sl[c % 2].ap().rearrange("p (h t) -> p h t", h=H),
                    in_=stsl_v[:, :, c * 129:(c + 1) * 129]).then_inc(ssl[c % 2].h, 16))
                P.op("gpsimd", lambda g, c=c: g.dma_start(
                    out=gf_c[c % 2][:], in_=st_gf[c]).then_inc(sgfc[c % 2].h, 16))
            P.wait("gpsimd", sst, out_wr[C - 1])

            flat = []
            for c in range(C):
                for h in range(H):
                    flat.append((c, h))
            NHG = 4
            for c in range(C):
                P.wait("vector", ssl[c % 2], ldsl[c])
                if c >= 2:
                    P.wait("vector", sPE, pe_h[c - 2][H - 1])
                    P.wait("vector", satp[c % 2], att_tp[c - 2])
                    P.wait("vector", sst, out_wr[c - 2])
                P.op("vector", lambda v, c=c: _dinc(v, v.tensor_add(
                    out=saug[c % 2][:], in0=s_runb[:], in1=sl_sl[c % 2][:]), sDV.h))
                for hg in range(H // NHG):
                    h0 = hg * NHG
                    gi0 = c * H + h0
                    for hh in range(NHG):
                        h = h0 + hh
                        pS = _pslot(h)
                        pM = (h % 4) * 512
                        P.wait("vector", sPE, pe_h[c][h])
                        P.op("vector", lambda v, h=h, pM=pM: v.tensor_copy(
                            out=mem_e[h % 4][:], in_=pA[:, pM:pM + 512]))
                        P.op("vector", lambda v, h=h, pS=pS: v.tensor_copy(
                            out=numc[h % 4][:], in_=pB[:, pS:pS + 128]))
                        P.op("vector", lambda v, h=h, pS=pS: v.tensor_scalar_max(
                            out=denc[h % 4][:], in0=pB[:, pS + 128:pS + 129], scalar1=1e-6))
                        P.op("vector", lambda v, h=h, pS=pS: v.tensor_scalar_max(
                            out=nrmc[h % 4][:], in0=pB[:, pS + 160:pS + 160 + M],
                            scalar1=1e-6).then_inc(sDV.h, 1))
                    P.op("vector", lambda v: v.drain())
                    for hh in range(NHG):
                        h = h0 + hh
                        P.op("vector", lambda v, h=h: v.reciprocal(
                            out=dreq[h % 4][:], in_=denc[h % 4][:]))
                        P.op("vector", lambda v, h=h: v.reciprocal(
                            out=nrq[h % 4][:], in_=nrmc[h % 4][:]))
                    P.op("vector", lambda v: v.drain())
                    for hh in range(NHG):
                        h = h0 + hh
                        P.op("vector", lambda v, h=h: v.tensor_mul(
                            out=sc4[h % 4][:], in0=nrq[h % 4][:], in1=W_b[:, h * M:(h + 1) * M]))
                    P.op("vector", lambda v: v.drain())
                    for hh in range(NHG):
                        h = h0 + hh
                        pS = _pslot(h)
                        P.op("vector", lambda v, h=h: v.tensor_scalar_mul(
                            out=loct[h % 4][:], in0=numc[h % 4][:], scalar1=dreq[h % 4][:]))
                        for m in range(M):
                            P.op("vector", lambda v, h=h, m=m: v.tensor_scalar_mul(
                                out=qm[h % 4][:, m * 128:(m + 1) * 128],
                                in0=mem_e[h % 4][:, m * 128:(m + 1) * 128],
                                scalar1=sc4[h % 4][:, m:m + 1]))
                    P.op("vector", lambda v: v.drain())
                    for hh in range(NHG):
                        h = h0 + hh
                        P.op("vector", lambda v, h=h: v.tensor_add(
                            out=s0123[h % 4][:], in0=qm[h % 4][:, 0:256], in1=qm[h % 4][:, 256:512]))
                        P.op("vector", lambda v, h=h: v.tensor_scalar_mul(
                            out=t1[h % 4][:], in0=loct[h % 4][:], scalar1=Gc_sb[:, h:h + 1]))
                    P.op("vector", lambda v: v.drain())
                    for hh in range(NHG):
                        h = h0 + hh
                        P.op("vector", lambda v, h=h: v.tensor_add(
                            out=macc[h % 4][:], in0=s0123[h % 4][:, 0:128],
                            in1=s0123[h % 4][:, 128:256]))
                    P.op("vector", lambda v: v.drain())
                    for hh in range(NHG):
                        h = h0 + hh
                        P.op("vector", lambda v, c=c, h=h: v.scalar_tensor_tensor(
                            out=att_tm[c % 2][:, h * 128:(h + 1) * 128], in0=macc[h % 4][:],
                            scalar=Geff[:, h:h + 1], in1=t1[h % 4][:],
                            op0=ALU.mult, op1=ALU.add))
                    P.op("vector", lambda v: v.drain().then_inc(sDV.h, NHG))
                P.wait("vector", sgfc[c % 2], ldgf[c])
                P.wait("vector", sx[c % 2], ldx[c])
                for half in range(2):
                    P.wait("vector", sPE, wo_done[c][half])
                    P.op("vector", lambda v, c=c, half=half: v.tensor_tensor(
                        out=tsb[:, 0:1024], in0=pB[:, 1024:2048],
                        in1=gf_c[c % 2][:, half * 1024:(half + 1) * 1024], op=ALU.mult))
                    P.op("vector", lambda v: v.drain())
                    P.op("vector", lambda v, c=c, half=half: _dinc(v, v.tensor_add(
                        out=out_sb[c % 2][:, half * 1024:(half + 1) * 1024],
                        in0=tsb[:, 0:1024],
                        in1=x_c[c % 2][:, half * 1024:(half + 1) * 1024]), sDV.h))

            for c in range(C):
                P.wait("tensor", sDV, saug_dv[c])
                P.wait("tensor", sqp[c % 2], ldqp[c])
                P.wait("tensor", sin_[c % 2], ldin[c])
                for h in range(H):
                    pS = _pslot(h)
                    pM = (h % 4) * 512
                    qp_t = qp_sl[c % 2][:, h * 128:(h + 1) * 128]
                    gi = c * H + h
                    if gi >= 2:
                        c2, h2 = flat[gi - 2]
                        P.wait("tensor", sDV, st1h[c2][h2])
                    P.op("tensor", lambda t, o=pB[:, pS:pS + 129], a=qp_t,
                         b=saug[c % 2][:, h * 129:(h + 1) * 129]:
                         t.matmul(o, a, b, start=True, stop=False))
                    P.op("tensor", lambda t, o=pB[:, pS:pS + 129],
                         b=in_sl[c % 2][:, h * 129:(h + 1) * 129]:
                         t.matmul(o, identb[:], b, start=False, stop=True))
                    P.op("tensor", lambda t, o=pA[:, pM:pM + 512], a=qp_t,
                         b=mem_bf[:, h * 512:(h + 1) * 512]:
                         t.matmul(o, a, b, start=True, stop=True))
                    P.op("tensor", lambda t, o=pB[:, pS + 160:pS + 160 + M], a=qp_t,
                         b=mnT_b[:, h * M:(h + 1) * M]:
                         t.matmul(o, a, b, start=True, stop=True).then_inc(sPE.h, 1))
                P.wait("tensor", satp[c % 2], att_tp[c])
                for half in range(2):
                    if c >= 1 or half >= 1:
                        P.wait("tensor", sDV, fin_dv[c - 1][1] if half == 0 else fin_dv[c][0])
                    for db in range(DB):
                        lhs = afm[c % 2][:, db * 128:(db + 1) * 128]
                        for sp in range(2):
                            o = pB[:, 1024 + sp * 512:1024 + (sp + 1) * 512]
                            rhs = w_oT[:, db * D + half * 1024 + sp * 512:
                                       db * D + half * 1024 + (sp + 1) * 512]
                            if db == DB - 1 and sp == 1:
                                P.op("tensor", lambda t, o=o, a=lhs, b=rhs:
                                     t.matmul(o, a, b, start=False, stop=True).then_inc(sPE.h, 1))
                            else:
                                P.op("tensor", lambda t, o=o, a=lhs, b=rhs, st=(db == 0),
                                     fi=(db == DB - 1): t.matmul(o, a, b, start=st, stop=fi))

            for c in range(C):
                for h in range(H):
                    P.wait("scalar", sDV, att_dv[c][h])
                    if c >= 2 and h == 0:
                        P.wait("scalar", sPE, wo_done[c - 2][1])
                    P.op("scalar", lambda e, c=c, h=h: e.dma_start_transpose(
                        afm[c % 2][:, h * 128:(h + 1) * 128],
                        att_tm[c % 2][:, h * 128:(h + 1) * 128]).then_inc(satp[c % 2].h, 16))

            for c in range(C):
                P.wait("sync", sDV, fin_dv[c][1])
                P.op("sync", lambda e, c=c: e.dma_start(
                    out=out[c * 128:(c + 1) * 128, :], in_=out_sb[c % 2][:]).then_inc(sst.h, 16))
            if "att0" in dbg:
                P.wait("sync", satp[0], att_tp[0])
                P.op("sync", lambda e: e.dma_start(out=dbg["att0"], in_=afm[0][:])
                     .then_inc(sst.h, 16))
                sst.inc(16)
            P.emit()
    return nc


def _mk_inputs(inputs):
    f = np.float32
    x = np.asarray(inputs["L0_attn_hidden_states"], f).reshape(T * NCORE, D)
    wq = np.asarray(inputs["L0_attn_w_q"], f)
    wk = np.asarray(inputs["L0_attn_w_k"], f)
    wv = np.asarray(inputs["L0_attn_w_v"], f)
    wo = np.asarray(inputs["L0_attn_w_o"], f)
    mem = np.asarray(inputs["L0_attn_memories"], f)
    mn = np.asarray(inputs["L0_attn_memory_norms"], f)
    gate = np.asarray(inputs["L0_attn_gate"], f)
    gw = np.asarray(inputs["L0_ffn_gate_w"], np.int32)
    gs = np.asarray(inputs["L0_ffn_gate_scale"], f)
    uw = np.asarray(inputs["L0_ffn_up_w"], np.int32)
    us = np.asarray(inputs["L0_ffn_up_scale"], f)
    dw = np.asarray(inputs["L0_ffn_down_w"], np.int32)
    ds = np.asarray(inputs["L0_ffn_down_scale"], f)

    com = {
        "wqT": np.ascontiguousarray(wq.T.reshape(DB, 128, H, 128).transpose(2, 0, 1, 3)),
        "wkT": np.ascontiguousarray(wk.T.reshape(DB, 128, H, 128).transpose(2, 0, 1, 3)),
        "wvT": np.ascontiguousarray(wv.T.reshape(DB, 128, D)),
        "woT": np.ascontiguousarray(wo.T.reshape(DB, 128, D)),
        "gw": np.ascontiguousarray(gw.T.reshape(HB, 128, IB, 128).transpose(2, 0, 1, 3)),
        "uw": np.ascontiguousarray(uw.T.reshape(HB, 128, IB, 128).transpose(2, 0, 1, 3)),
        "dw": np.ascontiguousarray(dw.T.reshape(IB, 128, HB, 128).transpose(2, 0, 1, 3)),
        "gsT": np.ascontiguousarray(gs.reshape(IB, 128).T),
        "usT": np.ascontiguousarray(us.reshape(IB, 128).T),
        "dsT": np.ascontiguousarray(ds.reshape(HB, 128).T),
        "mem": np.ascontiguousarray(mem),
        "mnTh": np.ascontiguousarray(mn.transpose(2, 1, 0).reshape(128, H * M)),
        "mnrows": np.ascontiguousarray(mn.reshape(M * H, 128)),
        "gate": np.ascontiguousarray(gate.reshape(1, H)),
        "tril": np.triu(np.ones((128, 128))).astype(bf16),
        "i_epsc": np.full((128, 1), 1e-5, f),
        "i_sent": np.arange(700, 708, dtype=f)[None, :] + np.zeros((128, 1), f),
        "i_onesb": np.ones((128, 1), bf16),
        "i_onesr": np.ones((1, 128), f),
        "msel": np.kron(np.eye(M, dtype=f), np.ones((H, 1), f)),
        "i_ident": np.eye(128).astype(bf16),
    }
    in_maps = []
    for r in range(NCORE):
        m = dict(com)
        m["x"] = np.ascontiguousarray(x[r * T:(r + 1) * T])
        rm = np.zeros((1, 8), f)
        rm[0, :r] = 1.0
        m["rmask"] = rm
        in_maps.append(m)
    return in_maps


def kernel(**inputs):
    in_maps = _mk_inputs(inputs)
    nc = build()
    res = run_bass_kernel_spmd(nc, in_maps, core_ids=list(range(NCORE)))
    outs = [np.asarray(res.results[r]["out"]) for r in range(NCORE)]
    return np.concatenate(outs, axis=0).reshape(1, T * NCORE, D).astype(np.float32)


# version 08 (final)
# speedup vs baseline: 1.0942x; 1.0196x over previous
import numpy as np
import ml_dtypes
import concourse.bass as bass
import concourse.mybir as mybir
from concourse.bass_utils import run_bass_kernel_spmd
from contextlib import ExitStack

F32 = mybir.dt.float32
BF16 = mybir.dt.bfloat16
I32 = mybir.dt.int32
AF = mybir.ActivationFunctionType
ALU = mybir.AluOpType
AX = mybir.AxisListType

NCORE = 8
T = 2048
C = 16
H = 16
HD = 128
D = 2048
DB = 16
II = 8192
IB = 64
HB = 16
M = 4
SCALE = HD ** -0.5
TTN = 512
NTT = T // TTN
NSP = TTN // 512
NBW = 4
NWB = IB // NBW
CCR = H * 128 + 16

bf16 = ml_dtypes.bfloat16
ENGS = ("tensor", "vector", "scalar", "gpsimd", "sync")

def _dinc(eng, ins, sem, k=1):
    eng.drain().then_inc(sem, k)
    return ins




class Ctr:
    def __init__(self, handle, name):
        self.h = handle
        self.v = 0
        self.name = name

    def inc(self, k=1):
        self.v += k
        return self.v


class Phase:
    def __init__(self, block):
        self.block = block
        self.s = {k: [] for k in ENGS}

    def op(self, eng, fn):
        self.s[eng].append(fn)

    def wait(self, eng, ctr, val):
        if val > 0:
            self.s[eng].append(lambda e, h=ctr.h, v=val: e.wait_ge(h, v))

    def emit(self):
        for name in ENGS:
            ops = self.s[name]
            if not ops:
                continue

            def sec(e, ops=ops):
                for f in ops:
                    f(e)
            getattr(self.block, name)(sec)
        self.s = {k: [] for k in ENGS}


def build(dbg_names=(), upto=6, no_cc=False):
    nc = bass.Bass("TRN2", target_bir_lowering=False, detect_race_conditions=False)
    ap = {}

    def din(name, shape, dt):
        ap[name] = nc.dram_tensor(name, shape, dt, kind="ExternalInput").ap()

    din("x", [T, D], F32)
    din("wqT", [H, DB, 128, 128], F32)
    din("wkT", [H, DB, 128, 128], F32)
    din("wvT", [DB, 128, D], F32)
    din("woT", [DB, 128, D], F32)
    din("gw", [IB, HB, 128, 128], I32)
    din("uw", [IB, HB, 128, 128], I32)
    din("dw", [HB, IB, 128, 128], I32)
    din("gsT", [128, IB], F32)
    din("usT", [128, IB], F32)
    din("dsT", [128, HB], F32)
    din("mem", [M, H, 128, 128], F32)
    din("mnTh", [128, H * M], F32)
    din("mnrows", [M * H, 128], F32)
    din("gate", [1, H], F32)
    din("tril", [128, 128], BF16)
    din("i_epsc", [128, 1], F32)
    din("i_sent", [128, 8], F32)
    din("i_onesb", [128, 1], BF16)
    din("i_onesr", [1, 128], F32)
    din("rmask", [1, 8], F32)
    din("msel", [M * H, M], F32)
    din("i_ident", [128, 128], BF16)
    out = nc.dram_tensor("out", [T, D], F32, kind="ExternalOutput").ap()

    st_xn = nc.dram_tensor("st_xn", [DB, 128, T], BF16, kind="Internal").ap()
    st_qp = nc.dram_tensor("st_qp", [H, 128, T], BF16, kind="Internal").ap()
    st_intra = nc.dram_tensor("st_intra", [H, 128, C * 129], BF16, kind="Internal").ap()
    st_sloc = nc.dram_tensor("st_sloc", [H, 128, C * 129], BF16, kind="Internal").ap()
    st_gf = nc.dram_tensor("st_gf", [C, 128, D], BF16, kind="Internal").ap()
    scr_gw = nc.dram_tensor("scr_gw", [IB, 128, HB * 128], BF16, kind="Internal").ap()
    scr_uw = nc.dram_tensor("scr_uw", [IB, 128, HB * 128], BF16, kind="Internal").ap()
    scr_dw = nc.dram_tensor("scr_dw", [HB, 128, IB * 128], BF16, kind="Internal").ap()
    cc_in = nc.dram_tensor("cc_in", [CCR, 129], F32, kind="Internal").ap()
    cc_out = nc.dram_tensor("cc_out", [NCORE * CCR, 129], F32, kind="Internal").ap()
    relbuf = nc.dram_tensor("relbuf", [1, H * M], F32, kind="Internal").ap()
    wbuf = nc.dram_tensor("wbuf", [H, M], F32, kind="Internal").ap()
    dbg = {}
    _dbg_shapes = {
        "stqp": ([H, 128, T], BF16), "stintra": ([H, 128, C * 129], BF16),
        "stsloc": ([H, 128, C * 129], BF16), "stgf": ([C, 128, D], BF16),
        "stxn": ([DB, 128, T], BF16), "srun": ([128, H * 129], F32),
        "relp": ([1, H * M], F32), "sinit": ([128, H * 129], F32),
        "wb": ([128, H * M], F32), "gg": ([128, 3 * H], F32),
        "att0": ([128, D], BF16), "ccout": ([NCORE * CCR, 129], F32),
        "xnfm": ([128, DB * T], BF16),
        "stats": ([128, 8 * C], F32),
        "hbuf0": ([128, IB * TTN], BF16),
        "xnsl0": ([128, HB * TTN], BF16),
    }
    for name in dbg_names:
        sh, dt = _dbg_shapes[name]
        dbg[name] = nc.dram_tensor("dbg_" + name, sh, dt, kind="ExternalOutput").ap()

    with ExitStack() as Gs:
        def sbuf(name, shape, dt, stack):
            return stack.enter_context(nc.sbuf_tensor(name, shape, dt))

        s_run = sbuf("s_run", [128, H * 129], F32, Gs)
        qsum = sbuf("qsum", [128, H], F32, Gs)
        mnT_f = sbuf("mnT_f", [128, M * H], F32, Gs)
        mnT_b = sbuf("mnT_b", [128, M * H], BF16, Gs)
        trilb = sbuf("trilb", [128, 128], BF16, Gs)
        identb = sbuf("identb", [128, 128], BF16, Gs)
        onesb = sbuf("onesb", [128, 1], BF16, Gs)
        onesr = sbuf("onesr", [1, 128], F32, Gs)
        rmk = sbuf("rmk", [1, 8], F32, Gs)
        epsc = sbuf("epsc", [128, 1], F32, Gs)
        dvflush = sbuf("dvflush", [128, 1], F32, Gs)
        attS = Gs.enter_context(ExitStack())
        xn_fm = sbuf("xn_fm", [128, DB * T], BF16, attS)
        v_tm = sbuf("v_tm", [128, C * D], BF16, attS)

        pA = Gs.enter_context(nc.psum_tensor("pA", [128, 2048], F32))
        pB = Gs.enter_context(nc.psum_tensor("pB", [128, 2048], F32))

        _sems = {}

        def S(name):
            if name not in _sems:
                _sems[name] = Ctr(Gs.enter_context(nc.semaphore(name)), name)
            return _sems[name]

        sPE, sDV, sAC, sCC, sB = S("sPE"), S("sDV"), S("sAC"), S("sCC"), S("sB")
        sGt, sst = S("sGt"), S("sst")
        sx = [S("sx0"), S("sx1")]
        stpS = [S("stpS0"), S("stpS1")]
        stpC = [S("stpC0"), S("stpC1")]
        swv = S("swv")
        swq = [S("swq0"), S("swq1")]
        swk = [S("swk0"), S("swk1")]
        skp = [S("skp0"), S("skp1")]
        sqst = [S("sqst0"), S("sqst1")]
        sist, ssst = S("sist"), S("ssst")
        sxn, ssc = S("sxn"), S("ssc")
        sgw = [S("sgw0"), S("sgw1")]
        suw = [S("suw0"), S("suw1")]
        sdw = [S("sdw0"), S("sdw1")]
        sscw = [S("sscw0"), S("sscw1")]
        sdww = [S("sdww0"), S("sdww1")]
        sgf = [S("sgf0"), S("sgf1")]
        satp = [S("satp0"), S("satp1")]
        sqp = [S("sqp0"), S("sqp1")]
        sin_ = [S("sin0"), S("sin1")]
        ssl = [S("ssl0"), S("ssl1")]
        sgfc = [S("sgfc0"), S("sgfc1")]
        srelg, smnr, sgt2, smsl = S("srelg"), S("smnr"), S("sgt2"), S("smsl")
        ssg = [S("ssg0"), S("ssg1")]
        srlt, swfl, srel, swbf = S("srlt"), S("swfl"), S("srel"), S("swbf")

        block = Gs.enter_context(nc.Block())
        P = Phase(block)

        def barrier():
            vals = [(s, s.v) for nm, s in sorted(_sems.items())
                    if nm not in ("sCC", "sB")]
            bv = sB.inc()
            for s, v in vals:
                if v:
                    P.wait("gpsimd", s, v)
            P.op("gpsimd", lambda g: g.engine_nop().then_inc(sB.h, 1))
            for eng in ("tensor", "vector", "scalar", "sync"):
                P.wait(eng, sB, bv)
            P.emit()

        def dbg_dump(name, src):
            if name in dbg:
                P.op("gpsimd", lambda g, o=dbg[name], i=src: g.dma_start(out=o, in_=i)
                     .then_inc(sGt.h, 16))
                sGt.inc(16)

        for dst, src in ((trilb[:], ap["tril"]), (identb[:], ap["i_ident"]),
                         (mnT_f[:], ap["mnTh"]),
                         (mnT_b[:], ap["mnTh"]), (rmk[:], ap["rmask"]),
                         (epsc[:], ap["i_epsc"]), (onesb[:], ap["i_onesb"]),
                         (onesr[:], ap["i_onesr"])):
            P.op("gpsimd", lambda g, d=dst, s=src: g.dma_start(out=d, in_=s).then_inc(sGt.h, 16))
            sGt.inc(16)

        def _init(v):
            v.memset(s_run[:], 0.0)
            v.memset(qsum[:], 0.0)
            v.tensor_copy(out=dvflush[:], in_=s_run[:, 0:1])
            v.drain().then_inc(sDV.h, 1)
        P.op("vector", _init)
        sDV.inc()
        P.emit()
        barrier()

        with ExitStack() as ph:
            x_sb = [sbuf(f"x_sb{i}", [128, D], F32, ph) for i in range(2)]
            xn_tm = [sbuf(f"xn_tm{i}", [128, D], BF16, ph) for i in range(2)]
            scrap = sbuf("scrap", [128, D], BF16, ph)
            xsumt = sbuf("xsumt", [128, 1], F32, ph)
            xsqt = sbuf("xsqt", [128, 1], F32, ph)
            meant = sbuf("meant", [128, 1], F32, ph)
            m2t = sbuf("m2t", [128, 1], F32, ph)
            vart = sbuf("vart", [128, 1], F32, ph)
            stdt = sbuf("stdt", [128, 1], F32, ph)
            rstdt = sbuf("rstdt", [128, 1], F32, ph)
            negmr = sbuf("negmr", [128, 1], F32, ph)
            flushc = sbuf("flushc", [128, 1], F32, ph)
            xcen = sbuf("xcen", [128, D], F32, ph)
            acflush = sbuf("acflush", [128, 1], F32, ph)
            statlog = sbuf("statlog", [128, 8 * C], F32, ph)

            if "stats" in dbg:
                for ii, tl in enumerate((xsumt, xsqt, meant, m2t, vart, stdt, rstdt, flushc)):
                    P.op("gpsimd", lambda g, ii=ii, tl=tl: g.dma_start(
                        out=tl[:], in_=ap["i_sent"][:, ii:ii + 1],
                        allow_slow_non_contiguous=True).then_inc(sGt.h, 16))
                    sGt.inc(16)
                sent_ld = sGt.v
                P.wait("vector", sGt, sent_ld)
                P.wait("scalar", sGt, sent_ld)
            xload = [sx[c % 2].inc(16) for c in range(C)]
            ln_red = [0] * C; ln_var = [0] * C; ln_rcp = [0] * C; ln_xn = [0] * C
            for c in range(C):
                ln_red[c] = sDV.inc(); ln_var[c] = sDV.inc()
                ln_rcp[c] = sDV.inc(); ln_xn[c] = sDV.inc()
            ln_sq = [0] * C; ln_sqrt = [0] * C
            xn_tpS = [0] * C; xn_tpC = [0] * C
            for c in range(C):
                ln_sq[c] = sAC.inc(); ln_sqrt[c] = sAC.inc()
                xn_tpC[c] = stpC[c % 2].inc(16 * 8)
            for c in range(C):
                xn_tpS[c] = stpS[c % 2].inc(16 * 8)
            xn_stash = [sst.inc(16) for _ in range(DB)]

            for c in range(C):
                if c >= 2:
                    P.wait("gpsimd", sDV, ln_xn[c - 2])
                P.op("gpsimd", lambda g, c=c: g.dma_start(
                    out=x_sb[c % 2][:], in_=ap["x"][c * 128:(c + 1) * 128, :]).then_inc(sx[c % 2].h, 16))

            for c in range(C):
                if c >= 1:
                    P.wait("vector", sAC, ln_sqrt[c - 1])
                P.wait("vector", sx[c % 2], xload[c])
                P.op("vector", lambda v, c=c: _dinc(v, v.reduce_sum(
                    out=xsumt[:], in_=x_sb[c % 2][:], axis=AX.X), sDV.h))
                P.wait("vector", sDV, ln_red[c])
                P.wait("vector", sAC, ln_sq[c])
                P.op("vector", lambda v: v.tensor_scalar_mul(out=meant[:], in0=xsumt[:], scalar1=1.0 / D))
                P.op("vector", lambda v: v.drain())
                P.op("vector", lambda v: v.tensor_mul(out=m2t[:], in0=meant[:], in1=meant[:]))
                P.op("vector", lambda v: v.drain())
                P.op("vector", lambda v: v.scalar_tensor_tensor(
                    out=vart[:], in0=xsqt[:], scalar=1.0 / D, in1=m2t[:],
                    op0=ALU.mult, op1=ALU.subtract))
                P.op("vector", lambda v: _dinc(v, v.tensor_copy(out=dvflush[:], in_=vart[:])
                     , sDV.h))
                P.wait("vector", sAC, ln_sqrt[c])
                P.op("vector", lambda v: _dinc(v, v.reciprocal(out=rstdt[:], in_=stdt[:])
                     , sDV.h))
                P.wait("vector", sDV, ln_rcp[c])
                if c >= 2:
                    P.wait("vector", stpS[c % 2], xn_tpS[c - 2])
                    P.wait("vector", stpC[c % 2], xn_tpC[c - 2])
                P.op("vector", lambda v, c=c: v.tensor_scalar(
                    out=xcen[:], in0=x_sb[c % 2][:], scalar1=meant[:],
                    scalar2=None, op0=ALU.subtract))
                P.op("vector", lambda v: v.drain())
                P.op("vector", lambda v, c=c: v.tensor_scalar(
                    out=xn_tm[c % 2][:], in0=xcen[:], scalar1=rstdt[:],
                    scalar2=None, op0=ALU.mult))
                P.op("vector", lambda v: v.drain())
                if "stats" in dbg:
                    for ii, tl in enumerate((xsumt, xsqt, meant, m2t, vart, stdt, rstdt)):
                        P.op("vector", lambda v, c=c, ii=ii, tl=tl: v.tensor_copy(
                            out=statlog[:, c * 8 + ii:c * 8 + ii + 1], in_=tl[:]))
                P.op("vector", lambda v, c=c: _dinc(v, v.reduce_sum(
                    out=flushc[:], in_=xn_tm[c % 2][:], axis=AX.X), sDV.h))

            for c in range(C):
                P.wait("scalar", sx[c % 2], xload[c])
                if c >= 1:
                    P.wait("scalar", sDV, ln_var[c - 1])
                P.op("scalar", lambda s, c=c: s.activation(
                    out=scrap[:], in_=x_sb[c % 2][:], func=AF.Square,
                    accum_out=xsqt[:]))
                P.op("scalar", lambda s: _dinc(s, s.copy(out=acflush[:], in_=xsqt[:]),
                                               sAC.h))
                P.wait("scalar", sDV, ln_var[c])
                if c >= 1:
                    P.wait("scalar", sDV, ln_rcp[c - 1])
                P.op("scalar", lambda s: s.activation(
                    out=stdt[:], in_=vart[:], func=AF.Sqrt, bias=epsc[:]))
                P.op("scalar", lambda s: _dinc(s, s.copy(out=acflush[:], in_=stdt[:]),
                                               sAC.h))


            for c in range(C):
                P.wait("sync", sDV, ln_xn[c])
                for db in range(8):
                    P.op("sync", lambda e, c=c, db=db: e.dma_start_transpose(
                        xn_fm[:, db * T + c * 128: db * T + (c + 1) * 128],
                        xn_tm[c % 2][:, db * 128:(db + 1) * 128]).then_inc(stpS[c % 2].h, 16))
                for db in range(8, 16):
                    P.op("sync", lambda e, c=c, db=db: e.dma_start_transpose(
                        xn_fm[:, db * T + c * 128: db * T + (c + 1) * 128],
                        xn_tm[c % 2][:, db * 128:(db + 1) * 128]).then_inc(stpC[c % 2].h, 16))
            P.wait("sync", stpC[0], xn_tpC[C - 2])
            P.wait("sync", stpC[1], xn_tpC[C - 1])
            P.wait("sync", stpS[0], xn_tpS[C - 2])
            P.wait("sync", stpS[1], xn_tpS[C - 1])
            for db in range(DB):
                P.op("sync", lambda e, db=db: e.dma_start(
                    out=st_xn[db], in_=xn_fm[:, db * T:(db + 1) * T]).then_inc(sst.h, 16))
            P.emit()
            barrier()
        dbg_dump("stxn", st_xn)
        if "stats" in dbg:
            P.op("sync", lambda e: e.dma_start(out=dbg["stats"], in_=statlog[:])
                 .then_inc(sst.h, 16))
            sst.inc(16)
        if "xnfm" in dbg:
            P.op("sync", lambda e: e.dma_start(out=dbg["xnfm"], in_=xn_fm[:])
                 .then_inc(sst.h, 16))
            sst.inc(16)
            barrier()
        P.emit()
        if upto <= 1:
            return nc

        with ExitStack() as ph:
            wv_b = sbuf("wv_b", [128, DB * 1024], BF16, ph)
            wvl = [[swv.inc(16) for _ in range(DB)] for _ in range(2)]
            pe_v = [sPE.inc() for _ in range(2 * C)]
            vev = [sDV.inc() for _ in range(2 * C)]

            P.wait("gpsimd", sDV, ln_xn[C - 1])
            P.wait("gpsimd", sAC, sAC.v)
            for hf in range(2):
                for db in range(DB):
                    if hf == 1 and db == 0:
                        P.wait("gpsimd", sPE, pe_v[C - 1])
                    P.op("gpsimd", lambda g, hf=hf, db=db: g.dma_start(
                        out=wv_b[:, db * 1024:(db + 1) * 1024],
                        in_=ap["wvT"][db][:, hf * 1024:(hf + 1) * 1024]).then_inc(swv.h, 16))

            for hf in range(2):
                for tc in range(C):
                    i = hf * C + tc
                    P.wait("tensor", swv, wvl[hf][DB - 1])
                    if i >= 2:
                        P.wait("tensor", sDV, vev[i - 2])
                    base = (tc % 2) * 1024
                    for db in range(DB):
                        lhs = xn_fm[:, db * T + tc * 128: db * T + (tc + 1) * 128]
                        for sp in range(2):
                            o = pA[:, base + sp * 512: base + (sp + 1) * 512]
                            rhs = wv_b[:, db * 1024 + sp * 512: db * 1024 + (sp + 1) * 512]
                            if db == DB - 1 and sp == 1:
                                P.op("tensor", lambda t, o=o, a=lhs, b=rhs:
                                     t.matmul(o, a, b, start=False, stop=True).then_inc(sPE.h, 1))
                            else:
                                P.op("tensor", lambda t, o=o, a=lhs, b=rhs, st=(db == 0),
                                     fi=(db == DB - 1): t.matmul(o, a, b, start=st, stop=fi))

            for hf in range(2):
                for tc in range(C):
                    i = hf * C + tc
                    P.wait("vector", sPE, pe_v[i])
                    base = (tc % 2) * 1024
                    P.op("vector", lambda v, tc=tc, hf=hf, base=base: _dinc(v, v.tensor_copy(
                        out=v_tm[:, tc * D + hf * 1024: tc * D + (hf + 1) * 1024],
                        in_=pA[:, base: base + 1024]), sDV.h))
            P.emit()
            v_done_dv = sDV.v
        if upto <= 2:
            barrier()
            return nc

        with ExitStack() as ph:
            wq_b = [sbuf(f"wq_b{i}", [128, D], BF16, ph) for i in range(2)]
            wk_b = [sbuf(f"wk_b{i}", [128, D], BF16, ph) for i in range(2)]
            q_sb = sbuf("q_sb", [128, T], BF16, ph)
            k_sb = sbuf("k_sb", [128, T], BF16, ph)
            qp_h = [sbuf(f"qp_h{i}", [128, T], BF16, ph) for i in range(2)]
            kp_fm = [sbuf(f"kp_fm{i}", [128, T], BF16, ph) for i in range(2)]
            kp_tm = [sbuf(f"kp_tm{i}", [128, T], BF16, ph) for i in range(2)]
            tmin = sbuf("tmin", [128, T], BF16, ph)
            texp = sbuf("texp", [128, T], BF16, ph)
            scm = [sbuf(f"scm{i}", [128, 128], BF16, ph) for i in range(2)]
            intra_stg = [sbuf("instg", [128, C * 129], BF16, ph)] * 2
            sloc_stg = [sbuf("slstg", [128, C * 129], BF16, ph)] * 2
            relp = sbuf("relp", [16, 129], F32, ph)

            wq_l = [0] * H; wk_l = [0] * H
            q_done = [0] * H; k_done = [0] * H
            sc_done = [[0] * C for _ in range(H)]
            kv_done = [[0] * C for _ in range(H)]
            num_done = [[0] * C for _ in range(H)]
            for h in range(H):
                wq_l[h] = swq[h % 2].inc(16)
                wk_l[h] = swk[h % 2].inc(16)
            for h in range(H):
                q_done[h] = sPE.inc()
                k_done[h] = sPE.inc()
                for c in range(C):
                    sc_done[h][c] = sPE.inc()
                    kv_done[h][c] = sPE.inc()
                    num_done[h][c] = sPE.inc()
            rel_done = sPE.inc()
            q_ev = [0] * H; texp_qv = [0] * H; k_ev = [0] * H; texp_kv = [0] * H
            for h in range(H):
                q_ev[h] = sAC.inc()
                texp_qv[h] = sAC.inc()
                k_ev[h] = sAC.inc()
                texp_kv[h] = sAC.inc()
            qp_dv = [0] * H; kp_dv = [0] * H
            mask_dv = [[0] * C for _ in range(H)]
            sadd_dv = [[0] * C for _ in range(H)]
            nev_dv = [[0] * C for _ in range(H)]
            for h in range(H):
                sDV.inc()
                qp_dv[h] = sDV.inc()
                sDV.inc()
                kp_dv[h] = sDV.inc()
                for c in range(C):
                    mask_dv[h][c] = sDV.inc()
                    sadd_dv[h][c] = sDV.inc()
                    nev_dv[h][c] = sDV.inc()
            relp_dv = sDV.inc()
            kp_tp = [skp[h % 2].inc(16 * C) for h in range(H)]
            cc_wr = sst.inc(16 * (H + 1))
            qp_st = [0] * H; intra_st = [0] * H; sloc_st = [0] * H
            for h in range(H):
                qp_st[h] = sqst[h % 2].inc(16)
                intra_st[h] = sist.inc(16)
                sloc_st[h] = ssst.inc(16)
            cc_done = sCC.inc()

            P.wait("gpsimd", sPE, pe_v[2 * C - 1])
            P.wait("gpsimd", stpS[1], xn_tpS[C - 1])
            P.wait("gpsimd", stpC[1], xn_tpC[C - 1])
            P.wait("gpsimd", sst, xn_stash[DB - 1])
            for h in range(H):
                if h >= 2:
                    P.wait("gpsimd", sPE, q_done[h - 2])
                P.op("gpsimd", lambda g, h=h: g.dma_start(
                    out=wq_b[h % 2].ap().rearrange("p (a b) -> p a b", a=DB),
                    in_=ap["wqT"][h].rearrange("a p b -> p a b")).then_inc(swq[h % 2].h, 16))
                if h >= 2:
                    P.wait("gpsimd", sPE, k_done[h - 2])
                P.op("gpsimd", lambda g, h=h: g.dma_start(
                    out=wk_b[h % 2].ap().rearrange("p (a b) -> p a b", a=DB),
                    in_=ap["wkT"][h].rearrange("a p b -> p a b")).then_inc(swk[h % 2].h, 16))
            P.wait("gpsimd", sst, cc_wr)
            if no_cc:
                P.op("gpsimd", lambda g: g.engine_nop().then_inc(sCC.h))
            else:
                P.op("gpsimd", lambda g: g.collective_compute(
                    "AllGather", ALU.bypass, replica_groups=[list(range(NCORE))],
                    ins=[cc_in.opt()], outs=[cc_out.opt()]).then_inc(sCC.h))

            for h in range(H):
                P.wait("tensor", swq[h % 2], wq_l[h])
                if h == 0:
                    P.wait("tensor", sDV, v_done_dv)
                if h >= 1:
                    P.wait("tensor", sDV, nev_dv[h - 1][C - 1])
                    P.wait("tensor", sDV, mask_dv[h - 1][C - 1])
                for db in range(DB):
                    lhs = wq_b[h % 2][:, db * 128:(db + 1) * 128]
                    for sp in range(4):
                        o = pA[:, sp * 512:(sp + 1) * 512]
                        rhs = xn_fm[:, db * T + sp * 512: db * T + (sp + 1) * 512]
                        if db == DB - 1 and sp == 3:
                            P.op("tensor", lambda t, o=o, a=lhs, b=rhs:
                                 t.matmul(o, a, b, start=False, stop=True).then_inc(sPE.h, 1))
                        else:
                            P.op("tensor", lambda t, o=o, a=lhs, b=rhs, st=(db == 0),
                                 fi=(db == DB - 1): t.matmul(o, a, b, start=st, stop=fi))
                P.wait("tensor", swk[h % 2], wk_l[h])
                if h >= 1:
                    P.wait("tensor", sDV, sadd_dv[h - 1][C - 1])
                for db in range(DB):
                    lhs = wk_b[h % 2][:, db * 128:(db + 1) * 128]
                    for sp in range(4):
                        o = pB[:, sp * 512:(sp + 1) * 512]
                        rhs = xn_fm[:, db * T + sp * 512: db * T + (sp + 1) * 512]
                        if db == DB - 1 and sp == 3:
                            P.op("tensor", lambda t, o=o, a=lhs, b=rhs:
                                 t.matmul(o, a, b, start=False, stop=True).then_inc(sPE.h, 1))
                        else:
                            P.op("tensor", lambda t, o=o, a=lhs, b=rhs, st=(db == 0),
                                 fi=(db == DB - 1): t.matmul(o, a, b, start=st, stop=fi))
                P.wait("tensor", sDV, qp_dv[h])
                P.wait("tensor", sDV, kp_dv[h])
                P.wait("tensor", skp[h % 2], kp_tp[h])
                P.wait("tensor", sAC, q_ev[h])
                P.wait("tensor", sAC, k_ev[h])
                for c in range(C):
                    cs = slice(c * 128, (c + 1) * 128)
                    vt = v_tm[:, c * D + h * 128: c * D + (h + 1) * 128]
                    if c >= 2:
                        P.wait("tensor", sDV, mask_dv[h][c - 2])
                    P.op("tensor", lambda t, o=pA[:, (c % 2) * 512:(c % 2) * 512 + 128],
                         a=kp_fm[h % 2][:, cs], b=qp_h[h % 2][:, cs]:
                         t.matmul(o, a, b, start=True, stop=True).then_inc(sPE.h, 1))
                    if c >= 2:
                        P.wait("tensor", sDV, sadd_dv[h][c - 2])
                    P.op("tensor", lambda t, o=pB[:, (c % 2) * 512:(c % 2) * 512 + 128],
                         a=kp_tm[h % 2][:, cs], b=vt:
                         t.matmul(o, a, b, start=True, stop=True))
                    P.op("tensor", lambda t, o=pB[:, (c % 2) * 512 + 128:(c % 2) * 512 + 129],
                         a=kp_tm[h % 2][:, cs]:
                         t.matmul(o, a, onesb[:], start=True, stop=True).then_inc(sPE.h, 1))
                    P.wait("tensor", sDV, mask_dv[h][c])
                    if c >= 2:
                        P.wait("tensor", sDV, nev_dv[h][c - 2])
                    P.op("tensor", lambda t, o=pA[:, (2 + c % 2) * 512:(2 + c % 2) * 512 + 128],
                         a=scm[c % 2][:], b=vt:
                         t.matmul(o, a, b, start=True, stop=True))
                    P.op("tensor", lambda t, o=pA[:, (2 + c % 2) * 512 + 128:(2 + c % 2) * 512 + 129],
                         a=scm[c % 2][:]:
                         t.matmul(o, a, onesb[:], start=True, stop=True).then_inc(sPE.h, 1))
            P.wait("tensor", sAC, q_ev[H - 1])
            P.wait("tensor", sDV, sadd_dv[H - 1][C - 1])
            P.wait("tensor", sDV, nev_dv[H - 1][C - 1])
            P.wait("tensor", sDV, mask_dv[H - 1][C - 1])
            for h in range(H):
                o = pA[0:1, h * 4:(h + 1) * 4]
                rhs = mnT_f[:, h * 4:(h + 1) * 4]
                if h == H - 1:
                    P.op("tensor", lambda t, o=o, a=qsum[:, h:h + 1], b=rhs:
                         t.matmul(o, a, b, start=True, stop=True).then_inc(sPE.h, 1))
                else:
                    P.op("tensor", lambda t, o=o, a=qsum[:, h:h + 1], b=rhs:
                         t.matmul(o, a, b, start=True, stop=True))

            for h in range(H):
                P.wait("scalar", sPE, q_done[h])
                if h >= 1:
                    P.wait("scalar", sDV, qp_dv[h - 1])
                P.op("scalar", lambda s, h=h: _dinc(s, s.activation(
                    out=q_sb[:], in_=pA[:, 0:2048], func=AF.Copy,
                    accum_out=qsum[:, h:h + 1]), sAC.h))
                P.wait("scalar", sDV, qp_dv[h] - 1)
                P.op("scalar", lambda s: _dinc(s, s.activation(
                    out=texp[:], in_=tmin[:], func=AF.Exp), sAC.h))
                P.wait("scalar", sPE, k_done[h])
                if h >= 1:
                    P.wait("scalar", sDV, kp_dv[h - 1])
                P.op("scalar", lambda s, h=h: _dinc(s, s.activation(
                    out=k_sb[:], in_=pB[:, 0:2048], func=AF.Copy), sAC.h))
                P.wait("scalar", sDV, kp_dv[h] - 1)
                P.op("scalar", lambda s: _dinc(s, s.activation(
                    out=texp[:], in_=tmin[:], func=AF.Exp), sAC.h))
                P.wait("scalar", sDV, qp_dv[h])
                P.op("scalar", lambda e, h=h: e.dma_start(
                    out=st_qp[h], in_=qp_h[h % 2][:]).then_inc(sqst[h % 2].h, 16))
                P.wait("scalar", sDV, nev_dv[h][C - 1])
                P.op("scalar", lambda e, h=h: e.dma_start(
                    out=st_intra[h], in_=intra_stg[h % 2][:]).then_inc(sist.h, 16))
                P.wait("scalar", sDV, sadd_dv[h][C - 1])
                P.op("scalar", lambda e, h=h: e.dma_start(
                    out=st_sloc[h], in_=sloc_stg[h % 2][:]).then_inc(ssst.h, 16))

            P.op("vector", lambda v: v.memset(relp[:], 0.0))
            for h in range(H):
                P.wait("vector", sAC, q_ev[h])
                P.op("vector", lambda v: _dinc(v, v.tensor_scalar_min(
                    out=tmin[:], in0=q_sb[:], scalar1=0.0), sDV.h))
                P.wait("vector", sAC, texp_qv[h])
                if h >= 2:
                    P.wait("vector", sqst[h % 2], qp_st[h - 2])
                    P.wait("vector", sPE, sc_done[h - 2][C - 1])
                    P.wait("vector", sPE, num_done[h - 2][C - 1])
                P.op("vector", lambda v, h=h: _dinc(v, v.scalar_tensor_tensor(
                    out=qp_h[h % 2][:], in0=q_sb[:], scalar=0.0, in1=texp[:],
                    op0=ALU.max, op1=ALU.add), sDV.h))
                P.wait("vector", sAC, k_ev[h])
                P.op("vector", lambda v: _dinc(v, v.tensor_scalar_min(
                    out=tmin[:], in0=k_sb[:], scalar1=0.0), sDV.h))
                P.wait("vector", sAC, texp_kv[h])
                if h >= 2:
                    P.wait("vector", skp[h % 2], kp_tp[h - 2])
                P.op("vector", lambda v, h=h: _dinc(v, v.scalar_tensor_tensor(
                    out=kp_fm[h % 2][:], in0=k_sb[:], scalar=0.0, in1=texp[:],
                    op0=ALU.max, op1=ALU.add), sDV.h))
                if h >= 1:
                    P.wait("vector", sist, intra_st[h - 1])
                    P.wait("vector", ssst, sloc_st[h - 1])
                for c in range(C):
                    P.wait("vector", sPE, sc_done[h][c])
                    if c >= 2:
                        P.wait("vector", sPE, num_done[h][c - 2])
                    elif h >= 1:
                        P.wait("vector", sPE, num_done[h - 1][C - 2 + c])
                    P.op("vector", lambda v, c=c: _dinc(v, v.tensor_tensor(
                        out=scm[c % 2][:], in0=pA[:, (c % 2) * 512:(c % 2) * 512 + 128],
                        in1=trilb[:], op=ALU.mult), sDV.h))
                    P.op("vector", lambda v, h=h, c=c: v.tensor_copy(
                        out=sloc_stg[h % 2][:, c * 129:(c + 1) * 129],
                        in_=s_run[:, h * 129:(h + 1) * 129]))
                    P.op("vector", lambda v: v.drain())
                    P.wait("vector", sPE, kv_done[h][c])
                    P.op("vector", lambda v, h=h, c=c: _dinc(v, v.tensor_add(
                        out=s_run[:, h * 129:(h + 1) * 129],
                        in0=s_run[:, h * 129:(h + 1) * 129],
                        in1=pB[:, (c % 2) * 512:(c % 2) * 512 + 129]), sDV.h))
                    P.wait("vector", sPE, num_done[h][c])
                    P.op("vector", lambda v, h=h, c=c: _dinc(v, v.tensor_copy(
                        out=intra_stg[h % 2][:, c * 129:(c + 1) * 129],
                        in_=pA[:, (2 + c % 2) * 512:(2 + c % 2) * 512 + 129]), sDV.h))
            P.wait("vector", sPE, rel_done)
            P.op("vector", lambda v: _dinc(v, v.tensor_scalar_mul(
                out=relp[0:1, 0:H * M], in0=pA[0:1, 0:H * M],
                scalar1=SCALE / (T * NCORE)), sDV.h))

            for h in range(H):
                P.wait("sync", sDV, kp_dv[h])
                if h >= 2:
                    P.wait("sync", sPE, kv_done[h - 2][C - 1])
                for c in range(C):
                    P.op("sync", lambda e, h=h, c=c: e.dma_start_transpose(
                        kp_tm[h % 2][:, c * 128:(c + 1) * 128],
                        kp_fm[h % 2][:, c * 128:(c + 1) * 128]).then_inc(skp[h % 2].h, 16))
            for h in range(H):
                P.wait("sync", sDV, sadd_dv[h][C - 1])
                P.op("sync", lambda e, h=h: e.dma_start(
                    out=cc_in[h * 128:(h + 1) * 128, :],
                    in_=s_run[:, h * 129:(h + 1) * 129]).then_inc(sst.h, 16))
            P.wait("sync", sDV, relp_dv)
            P.op("sync", lambda e: e.dma_start(
                out=cc_in[H * 128:H * 128 + 16, :], in_=relp[:, 0:129]).then_inc(sst.h, 16))
            P.emit()
            barrier()
        dbg_dump("stqp", st_qp)
        dbg_dump("stintra", st_intra)
        dbg_dump("stsloc", st_sloc)
        dbg_dump("srun", s_run[:])
        P.emit()
        attS.close()
        if upto <= 3:
            return nc

        with ExitStack() as ph:
            h_buf = sbuf("h_buf", [128, IB * TTN], BF16, ph)
            xnsl = sbuf("xnsl", [128, HB * TTN], BF16, ph)
            gw_q = [sbuf(f"gw_q{i}", [128, NBW * HB * 128], BF16, ph) for i in range(2)]
            uw_q = [sbuf(f"uw_q{i}", [128, NBW * HB * 128], BF16, ph) for i in range(2)]
            dw_q = [sbuf(f"dw_q{i}", [128, IB * 128], BF16, ph) for i in range(2)]
            g_sb = [sbuf(f"g_sb{i}", [128, TTN], BF16, ph) for i in range(2)]
            u_sb = [sbuf(f"u_sb{i}", [128, TTN], BF16, ph) for i in range(2)]
            gfst = [sbuf(f"gfst{i}", [128, 512], BF16, ph) for i in range(2)]
            gftst = [sbuf(f"gftst{i}", [128, 512], BF16, ph) for i in range(2)]
            gst = sbuf("gst", [128, IB], F32, ph)
            ust = sbuf("ust", [128, IB], F32, ph)
            dsc = sbuf("dsc", [128, HB], F32, ph)

            for dst, src in ((gst[:], ap["gsT"]), (ust[:], ap["usT"]), (dsc[:], ap["dsT"])):
                P.op("gpsimd", lambda g, d=dst, s=src: g.dma_start(out=d, in_=s).then_inc(ssc.h, 16))
            ld_scales = ssc.inc(48)

            prev = None
            for tt in range(NTT):
                xnl = [sxn.inc(16) for _ in range(HB)]
                gwl = [0] * NWB; uwl = [0] * NWB
                for b in range(NWB):
                    gwl[b] = sgw[b % 2].inc(16)
                    uwl[b] = suw[b % 2].inc(16)
                dwl = [sdw[hb % 2].inc(16) for hb in range(HB)]
                g_done = [0] * IB; u_done = [0] * IB
                for ib in range(IB):
                    g_done[ib] = sPE.inc()
                    u_done[ib] = sPE.inc()
                dn_done = [[sPE.inc() for _ in range(NSP)] for _ in range(HB)]
                gwsh = [0] * NWB; uwsh = [0] * NWB
                uev = [0] * IB; h_done = [0] * IB
                for ib in range(IB):
                    if tt == 0 and ib % NBW == 0:
                        gwsh[ib // NBW] = sDV.inc()
                        uwsh[ib // NBW] = sDV.inc()
                    uev[ib] = sDV.inc()
                    h_done[ib] = sDV.inc()
                dwsh = [sDV.inc() if tt == 0 else 0 for _ in range(HB)]
                g_ev = [sAC.inc() for _ in range(IB)]
                scw_g = [0] * NWB; scw_u = [0] * NWB
                if tt == 0:
                    for b in range(NWB):
                        scw_g[b] = sscw[b % 2].inc(16)
                        scw_u[b] = sscw[b % 2].inc(16)
                gf_ev = [[0] * NSP for _ in range(HB)]
                scw_d = [0] * HB
                for hb in range(HB):
                    if tt == 0:
                        scw_d[hb] = sdww[hb % 2].inc(16)
                    for ts in range(NSP):
                        gf_ev[hb][ts] = sAC.inc()
                gf_tpa = [[0] * NSP for _ in range(HB)]
                gf_tp = [[0] * NSP for _ in range(HB)]
                for hb in range(HB):
                    for ts in range(NSP):
                        par = (hb * NSP + ts) % 2
                        gf_tpa[hb][ts] = sgf[par].inc(16 * 4)
                        gf_tp[hb][ts] = sgf[par].inc(16 * 4)

                if prev is not None:
                    P.wait("gpsimd", sPE, prev["u_done"][IB - 1])
                for hb in range(HB):
                    P.op("gpsimd", lambda g, hb=hb, tt=tt: g.dma_start(
                        out=xnsl[:, hb * TTN:(hb + 1) * TTN],
                        in_=st_xn[hb][:, tt * TTN:(tt + 1) * TTN]).then_inc(sxn.h, 16))
                for b in range(NWB):
                    if b >= 2:
                        P.wait("gpsimd", sPE, g_done[(b - 2) * NBW + NBW - 1])
                        if tt == 0:
                            P.wait("gpsimd", sscw[b % 2], scw_g[b - 2])
                    elif prev is not None:
                        P.wait("gpsimd", sPE, prev["g_done"][IB - 1])
                        if tt == 1:
                            P.wait("gpsimd", sscw[b % 2], prev["scw_g"][NWB - 2 + b])
                    if tt == 0:
                        P.op("gpsimd", lambda g, b=b: g.dma_start(
                            out=gw_q[b % 2].ap().rearrange("p (n a b) -> p n a b", n=NBW, a=HB),
                            in_=ap["gw"][b * NBW:(b + 1) * NBW].rearrange("n a p b -> p n a b"))
                            .then_inc(sgw[b % 2].h, 16))
                    else:
                        P.op("gpsimd", lambda g, b=b: g.dma_start(
                            out=gw_q[b % 2].ap().rearrange("p (n f) -> p n f", n=NBW),
                            in_=scr_gw[b * NBW:(b + 1) * NBW].rearrange("n p f -> p n f"))
                            .then_inc(sgw[b % 2].h, 16))
                    if b >= 2:
                        P.wait("gpsimd", sPE, u_done[(b - 2) * NBW + NBW - 1])
                        if tt == 0:
                            P.wait("gpsimd", sscw[b % 2], scw_u[b - 2])
                    elif prev is not None:
                        P.wait("gpsimd", sPE, prev["u_done"][IB - 1])
                        if tt == 1:
                            P.wait("gpsimd", sscw[b % 2], prev["scw_u"][NWB - 2 + b])
                    if tt == 0:
                        P.op("gpsimd", lambda g, b=b: g.dma_start(
                            out=uw_q[b % 2].ap().rearrange("p (n a b) -> p n a b", n=NBW, a=HB),
                            in_=ap["uw"][b * NBW:(b + 1) * NBW].rearrange("n a p b -> p n a b"))
                            .then_inc(suw[b % 2].h, 16))
                    else:
                        P.op("gpsimd", lambda g, b=b: g.dma_start(
                            out=uw_q[b % 2].ap().rearrange("p (n f) -> p n f", n=NBW),
                            in_=scr_uw[b * NBW:(b + 1) * NBW].rearrange("n p f -> p n f"))
                            .then_inc(suw[b % 2].h, 16))
                for hb in range(HB):
                    if hb >= 2:
                        P.wait("gpsimd", sPE, dn_done[hb - 2][NSP - 1])
                        if tt == 0:
                            P.wait("gpsimd", sdww[hb % 2], scw_d[hb - 2])
                    elif prev is not None:
                        P.wait("gpsimd", sPE, prev["dn_done"][HB - 1][NSP - 1])
                        if tt == 1:
                            P.wait("gpsimd", sdww[hb % 2], prev["scw_d"][HB - 2 + hb])
                    if tt == 0:
                        P.op("gpsimd", lambda g, hb=hb: g.dma_start(
                            out=dw_q[hb % 2].ap().rearrange("p (a b) -> p a b", a=IB),
                            in_=ap["dw"][hb].rearrange("a p b -> p a b")).then_inc(sdw[hb % 2].h, 16))
                    else:
                        P.op("gpsimd", lambda g, hb=hb: g.dma_start(
                            out=dw_q[hb % 2][:], in_=scr_dw[hb]).then_inc(sdw[hb % 2].h, 16))

                for ib in range(IB):
                    if tt == 0 and ib % NBW == 0:
                        b = ib // NBW
                        P.wait("vector", sgw[b % 2], gwl[b])
                        P.op("vector", lambda v, b=b: _dinc(v, v.tensor_scalar_add(
                            out=gw_q[b % 2][:], in0=gw_q[b % 2][:], scalar1=-128.0)
                            , sDV.h))
                        P.wait("vector", suw[b % 2], uwl[b])
                        P.op("vector", lambda v, b=b: _dinc(v, v.tensor_scalar_add(
                            out=uw_q[b % 2][:], in0=uw_q[b % 2][:], scalar1=-128.0)
                            , sDV.h))
                    if ib == 0 and prev is not None:
                        P.wait("vector", sPE, prev["dn_done"][HB - 1][NSP - 1])
                    P.wait("vector", sPE, u_done[ib])
                    ubase = TTN * (ib % 2)
                    P.op("vector", lambda v, ib=ib, ubase=ubase: _dinc(v, v.tensor_scalar_mul(
                        out=u_sb[ib % 2][:], in0=pB[:, ubase:ubase + TTN],
                        scalar1=ust[:, ib:ib + 1]), sDV.h))
                    P.wait("vector", sAC, g_ev[ib])
                    P.op("vector", lambda v, ib=ib: _dinc(v, v.tensor_mul(
                        out=h_buf[:, ib * TTN:(ib + 1) * TTN],
                        in0=g_sb[ib % 2][:], in1=u_sb[ib % 2][:]), sDV.h))
                if tt == 0:
                    for hb in range(HB):
                        P.wait("vector", sdw[hb % 2], dwl[hb])
                        P.op("vector", lambda v, hb=hb: _dinc(v, v.tensor_scalar_add(
                            out=dw_q[hb % 2][:], in0=dw_q[hb % 2][:], scalar1=-128.0)
                            , sDV.h))

                for ib in range(IB):
                    gbase = TTN * (ib % 2)
                    bi, bo = ib // NBW, ib % NBW
                    if ib == 0:
                        P.wait("tensor", sxn, xnl[HB - 1])
                        P.wait("tensor", ssc, ld_scales)
                        if prev is not None:
                            P.wait("tensor", sAC, prev["gf_ev"][HB - 1][NSP - 1])
                            P.wait("tensor", sDV, prev["uev"][IB - 1])
                    if bo == 0:
                        if tt == 0:
                            P.wait("tensor", sDV, gwsh[bi])
                        else:
                            P.wait("tensor", sgw[bi % 2], gwl[bi])
                    if ib >= 2:
                        P.wait("tensor", sAC, g_ev[ib - 2])
                    for hb in range(HB):
                        lhs = gw_q[bi % 2][:, (bo * HB + hb) * 128:(bo * HB + hb + 1) * 128]
                        for sp in range(NSP):
                            o = pA[:, gbase + sp * 512: gbase + (sp + 1) * 512]
                            rhs = xnsl[:, hb * TTN + sp * 512: hb * TTN + (sp + 1) * 512]
                            if hb == HB - 1 and sp == NSP - 1:
                                P.op("tensor", lambda t, o=o, a=lhs, b=rhs:
                                     t.matmul(o, a, b, start=False, stop=True).then_inc(sPE.h, 1))
                            else:
                                P.op("tensor", lambda t, o=o, a=lhs, b=rhs, st=(hb == 0),
                                     fi=(hb == HB - 1): t.matmul(o, a, b, start=st, stop=fi))
                    if bo == 0:
                        if tt == 0:
                            P.wait("tensor", sDV, uwsh[bi])
                        else:
                            P.wait("tensor", suw[bi % 2], uwl[bi])
                    if ib >= 2:
                        P.wait("tensor", sDV, uev[ib - 2])
                    for hb in range(HB):
                        lhs = uw_q[bi % 2][:, (bo * HB + hb) * 128:(bo * HB + hb + 1) * 128]
                        for sp in range(NSP):
                            o = pB[:, gbase + sp * 512: gbase + (sp + 1) * 512]
                            rhs = xnsl[:, hb * TTN + sp * 512: hb * TTN + (sp + 1) * 512]
                            if hb == HB - 1 and sp == NSP - 1:
                                P.op("tensor", lambda t, o=o, a=lhs, b=rhs:
                                     t.matmul(o, a, b, start=False, stop=True).then_inc(sPE.h, 1))
                            else:
                                P.op("tensor", lambda t, o=o, a=lhs, b=rhs, st=(hb == 0),
                                     fi=(hb == HB - 1): t.matmul(o, a, b, start=st, stop=fi))
                if tt == 0 and "hbuf0" in dbg:
                    P.wait("sync", sDV, h_done[IB - 1])
                    P.op("sync", lambda e: e.dma_start(out=dbg["hbuf0"], in_=h_buf[:])
                         .then_inc(sst.h, 16))
                    sst.inc(16)
                if tt == 0 and "xnsl0" in dbg:
                    P.wait("sync", sxn, xnl[HB - 1])
                    P.op("sync", lambda e: e.dma_start(out=dbg["xnsl0"], in_=xnsl[:])
                         .then_inc(sst.h, 16))
                    sst.inc(16)
                P.wait("tensor", sDV, h_done[IB - 1])
                P.wait("tensor", sAC, g_ev[IB - 1])
                for hb in range(HB):
                    for ts in range(NSP):
                        idx = hb * NSP + ts
                        par = idx % 2
                        if ts == 0:
                            if tt == 0:
                                P.wait("tensor", sDV, dwsh[hb])
                            else:
                                P.wait("tensor", sdw[hb % 2], dwl[hb])
                        if idx >= 2:
                            P.wait("tensor", sAC, gf_ev[(idx - 2) // NSP][(idx - 2) % NSP])
                        for ib in range(IB):
                            lhs = dw_q[hb % 2][:, ib * 128:(ib + 1) * 128]
                            rhs = h_buf[:, ib * TTN + ts * 512: ib * TTN + (ts + 1) * 512]
                            o = pA[:, par * 512:(par + 1) * 512]
                            if ib == IB - 1:
                                P.op("tensor", lambda t, o=o, a=lhs, b=rhs:
                                     t.matmul(o, a, b, start=False, stop=True).then_inc(sPE.h, 1))
                            else:
                                P.op("tensor", lambda t, o=o, a=lhs, b=rhs, st=(ib == 0):
                                     t.matmul(o, a, b, start=st, stop=False))

                for ib in range(IB):
                    P.wait("scalar", sPE, g_done[ib])
                    if ib >= 2:
                        P.wait("scalar", sDV, h_done[ib - 2])
                    elif prev is not None:
                        P.wait("scalar", sDV, prev["h_done"][IB - 1])
                    gbase = TTN * (ib % 2)
                    P.op("scalar", lambda s, ib=ib, gbase=gbase: _dinc(s, s.activation(
                        out=g_sb[ib % 2][:], in_=pA[:, gbase:gbase + TTN], func=AF.Silu,
                        scale=gst[:, ib:ib + 1]), sAC.h))
                    if tt == 0 and ib % NBW == 0:
                        b = ib // NBW
                        P.wait("scalar", sDV, gwsh[b])
                        P.op("scalar", lambda e, b=b: e.dma_start(
                            out=scr_gw[b * NBW:(b + 1) * NBW].rearrange("n p f -> p n f"),
                            in_=gw_q[b % 2].ap().rearrange("p (n f) -> p n f", n=NBW))
                            .then_inc(sscw[b % 2].h, 16))
                        P.wait("scalar", sDV, uwsh[b])
                        P.op("scalar", lambda e, b=b: e.dma_start(
                            out=scr_uw[b * NBW:(b + 1) * NBW].rearrange("n p f -> p n f"),
                            in_=uw_q[b % 2].ap().rearrange("p (n f) -> p n f", n=NBW))
                            .then_inc(sscw[b % 2].h, 16))
                for hb in range(HB):
                    if tt == 0:
                        P.wait("scalar", sDV, dwsh[hb])
                        P.op("scalar", lambda e, hb=hb: e.dma_start(
                            out=scr_dw[hb], in_=dw_q[hb % 2][:]).then_inc(sdww[hb % 2].h, 16))
                    for ts in range(NSP):
                        idx = hb * NSP + ts
                        par = idx % 2
                        P.wait("scalar", sPE, dn_done[hb][ts])
                        if idx >= 2:
                            P.wait("scalar", sgf[idx % 2], gf_tp[(idx - 2) // NSP][(idx - 2) % NSP])
                        elif prev is not None:
                            P.wait("scalar", sgf[idx % 2], prev["gf_tp"][HB - 1][NSP - 1])
                        P.op("scalar", lambda s, hb=hb, par=par: _dinc(s, s.activation(
                            out=gfst[par][:], in_=pA[:, par * 512:(par + 1) * 512],
                            func=AF.Silu, scale=dsc[:, hb:hb + 1]), sAC.h))

                for hb in range(HB):
                    for ts in range(NSP):
                        idx = hb * NSP + ts
                        par = idx % 2
                        P.wait("sync", sAC, gf_ev[hb][ts])
                        if idx >= 2:
                            P.wait("sync", sgf[par], gf_tp[(idx - 2) // NSP][(idx - 2) % NSP])
                        elif prev is not None:
                            P.wait("sync", sgf[par], prev["gf_tp"][HB - 1][NSP - 1])
                        for tc4 in range(4):
                            P.op("sync", lambda e, par=par, tc4=tc4: e.dma_start_transpose(
                                gftst[par][:, tc4 * 128:(tc4 + 1) * 128],
                                gfst[par][:, tc4 * 128:(tc4 + 1) * 128]).then_inc(sgf[par].h, 16))
                        P.wait("sync", sgf[par], gf_tpa[hb][ts])
                        for tc4 in range(4):
                            tchunk = tt * (TTN // 128) + ts * 4 + tc4
                            P.op("sync", lambda e, par=par, tc4=tc4, tchunk=tchunk, hb=hb:
                                 e.dma_start(
                                     out=st_gf[tchunk][:, hb * 128:(hb + 1) * 128],
                                     in_=gftst[par][:, tc4 * 128:(tc4 + 1) * 128])
                                 .then_inc(sgf[par].h, 16))
                P.emit()
                prev = dict(g_done=g_done, u_done=u_done, dn_done=dn_done,
                            gf_ev=gf_ev, gf_tp=gf_tp, h_done=h_done, uev=uev,
                            scw_g=scw_g, scw_u=scw_u, scw_d=scw_d)
            barrier()
        dbg_dump("stgf", st_gf)
        P.emit()
        if upto <= 4:
            return nc

        with ExitStack() as ph5:
            mem_bf = sbuf("mem_bf", [128, H * 512], BF16, Gs)
            s_runb = sbuf("s_runb", [128, H * 129], BF16, Gs)
            w_oT = sbuf("w_oT", [128, DB * D], BF16, Gs)
            W_b = sbuf("W_b", [128, H * M], F32, Gs)
            WG = sbuf("WG", [128, H * M], F32, Gs)
            G_sb = sbuf("G_sb", [128, H], F32, Gs)
            Gc_sb = sbuf("Gc_sb", [128, H], F32, Gs)
            Geff = sbuf("Geff", [128, H], F32, Gs)

            sgath = [sbuf(f"sgath{i}", [128, 8 * 129], F32, ph5) for i in range(2)]
            relg = sbuf("relg", [1, 8 * H * M], F32, ph5)
            rel_all = sbuf("rel_all", [1, H * M], F32, ph5)
            rel_t = sbuf("rel_t", [H, M], F32, ph5)
            mnr_sb = sbuf("mnr_sb", [M * H, 128], F32, ph5)
            mnr1 = sbuf("mnr1", [M * H, 1], F32, ph5)
            msel_sb = sbuf("msel_sb", [M * H, M], F32, ph5)
            e_sb = sbuf("e_sb", [1, M], F32, ph5)
            esum = sbuf("esum", [1, 1], F32, ph5)
            allf = sbuf("allf", [1, 1], F32, ph5)
            notall = sbuf("notall", [1, 1], F32, ph5)
            e_b = sbuf("e_b", [H, M], F32, ph5)
            one_m = sbuf("one_m", [H, M], F32, ph5)
            relm = sbuf("relm", [H, M], F32, ph5)
            rmax = sbuf("rmax", [H, 1], F32, ph5)
            negmax = sbuf("negmax", [H, 1], F32, ph5)
            wex = sbuf("wex", [H, M], F32, ph5)
            ssum = sbuf("ssum", [H, 1], F32, ph5)
            srq = sbuf("srq", [H, 1], F32, ph5)
            W_sb = sbuf("W_sb", [H, M], F32, ph5)
            gate_sb = sbuf("gate_sb", [1, H], F32, ph5)
            gsig = sbuf("gsig", [1, H], F32, ph5)
            gsig2 = sbuf("gsig2", [1, H], F32, ph5)
            maskcol = sbuf("maskcol", [128, 8], F32, ph5)

            P.wait("gpsimd", sCC, 1)

            def gld(sem, dst, src, **kw):
                P.op("gpsimd", lambda g, d=dst, s=src, sem=sem, kw=kw:
                     g.dma_start(out=d, in_=s, **kw).then_inc(sem.h, 16))
                return sem.inc(16)
            ccv = cc_out.rearrange("(r q) f -> r q f", q=CCR)
            ld_relg = gld(srelg, relg.ap().rearrange("a (r f) -> a r f", r=8),
                          ccv[:, H * 128:H * 128 + 1, 0:H * M].rearrange("r a f -> a r f"))
            ld_mnr = gld(smnr, mnr_sb[:], ap["mnrows"])
            ld_gate = gld(sgt2, gate_sb[:], ap["gate"])
            ld_msel = gld(smsl, msel_sb[:], ap["msel"])
            ld_sg = [0] * H
            pe_mkc = sPE.inc()
            pe_emp = sPE.inc()
            pe_eb = sPE.inc()
            pe_wb = sPE.inc()
            pe_gb = sPE.inc()
            pe_geff = sPE.inc()
            ac_wex = sAC.inc()
            ac_gsig = sAC.inc()
            dv_rel = sDV.inc()
            dv_mnred = sDV.inc()
            dv_mkc = sDV.inc()
            dv_e = sDV.inc()
            dv_esum = sDV.inc()
            dv_notall = sDV.inc()
            dv_sinit = [0] * H
            for h in range(H):
                dv_sinit[h] = sDV.inc()
            dv_srunb = sDV.inc()
            dv_ebm = sDV.inc()
            dv_relm = sDV.inc()
            dv_rmax = sDV.inc()
            dv_negmax = sDV.inc()
            dv_ssum = sDV.inc()
            dv_srq = sDV.inc()
            dv_wsb = sDV.inc()
            dv_wbev = sDV.inc()
            dv_gsig2 = sDV.inc()
            dv_gbev = sDV.inc()
            dv_geffev = sDV.inc()
            sy_relbuf = srel.inc(16)
            sy_wbuf = swbf.inc(16)

            for h in range(H):
                if h >= 2:
                    P.wait("gpsimd", sDV, dv_sinit[h - 2])
                ld_sg[h] = gld(
                    ssg[h % 2],
                    sgath[h % 2].ap().rearrange("p (r f) -> p r f", r=8),
                    ccv[:, h * 128:(h + 1) * 128, :].rearrange("r p f -> p r f"))
            ld_mem = [0] * H
            for h in range(H):
                ld_mem[h] = gld(
                    sGt,
                    mem_bf[:, h * 512:(h + 1) * 512].rearrange("p (m e) -> p m e", m=M),
                    ap["mem"][:, h].rearrange("m p e -> p m e"))
            ld_wo = [0] * DB
            for db in range(DB):
                ld_wo[db] = gld(sGt, w_oT[:, db * D:(db + 1) * D], ap["woT"][db])
            P.wait("gpsimd", srel, sy_relbuf)
            ld_relt = gld(srlt, rel_t[:], relbuf.rearrange("a (h m) -> (a h) m", h=H))
            P.wait("gpsimd", swbf, sy_wbuf)
            ld_wfl = gld(swfl, relg[:, 0:H * M], wbuf)

            P.op("tensor", lambda t: t.matmul(pB[:, 0:8], onesr[:], rmk[:],
                                              start=True, stop=True).then_inc(sPE.h, 1))
            P.wait("tensor", sDV, dv_mnred)
            P.wait("tensor", smsl, ld_msel)
            P.op("tensor", lambda t: t.matmul(pA[0:1, 0:4], mnr1[:], msel_sb[:],
                                              start=True, stop=True).then_inc(sPE.h, 1))
            P.wait("tensor", sDV, dv_e)
            P.op("tensor", lambda t: t.matmul(pA[0:H, 512:516], onesr[:, 0:H], e_sb[:],
                                              start=True, stop=True).then_inc(sPE.h, 1))
            P.wait("tensor", swfl, ld_wfl)
            P.op("tensor", lambda t: t.matmul(pB[:, 512:512 + H * M], onesr[:], relg[:, 0:H * M],
                                              start=True, stop=True).then_inc(sPE.h, 1))
            P.wait("tensor", sAC, ac_gsig)
            P.op("tensor", lambda t: t.matmul(pB[:, 1024:1024 + H], onesr[:], gsig[:],
                                              start=True, stop=True).then_inc(sPE.h, 1))
            P.wait("tensor", sDV, dv_gsig2)
            P.op("tensor", lambda t: t.matmul(pB[:, 1536:1536 + H], onesr[:], gsig2[:],
                                              start=True, stop=True).then_inc(sPE.h, 1))

            P.wait("vector", srelg, ld_relg)
            P.op("vector", lambda v: v.tensor_add(out=rel_all[:], in0=relg[:, 0:64],
                                                  in1=relg[:, 64:128]))
            P.op("vector", lambda v: v.drain())
            for r in range(2, 8):
                if r == 7:
                    P.op("vector", lambda v, r=r: _dinc(v, v.tensor_add(
                        out=rel_all[:], in0=rel_all[:],
                        in1=relg[:, r * 64:(r + 1) * 64]), sDV.h))
                else:
                    P.op("vector", lambda v, r=r: v.tensor_add(
                        out=rel_all[:], in0=rel_all[:], in1=relg[:, r * 64:(r + 1) * 64]))
                    P.op("vector", lambda v: v.drain())
            P.wait("vector", smnr, ld_mnr)
            P.op("vector", lambda v: _dinc(v, v.reduce_sum(
                out=mnr1[:], in_=mnr_sb[:], axis=AX.X), sDV.h))
            P.wait("vector", sPE, pe_mkc)
            P.op("vector", lambda v: _dinc(v, v.tensor_copy(out=maskcol[:], in_=pB[:, 0:8])
                 , sDV.h))
            P.wait("vector", sPE, pe_emp)
            P.op("vector", lambda v: _dinc(v, v.tensor_single_scalar(
                out=e_sb[:], in_=pA[0:1, 0:4], scalar=1e-6, op=ALU.is_lt), sDV.h))
            P.wait("vector", sDV, dv_e)
            P.op("vector", lambda v: _dinc(v, v.reduce_sum(out=esum[:], in_=e_sb[:], axis=AX.X)
                 , sDV.h))
            P.wait("vector", sDV, dv_esum)
            P.op("vector", lambda v: v.tensor_single_scalar(
                out=allf[:], in_=esum[:], scalar=3.5, op=ALU.is_ge))
            P.op("vector", lambda v: v.drain())
            P.op("vector", lambda v: _dinc(v, v.tensor_scalar(
                out=notall[:], in0=allf[:], scalar1=-1.0, scalar2=1.0,
                op0=ALU.mult, op1=ALU.add), sDV.h))
            for h in range(H):
                P.wait("vector", ssg[h % 2], ld_sg[h])
                P.wait("vector", sDV, dv_mkc)
                P.op("vector", lambda v, h=h: v.tensor_scalar_mul(
                    out=s_run[:, h * 129:(h + 1) * 129],
                    in0=sgath[h % 2][:, 0:129], scalar1=maskcol[:, 0:1]))
                for r in range(1, 8):
                    def _sin(v, h=h, r=r):
                        v.drain()
                        v.scalar_tensor_tensor(
                            out=s_run[:, h * 129:(h + 1) * 129],
                            in0=sgath[h % 2][:, r * 129:(r + 1) * 129],
                            scalar=maskcol[:, r:r + 1],
                            in1=s_run[:, h * 129:(h + 1) * 129],
                            op0=ALU.mult, op1=ALU.add)
                        if r == 7:
                            v.drain().then_inc(sDV.h, 1)
                    P.op("vector", _sin)
            P.op("vector", lambda v: _dinc(v, v.tensor_copy(out=s_runb[:], in_=s_run[:]), sDV.h))
            P.wait("vector", sPE, pe_eb)
            P.op("vector", lambda v: v.tensor_copy(out=e_b[:], in_=pA[0:H, 512:516]))
            P.op("vector", lambda v: v.drain())
            P.op("vector", lambda v: _dinc(v, v.tensor_scalar(
                out=one_m[:], in0=e_b[:], scalar1=-1.0, scalar2=1.0,
                op0=ALU.mult, op1=ALU.add), sDV.h))
            P.wait("vector", srlt, ld_relt)
            P.op("vector", lambda v: v.tensor_mul(out=relm[:], in0=rel_t[:], in1=one_m[:]))
            P.op("vector", lambda v: v.drain())
            P.op("vector", lambda v: _dinc(v, v.scalar_tensor_tensor(
                out=relm[:], in0=e_b[:], scalar=-1e30, in1=relm[:],
                op0=ALU.mult, op1=ALU.add), sDV.h))
            P.wait("vector", sDV, dv_relm)
            P.op("vector", lambda v: _dinc(v, v.reduce_max(out=rmax[:], in_=relm[:], axis=AX.X)
                 , sDV.h))
            P.wait("vector", sDV, dv_rmax)
            P.op("vector", lambda v: _dinc(v, v.tensor_scalar_mul(out=negmax[:], in0=rmax[:], scalar1=-1.0)
                 , sDV.h))
            P.wait("vector", sAC, ac_wex)
            P.op("vector", lambda v: _dinc(v, v.reduce_sum(out=ssum[:], in_=wex[:], axis=AX.X)
                 , sDV.h))
            P.wait("vector", sDV, dv_ssum)
            P.op("vector", lambda v: _dinc(v, v.reciprocal(out=srq[:], in_=ssum[:]), sDV.h))
            P.wait("vector", sDV, dv_srq)
            P.op("vector", lambda v: _dinc(v, v.tensor_scalar_mul(out=W_sb[:], in0=wex[:], scalar1=srq[:])
                 , sDV.h))
            P.wait("vector", sPE, pe_wb)
            P.op("vector", lambda v: _dinc(v, v.tensor_copy(out=W_b[:], in_=pB[:, 512:512 + H * M])
                 , sDV.h))
            P.wait("vector", sAC, ac_gsig)
            P.wait("vector", sDV, dv_notall)
            P.op("vector", lambda v: _dinc(v, v.tensor_scalar_mul(
                out=gsig2[:], in0=gsig[:], scalar1=notall[0:1, 0:1]), sDV.h))
            P.wait("vector", sPE, pe_gb)
            P.op("vector", lambda v: v.tensor_copy(out=G_sb[:], in_=pB[:, 1024:1024 + H]))
            P.op("vector", lambda v: v.drain())
            P.op("vector", lambda v: _dinc(v, v.tensor_scalar(
                out=Gc_sb[:], in0=G_sb[:], scalar1=-1.0, scalar2=1.0,
                op0=ALU.mult, op1=ALU.add), sDV.h))
            P.wait("vector", sPE, pe_geff)
            P.op("vector", lambda v: _dinc(v, v.tensor_copy(out=Geff[:], in_=pB[:, 1536:1536 + H])
                 , sDV.h))
            for h in range(H):
                P.op("vector", lambda v, h=h: v.tensor_scalar_mul(
                    out=WG[:, h * M:(h + 1) * M], in0=W_b[:, h * M:(h + 1) * M],
                    scalar1=Geff[:, h:h + 1]))
            P.op("vector", lambda v: v.drain())

            P.wait("scalar", sDV, dv_negmax)
            P.op("scalar", lambda s: _dinc(s, s.activation(out=wex[:], in_=relm[:], func=AF.Exp,
                                                  bias=negmax[:]), sAC.h))
            P.wait("scalar", sgt2, ld_gate)
            P.op("scalar", lambda s: _dinc(s, s.activation(out=gsig[:], in_=gate_sb[:], func=AF.Sigmoid)
                 , sAC.h))

            P.wait("sync", sDV, dv_rel)
            P.op("sync", lambda e: e.dma_start(out=relbuf, in_=rel_all[:]).then_inc(srel.h, 16))
            P.wait("sync", sDV, dv_wsb)
            P.op("sync", lambda e: e.dma_start(out=wbuf, in_=W_sb[:]).then_inc(swbf.h, 16))
            P.emit()
            barrier()
        dbg_dump("sinit", s_run[:])
        dbg_dump("wb", W_b[:])
        dbg_dump("ccout", cc_out)
        if "gg" in dbg:
            P.op("gpsimd", lambda g: g.dma_start(
                out=dbg["gg"][:, 0:H], in_=G_sb[:]).then_inc(sGt.h, 16)); sGt.inc(16)
            P.op("gpsimd", lambda g: g.dma_start(
                out=dbg["gg"][:, H:2 * H], in_=Gc_sb[:]).then_inc(sGt.h, 16)); sGt.inc(16)
            P.op("gpsimd", lambda g: g.dma_start(
                out=dbg["gg"][:, 2 * H:3 * H], in_=Geff[:]).then_inc(sGt.h, 16)); sGt.inc(16)
        P.emit()
        if upto <= 5:
            return nc

        with ExitStack() as ph:
            x_c = [sbuf(f"x_c{i}", [128, D], F32, ph) for i in range(2)]
            qp_sl = [sbuf(f"qp_sl{i}", [128, H * 128], BF16, ph) for i in range(2)]
            in_sl = [sbuf(f"in_sl{i}", [128, H * 129], BF16, ph) for i in range(2)]
            sl_sl = [sbuf(f"sl_sl{i}", [128, H * 129], BF16, ph) for i in range(2)]
            gf_c = [sbuf(f"gf_c{i}", [128, D], BF16, ph) for i in range(2)]
            saug = [sbuf(f"saug{i}", [128, H * 129], BF16, ph) for i in range(2)]
            att_tm = [sbuf(f"att_tm{i}", [128, D], BF16, ph) for i in range(2)]
            afm = [sbuf(f"afm{i}", [128, D], BF16, ph) for i in range(2)]
            out_sb = [sbuf(f"out_sb{i}", [128, D], F32, ph) for i in range(2)]
            tsb = sbuf("tsb", [128, D], F32, ph)
            denc = [sbuf(f"denc{i}", [128, 1], F32, ph) for i in range(4)]
            dreq = [sbuf(f"dreq{i}", [128, 1], F32, ph) for i in range(4)]
            dGc = [sbuf(f"dGc{i}", [128, 1], F32, ph) for i in range(4)]
            nrmc = [sbuf(f"nrmc{i}", [128, M], F32, ph) for i in range(4)]
            nrq = [sbuf(f"nrq{i}", [128, M], F32, ph) for i in range(4)]
            sc4 = [sbuf(f"sc4_{i}", [128, M], F32, ph) for i in range(4)]
            mem_e = [sbuf(f"mem_e{i}", [128, 512], BF16, ph) for i in range(4)]
            numc = [sbuf(f"numc{i}", [128, 128], BF16, ph) for i in range(4)]
            qm = [sbuf(f"qm{i}", [128, 4 * 128], BF16, ph) for i in range(4)]
            s0123 = [sbuf(f"s0123_{i}", [128, 256], BF16, ph) for i in range(4)]
            macc = [sbuf(f"macc{i}", [128, 128], BF16, ph) for i in range(4)]
            t1 = [sbuf(f"t1_{i}", [128, 128], BF16, ph) for i in range(4)]

            ldx = [0] * C; ldqp = [0] * C; ldin = [0] * C; ldsl = [0] * C; ldgf = [0] * C
            for c in range(C):
                ldx[c] = sx[c % 2].inc(16)
                ldqp[c] = sqp[c % 2].inc(16)
                ldin[c] = sin_[c % 2].inc(16)
                ldsl[c] = ssl[c % 2].inc(16)
                ldgf[c] = sgfc[c % 2].inc(16)
            pe_h = [[0] * H for _ in range(C)]
            wo_done = [[0, 0] for _ in range(C)]
            for c in range(C):
                for h in range(H):
                    pe_h[c][h] = sPE.inc()
                wo_done[c][0] = sPE.inc()
                wo_done[c][1] = sPE.inc()
            NHG = 4
            saug_dv = [0] * C
            st1h = [[0] * H for _ in range(C)]
            att_dv = [[0] * H for _ in range(C)]
            fin_dv = [[0, 0] for _ in range(C)]
            for c in range(C):
                saug_dv[c] = sDV.inc()
                for hg in range(H // NHG):
                    for hh in range(NHG):
                        st1h[c][hg * NHG + hh] = sDV.inc()
                    for hh in range(NHG):
                        att_dv[c][hg * NHG + hh] = sDV.inc()
                fin_dv[c][0] = sDV.inc()
                fin_dv[c][1] = sDV.inc()
            att_tp = [satp[c % 2].inc(16 * H) for c in range(C)]
            out_wr = [sst.inc(16) for _ in range(C)]

            def _pslot(h):
                s = h % 4
                return (s % 2) * 512 + (s // 2) * 256

            stqp_v = st_qp.rearrange("h p t -> p h t")
            stin_v = st_intra.rearrange("h p t -> p h t")
            stsl_v = st_sloc.rearrange("h p t -> p h t")
            for c in range(C):
                if c >= 2:
                    P.wait("gpsimd", sDV, fin_dv[c - 2][1])
                    P.wait("gpsimd", sPE, pe_h[c - 2][H - 1])
                P.op("gpsimd", lambda g, c=c: g.dma_start(
                    out=x_c[c % 2][:], in_=ap["x"][c * 128:(c + 1) * 128, :]).then_inc(sx[c % 2].h, 16))
                P.op("gpsimd", lambda g, c=c: g.dma_start(
                    out=qp_sl[c % 2].ap().rearrange("p (h t) -> p h t", h=H),
                    in_=stqp_v[:, :, c * 128:(c + 1) * 128]).then_inc(sqp[c % 2].h, 16))
                P.op("gpsimd", lambda g, c=c: g.dma_start(
                    out=in_sl[c % 2].ap().rearrange("p (h t) -> p h t", h=H),
                    in_=stin_v[:, :, c * 129:(c + 1) * 129]).then_inc(sin_[c % 2].h, 16))
                P.op("gpsimd", lambda g, c=c: g.dma_start(
                    out=sl_sl[c % 2].ap().rearrange("p (h t) -> p h t", h=H),
                    in_=stsl_v[:, :, c * 129:(c + 1) * 129]).then_inc(ssl[c % 2].h, 16))
                P.op("gpsimd", lambda g, c=c: g.dma_start(
                    out=gf_c[c % 2][:], in_=st_gf[c]).then_inc(sgfc[c % 2].h, 16))
            P.wait("gpsimd", sst, out_wr[C - 1])

            flat = []
            for c in range(C):
                for h in range(H):
                    flat.append((c, h))
            NHG = 4
            for c in range(C):
                P.wait("vector", ssl[c % 2], ldsl[c])
                if c >= 2:
                    P.wait("vector", sPE, pe_h[c - 2][H - 1])
                    P.wait("vector", satp[c % 2], att_tp[c - 2])
                    P.wait("vector", sst, out_wr[c - 2])
                P.op("vector", lambda v, c=c: _dinc(v, v.tensor_add(
                    out=saug[c % 2][:], in0=s_runb[:], in1=sl_sl[c % 2][:]), sDV.h))
                for hg in range(H // NHG):
                    h0 = hg * NHG
                    gi0 = c * H + h0
                    for hh in range(NHG):
                        h = h0 + hh
                        pS = _pslot(h)
                        pM = (h % 4) * 512
                        P.wait("vector", sPE, pe_h[c][h])
                        P.op("vector", lambda v, h=h, pM=pM: v.tensor_copy(
                            out=mem_e[h % 4][:], in_=pA[:, pM:pM + 512]))
                        P.op("vector", lambda v, h=h, pS=pS: v.tensor_copy(
                            out=numc[h % 4][:], in_=pB[:, pS:pS + 128]))
                        P.op("vector", lambda v, h=h, pS=pS: v.tensor_scalar_max(
                            out=denc[h % 4][:], in0=pB[:, pS + 128:pS + 129], scalar1=1e-6))
                        P.op("vector", lambda v, h=h, pS=pS: v.tensor_scalar_max(
                            out=nrmc[h % 4][:], in0=pB[:, pS + 160:pS + 160 + M],
                            scalar1=1e-6).then_inc(sDV.h, 1))
                    P.op("vector", lambda v: v.drain())
                    for hh in range(NHG):
                        h = h0 + hh
                        P.op("vector", lambda v, h=h: v.reciprocal(
                            out=dreq[h % 4][:], in_=denc[h % 4][:]))
                        P.op("vector", lambda v, h=h: v.reciprocal(
                            out=nrq[h % 4][:], in_=nrmc[h % 4][:]))
                    P.op("vector", lambda v: v.drain())
                    for hh in range(NHG):
                        h = h0 + hh
                        P.op("vector", lambda v, h=h: v.tensor_mul(
                            out=sc4[h % 4][:], in0=nrq[h % 4][:], in1=WG[:, h * M:(h + 1) * M]))
                        P.op("vector", lambda v, h=h: v.tensor_mul(
                            out=dGc[h % 4][:], in0=dreq[h % 4][:], in1=Gc_sb[:, h:h + 1]))
                    P.op("vector", lambda v: v.drain())
                    for hh in range(NHG):
                        h = h0 + hh
                        pS = _pslot(h)
                        P.op("vector", lambda v, h=h: v.tensor_scalar_mul(
                            out=t1[h % 4][:], in0=numc[h % 4][:], scalar1=dGc[h % 4][:]))
                        for m in range(M):
                            P.op("vector", lambda v, h=h, m=m: v.tensor_scalar_mul(
                                out=qm[h % 4][:, m * 128:(m + 1) * 128],
                                in0=mem_e[h % 4][:, m * 128:(m + 1) * 128],
                                scalar1=sc4[h % 4][:, m:m + 1]))
                    P.op("vector", lambda v: v.drain())
                    for hh in range(NHG):
                        h = h0 + hh
                        P.op("vector", lambda v, h=h: v.tensor_add(
                            out=s0123[h % 4][:], in0=qm[h % 4][:, 0:256], in1=qm[h % 4][:, 256:512]))
                    P.op("vector", lambda v: v.drain())
                    for hh in range(NHG):
                        h = h0 + hh
                        P.op("vector", lambda v, h=h: v.tensor_add(
                            out=macc[h % 4][:], in0=s0123[h % 4][:, 0:128],
                            in1=s0123[h % 4][:, 128:256]))
                    P.op("vector", lambda v: v.drain())
                    for hh in range(NHG):
                        h = h0 + hh
                        P.op("vector", lambda v, c=c, h=h: v.tensor_add(
                            out=att_tm[c % 2][:, h * 128:(h + 1) * 128], in0=macc[h % 4][:],
                            in1=t1[h % 4][:]))
                    P.op("vector", lambda v: v.drain().then_inc(sDV.h, NHG))
                P.wait("vector", sgfc[c % 2], ldgf[c])
                P.wait("vector", sx[c % 2], ldx[c])
                for half in range(2):
                    P.wait("vector", sPE, wo_done[c][half])
                    P.op("vector", lambda v, c=c, half=half: v.tensor_tensor(
                        out=tsb[:, 0:1024], in0=pB[:, 1024:2048],
                        in1=gf_c[c % 2][:, half * 1024:(half + 1) * 1024], op=ALU.mult))
                    P.op("vector", lambda v: v.drain())
                    P.op("vector", lambda v, c=c, half=half: _dinc(v, v.tensor_add(
                        out=out_sb[c % 2][:, half * 1024:(half + 1) * 1024],
                        in0=tsb[:, 0:1024],
                        in1=x_c[c % 2][:, half * 1024:(half + 1) * 1024]), sDV.h))

            for c in range(C):
                P.wait("tensor", sDV, saug_dv[c])
                P.wait("tensor", sqp[c % 2], ldqp[c])
                P.wait("tensor", sin_[c % 2], ldin[c])
                for h in range(H):
                    pS = _pslot(h)
                    pM = (h % 4) * 512
                    qp_t = qp_sl[c % 2][:, h * 128:(h + 1) * 128]
                    gi = c * H + h
                    if gi >= 2:
                        c2, h2 = flat[gi - 2]
                        P.wait("tensor", sDV, st1h[c2][h2])
                    P.op("tensor", lambda t, o=pB[:, pS:pS + 129], a=qp_t,
                         b=saug[c % 2][:, h * 129:(h + 1) * 129]:
                         t.matmul(o, a, b, start=True, stop=False))
                    P.op("tensor", lambda t, o=pB[:, pS:pS + 129],
                         b=in_sl[c % 2][:, h * 129:(h + 1) * 129]:
                         t.matmul(o, identb[:], b, start=False, stop=True))
                    P.op("tensor", lambda t, o=pA[:, pM:pM + 512], a=qp_t,
                         b=mem_bf[:, h * 512:(h + 1) * 512]:
                         t.matmul(o, a, b, start=True, stop=True))
                    P.op("tensor", lambda t, o=pB[:, pS + 160:pS + 160 + M], a=qp_t,
                         b=mnT_b[:, h * M:(h + 1) * M]:
                         t.matmul(o, a, b, start=True, stop=True).then_inc(sPE.h, 1))
                P.wait("tensor", satp[c % 2], att_tp[c])
                for half in range(2):
                    if c >= 1 or half >= 1:
                        P.wait("tensor", sDV, fin_dv[c - 1][1] if half == 0 else fin_dv[c][0])
                    for db in range(DB):
                        lhs = afm[c % 2][:, db * 128:(db + 1) * 128]
                        for sp in range(2):
                            o = pB[:, 1024 + sp * 512:1024 + (sp + 1) * 512]
                            rhs = w_oT[:, db * D + half * 1024 + sp * 512:
                                       db * D + half * 1024 + (sp + 1) * 512]
                            if db == DB - 1 and sp == 1:
                                P.op("tensor", lambda t, o=o, a=lhs, b=rhs:
                                     t.matmul(o, a, b, start=False, stop=True).then_inc(sPE.h, 1))
                            else:
                                P.op("tensor", lambda t, o=o, a=lhs, b=rhs, st=(db == 0),
                                     fi=(db == DB - 1): t.matmul(o, a, b, start=st, stop=fi))

            for c in range(C):
                for h in range(H):
                    P.wait("scalar", sDV, att_dv[c][h])
                    if c >= 2 and h == 0:
                        P.wait("scalar", sPE, wo_done[c - 2][1])
                    P.op("scalar", lambda e, c=c, h=h: e.dma_start_transpose(
                        afm[c % 2][:, h * 128:(h + 1) * 128],
                        att_tm[c % 2][:, h * 128:(h + 1) * 128]).then_inc(satp[c % 2].h, 16))

            for c in range(C):
                P.wait("sync", sDV, fin_dv[c][1])
                P.op("sync", lambda e, c=c: e.dma_start(
                    out=out[c * 128:(c + 1) * 128, :], in_=out_sb[c % 2][:]).then_inc(sst.h, 16))
            if "att0" in dbg:
                P.wait("sync", satp[0], att_tp[0])
                P.op("sync", lambda e: e.dma_start(out=dbg["att0"], in_=afm[0][:])
                     .then_inc(sst.h, 16))
                sst.inc(16)
            P.emit()
    return nc


def _mk_inputs(inputs):
    f = np.float32
    x = np.asarray(inputs["L0_attn_hidden_states"], f).reshape(T * NCORE, D)
    wq = np.asarray(inputs["L0_attn_w_q"], f)
    wk = np.asarray(inputs["L0_attn_w_k"], f)
    wv = np.asarray(inputs["L0_attn_w_v"], f)
    wo = np.asarray(inputs["L0_attn_w_o"], f)
    mem = np.asarray(inputs["L0_attn_memories"], f)
    mn = np.asarray(inputs["L0_attn_memory_norms"], f)
    gate = np.asarray(inputs["L0_attn_gate"], f)
    gw = np.asarray(inputs["L0_ffn_gate_w"], np.int32)
    gs = np.asarray(inputs["L0_ffn_gate_scale"], f)
    uw = np.asarray(inputs["L0_ffn_up_w"], np.int32)
    us = np.asarray(inputs["L0_ffn_up_scale"], f)
    dw = np.asarray(inputs["L0_ffn_down_w"], np.int32)
    ds = np.asarray(inputs["L0_ffn_down_scale"], f)

    com = {
        "wqT": np.ascontiguousarray(wq.T.reshape(DB, 128, H, 128).transpose(2, 0, 1, 3)),
        "wkT": np.ascontiguousarray(wk.T.reshape(DB, 128, H, 128).transpose(2, 0, 1, 3)),
        "wvT": np.ascontiguousarray(wv.T.reshape(DB, 128, D)),
        "woT": np.ascontiguousarray(wo.T.reshape(DB, 128, D)),
        "gw": np.ascontiguousarray(gw.T.reshape(HB, 128, IB, 128).transpose(2, 0, 1, 3)),
        "uw": np.ascontiguousarray(uw.T.reshape(HB, 128, IB, 128).transpose(2, 0, 1, 3)),
        "dw": np.ascontiguousarray(dw.T.reshape(IB, 128, HB, 128).transpose(2, 0, 1, 3)),
        "gsT": np.ascontiguousarray(gs.reshape(IB, 128).T),
        "usT": np.ascontiguousarray(us.reshape(IB, 128).T),
        "dsT": np.ascontiguousarray(ds.reshape(HB, 128).T),
        "mem": np.ascontiguousarray(mem),
        "mnTh": np.ascontiguousarray(mn.transpose(2, 1, 0).reshape(128, H * M)),
        "mnrows": np.ascontiguousarray(mn.reshape(M * H, 128)),
        "gate": np.ascontiguousarray(gate.reshape(1, H)),
        "tril": np.triu(np.ones((128, 128))).astype(bf16),
        "i_epsc": np.full((128, 1), 1e-5, f),
        "i_sent": np.arange(700, 708, dtype=f)[None, :] + np.zeros((128, 1), f),
        "i_onesb": np.ones((128, 1), bf16),
        "i_onesr": np.ones((1, 128), f),
        "msel": np.kron(np.eye(M, dtype=f), np.ones((H, 1), f)),
        "i_ident": np.eye(128).astype(bf16),
    }
    in_maps = []
    for r in range(NCORE):
        m = dict(com)
        m["x"] = np.ascontiguousarray(x[r * T:(r + 1) * T])
        rm = np.zeros((1, 8), f)
        rm[0, :r] = 1.0
        m["rmask"] = rm
        in_maps.append(m)
    return in_maps


def kernel(**inputs):
    in_maps = _mk_inputs(inputs)
    nc = build()
    res = run_bass_kernel_spmd(nc, in_maps, core_ids=list(range(NCORE)))
    outs = [np.asarray(res.results[r]["out"]) for r in range(NCORE)]
    return np.concatenate(outs, axis=0).reshape(1, T * NCORE, D).astype(np.float32)
